# version 14
# baseline (speedup 1.0000x reference)
import numpy as np
import ml_dtypes
from contextlib import ExitStack

import concourse.bass as bass
import concourse.mybir as mybir
from concourse.bass_utils import run_bass_kernel_spmd

F32 = mybir.dt.float32
BF16 = mybir.dt.bfloat16
I32 = mybir.dt.int32
AF = mybir.ActivationFunctionType
ALU = mybir.AluOpType
AX = mybir.AxisListType

NCORES = 8
D = 2048
S = 2048
NT = 16
HID0 = 5632
NJ0 = 44
HID1 = 5504
NJ1 = 43
NHEAD = 16
DH = 128
EPS = 1e-5

ENGINES = ("pe", "act", "dve", "pool", "sp")
_BLK_ATTR = {"pe": "tensor", "act": "scalar", "dve": "vector", "pool": "gpsimd", "sp": "sync"}


class Op:
    __slots__ = ("idx", "eng", "emit", "reads", "writes", "tag", "is_dma",
                 "group", "gidx", "deps", "need_inc", "inc_idx", "waits", "gwaits")

    def __init__(self, idx, eng, emit, reads, writes, tag, is_dma, group, gidx):
        self.idx = idx
        self.eng = eng
        self.emit = emit
        self.reads = reads
        self.writes = writes
        self.tag = tag
        self.is_dma = is_dma
        self.group = group
        self.gidx = gidx
        self.deps = []
        self.need_inc = is_dma
        self.inc_idx = None
        self.waits = []
        self.gwaits = {}


class Prog:
    def __init__(self, nc):
        self.nc = nc
        self.ops = []
        self.last_writer = {}
        self.readers = {}
        self.group_count = {}
        self.children = {}
        self.group_waiters = {}
        self.pending_alias = {}

    def _dep(self, op, j):
        if j is not None and j != op.idx:
            op.deps.append(j)

    def _touch_alias(self, op, key):
        if ":" in key:
            par = key.split(":", 1)[0]
            for j in self.pending_alias.get(par, ()):
                self._dep(op, j)
            deps = self.pending_alias.pop(key, None)
        else:
            deps = self.pending_alias.pop(key, None)
        if deps:
            for j in deps:
                self._dep(op, j)

    def add(self, eng, emit, reads=(), writes=(), tag="", group=None):
        is_dma = group is not None
        gidx = None
        if is_dma:
            self.group_count[group] = self.group_count.get(group, 0) + 1
            gidx = self.group_count[group]
        op = Op(len(self.ops), eng, emit, tuple(reads), tuple(writes), tag, is_dma, group, gidx)

        def related(key):
            if ":" in key:
                yield key
                yield key.split(":", 1)[0]
            else:
                yield key
                for ch in self.children.get(key, ()):
                    yield ch

        for k in op.reads:
            self._touch_alias(op, k)
            for rk in related(k):
                self._dep(op, self.last_writer.get(rk))
        for k in op.writes:
            self._touch_alias(op, k)
            for rk in related(k):
                self._dep(op, self.last_writer.get(rk))
                for j in self.readers.get(rk, ()):
                    self._dep(op, j)
        if is_dma:
            for j in self.group_waiters.pop(group, ()):
                self._dep(op, j)
        for k in op.reads:
            self.readers.setdefault(k, []).append(op.idx)
            if ":" in k:
                self.children.setdefault(k.split(":", 1)[0], set()).add(k)
        for k in op.writes:
            self.last_writer[k] = op.idx
            self.readers[k] = []
            if ":" in k:
                self.children.setdefault(k.split(":", 1)[0], set()).add(k)
            else:
                for ch in self.children.get(k, ()):
                    self.last_writer.pop(ch, None)
                    self.readers.pop(ch, None)
                self.children.pop(k, None)
        drop = set()
        for j in set(op.deps):
            dop = self.ops[j]
            if dop.is_dma:
                if is_dma and dop.group == group:
                    assert dop.eng == eng, (
                        f"same-group dma dep needs same queue: {dop.tag} -> {tag}")
                    drop.add(j)
                    continue
                self.group_waiters.setdefault(dop.group, []).append(op.idx)
                v = 16 * (self.group_count[dop.group] - (1 if is_dma and dop.group == group else 0))
                if self.gwaits_get(op, dop.group) < v:
                    op.gwaits[dop.group] = v
        if drop:
            op.deps = [j for j in op.deps if j not in drop]
        self.ops.append(op)
        return op.idx

    @staticmethod
    def gwaits_get(op, g):
        return op.gwaits.get(g, 0)

    def alias_ops(self, new_keys, dep_ops):
        if dep_ops:
            for nk in new_keys:
                self.pending_alias.setdefault(nk, []).extend(dep_ops)

    def resolve_accessors(self, keys):
        deps = []
        for k in keys:
            j = self.last_writer.get(k)
            if j is not None:
                deps.append(j)
            deps.extend(self.readers.get(k, ()))
        return deps

    def check(self):
        per_eng = {e: [] for e in ENGINES}
        for op in self.ops:
            per_eng[op.eng].append(op)
        pos = {e: 0 for e in ENGINES}
        sem = {}
        progress = True
        while progress:
            progress = False
            for e in ENGINES:
                while pos[e] < len(per_eng[e]):
                    op = per_eng[e][pos[e]]
                    ok = all(sem.get(k, 0) >= v for k, v in op.waits)
                    if not ok:
                        break
                    if op.is_dma:
                        sem[("g", op.group)] = sem.get(("g", op.group), 0) + 16
                    elif op.need_inc:
                        sem[("e", e)] = sem.get(("e", e), 0) + 1
                    pos[e] += 1
                    progress = True
        stuck = {e: per_eng[e][pos[e]] for e in ENGINES if pos[e] < len(per_eng[e])}
        if stuck:
            msg = []
            for e, op in stuck.items():
                blocked = [(k, v, sem.get(k, 0)) for k, v in op.waits if sem.get(k, 0) < v]
                msg.append(f"{e}: op#{op.idx} tag={op.tag} blocked_on={blocked}")
            raise RuntimeError("tracker deadlock:\n" + "\n".join(msg))

    def emit(self, stack):
        nc = self.nc
        for op in self.ops:
            for j in set(op.deps):
                self.ops[j].need_inc = True
        counts = {e: 0 for e in ENGINES}
        for op in self.ops:
            if op.is_dma:
                continue
            if op.need_inc:
                counts[op.eng] += 1
                op.inc_idx = counts[op.eng]
        eng_sems = {}
        for e in ENGINES:
            if counts[e] > 0:
                eng_sems[e] = stack.enter_context(nc.semaphore(f"sem_{e}"))
        grp_sems = {}
        for g in self.group_count:
            grp_sems[g] = stack.enter_context(nc.semaphore(f"sem_g_{g}"))
        know = {e: {} for e in ENGINES}
        snap = {}

        def ev(op, waiter=None):
            if op.is_dma:
                v = 16 * op.gidx if waiter is None else waiter.gwaits[op.group]
                return ("g", op.group), v
            return ("e", op.eng), op.inc_idx

        def merge(dst, src):
            for k, v in src.items():
                if dst.get(k, -1) < v:
                    dst[k] = v

        for op in self.ops:
            kn = know[op.eng]
            for j in sorted(set(op.deps)):
                dop = self.ops[j]
                key, val = ev(dop, op)
                if kn.get(key, 0) >= val:
                    continue
                op.waits.append((key, val))
                merge(kn, snap.get(j, {}))
                if kn.get(key, 0) < val:
                    kn[key] = val
            if op.need_inc:
                s = dict(kn)
                key, val = ev(op)
                s[key] = val
                snap[op.idx] = s

        per_eng = {e: [] for e in ENGINES}
        for op in self.ops:
            per_eng[op.eng].append(op)

        def sem_of(key):
            kind, name = key
            return grp_sems[name] if kind == "g" else eng_sems[name]

        self.check()
        blk = stack.enter_context(nc.Block())
        for e in ENGINES:
            ops = per_eng[e]
            if not ops and e != "sp":
                continue

            def body(engine, ops=ops, e=e):
                for op in ops:
                    for key, val in op.waits:
                        engine.wait_ge(sem_of(key), val)
                    inst = op.emit(engine)
                    if op.is_dma:
                        inst.then_inc(grp_sems[op.group], 16)
                    elif op.need_inc:
                        inst.then_inc(eng_sems[op.eng], 1)
                if e == "sp":
                    for g, cnt in self.group_count.items():
                        engine.wait_ge(grp_sems[g], 16 * cnt)
                    for oe in ENGINES:
                        if oe != "sp" and counts[oe] > 0:
                            engine.wait_ge(eng_sems[oe], counts[oe])

            getattr(blk, _BLK_ATTR[e])(body)


class Buf:
    __slots__ = ("name", "shape", "dtype", "off", "size", "_ap")

    def __init__(self, name, shape, dtype, off, size, ap):
        self.name = name
        self.shape = shape
        self.dtype = dtype
        self.off = off
        self.size = size
        self._ap = ap

    def __getitem__(self, sl):
        return self._ap[sl]

    @property
    def ap(self):
        return self._ap


_DT_SIZE = {F32: 4, BF16: 2, I32: 4}


class SBAlloc:

    def __init__(self, nc, prog, base=16512, top=229344):
        self.nc = nc
        self.prog = prog
        self.size = top - base
        self.arena = nc.alloc_sbuf_tensor_at("arena", [128, self.size], mybir.dt.uint8,
                                             offset=base, align_bytes=64)
        self.free = [(0, self.size)]
        self.n = 0
        self.live = {}
        self.region_keys = []

    def alloc(self, name, shape, dtype, keys=None):
        assert len(shape) == 2
        per_part = shape[1] * _DT_SIZE[dtype]
        size = (per_part + 63) & ~63
        for i, (a, b) in enumerate(self.free):
            if b - a >= size:
                off = a
                if a + size == b:
                    self.free.pop(i)
                else:
                    self.free[i] = (a + size, b)
                break
        else:
            livestr = ", ".join(f"{n}@{o}+{s}" for n, (o, s) in sorted(self.live.items(), key=lambda x: x[1][0]))
            raise MemoryError(f"SBUF out of space for {name} ({size}B): free={self.free}\nlive: {livestr}")
        self.n += 1
        ap = self.arena.ap()[0:shape[0], off:off + shape[1] * _DT_SIZE[dtype]].bitcast(dtype)
        buf = Buf(name, shape, dtype, off, size, ap)
        self.live[name] = (off, size)
        hit = []
        nr = []
        for (ra, rb, ks) in self.region_keys:
            if ra < off + size and off < rb:
                hit.extend(ks)
                if ra < off:
                    nr.append((ra, off, ks))
                if rb > off + size:
                    nr.append((off + size, rb, ks))
            else:
                nr.append((ra, rb, ks))
        self.region_keys = nr
        if hit:
            self.prog.alias_ops([buf.name], sorted(set(hit)))
        return buf

    def freeb(self, *bufs):
        for buf in bufs:
            self.live.pop(buf.name, None)
            self.free.append((buf.off, buf.off + buf.size))
            pre = buf.name + ":"
            ks = set([buf.name])
            for kk in list(self.prog.last_writer) + list(self.prog.readers):
                if kk.startswith(pre):
                    ks.add(kk)
            dep_ops = sorted(set(self.prog.resolve_accessors(sorted(ks))))
            self.region_keys.append((buf.off, buf.off + buf.size, dep_ops))
        self.free.sort()
        out = []
        for a, b in self.free:
            if out and out[-1][1] >= a:
                out[-1] = (out[-1][0], max(out[-1][1], b))
            else:
                out.append((a, b))
        self.free = out


def _mk(prog, sb):
    class H:
        pass
    return H


class K:

    def __init__(self, nc):
        self.nc = nc
        self.P = Prog(nc)
        self.stack = ExitStack()
        self.sb = SBAlloc(nc, self.P)
        self.ps_h = None
        self.uid = 0
        self.bg = []

    def pump(self, n=None):
        if n is None:
            n = 2 if len(self.bg) > 240 else 1
        for _ in range(n):
            if self.bg:
                self.bg.pop(0)()

    def u(self, s):
        self.uid += 1
        return f"{s}{self.uid}"

    def psum_init(self):
        self.ps_h = self.stack.enter_context(self.nc.psum_tensor("PS", [128, 4096], F32))

    def ps(self, bank, rows=slice(0, 128), cols=slice(0, 512)):
        c0 = bank * 512 + cols.start
        c1 = bank * 512 + cols.stop
        return self.ps_h.ap()[rows, c0:c1]

    @staticmethod
    def pk(bank, n=1):
        return tuple(f"PSB{b}" for b in range(bank, bank + n))

    def dma(self, queue, dst_ap, src_ap, reads, writes, group, tag="", accum=False):
        if accum:
            fn = lambda e, d=dst_ap, s=src_ap: e.dma_start(d, s, accum_op=ALU.add)
        else:
            fn = lambda e, d=dst_ap, s=src_ap: e.dma_start(d, s)
        self.P.add(queue, fn, reads=reads, writes=writes, tag=tag, group=group)

    def op(self, eng, fn, reads=(), writes=(), tag=""):
        self.P.add(eng, fn, reads=reads, writes=writes, tag=tag)


def build_ln(k: K, tag, x_dram, x_keyf, tok0, width, consts):
    P, sb, nc = k.P, k.sb, k.nc
    nch = width // 512
    assert nch <= 4
    xn = [sb.alloc(k.u(f"{tag}xn"), [128, width], BF16) for _ in range(NT)]
    xsq = [sb.alloc(k.u(f"{tag}xsq"), [128, width], BF16) for _ in range(2)]
    for t in range(NT):
        if t % 4 == 0:
            k.pump()
        k.dma("pool", xn[t][:, :], x_dram[t * 128:(t + 1) * 128, tok0:tok0 + width],
              reads=(f"{x_keyf}:{t}",), writes=(xn[t].name,), group="gxbf", tag=f"{tag}ldx{t}")
        q = xsq[t % 2]
        k.op("act", lambda e, a=q, b=xn[t]: e.activation(a[:, :], b[:, :], AF.Square),
             reads=(xn[t].name,), writes=(q.name,), tag=f"{tag}sq{t}")

        def mm_mean(e, t=t, b=xn[t]):
            for c in range(nch):
                r = e.matmul(k.ps(c, slice(0, 1)), consts["ones_mean"][:, :],
                             b[:, c * 512:(c + 1) * 512], start=(t == 0), stop=(t == NT - 1))
            return r

        k.op("pe", mm_mean, reads=(xn[t].name, "ones_mean"), writes=k.pk(0, nch), tag=f"{tag}mmM{t}")

        def mm_sq(e, t=t, q=q):
            for c in range(nch):
                r = e.matmul(k.ps(4 + c, slice(0, 1)), consts["ones_mean"][:, :],
                             q[:, c * 512:(c + 1) * 512], start=(t == 0), stop=(t == NT - 1))
            return r

        k.op("pe", mm_sq, reads=(q.name, "ones_mean"), writes=k.pk(4, nch), tag=f"{tag}mmS{t}")
    r1 = sb.alloc(k.u(f"{tag}r1"), [1, width], F32)
    r2 = sb.alloc(k.u(f"{tag}r2"), [1, width], F32)
    r3 = sb.alloc(k.u(f"{tag}r3"), [1, width], F32)
    MEAN = r1[0:1, :]
    NMR = r1[0:1, :]
    TMP = r2[0:1, :]
    RSTD = r2[0:1, :]
    MSQ = r3[0:1, :]
    VAR = r3[0:1, :]
    k.op("dve", lambda e: e.tensor_copy(MEAN, k.ps(0, slice(0, 1), slice(0, 512 * nch))),
         reads=k.pk(0, nch), writes=(r1.name,), tag=f"{tag}cpM")
    k.op("dve", lambda e: e.tensor_copy(MSQ, k.ps(4, slice(0, 1), slice(0, 512 * nch))),
         reads=k.pk(4, nch), writes=(r3.name,), tag=f"{tag}cpS")
    k.op("dve", lambda e: e.tensor_mul(TMP, MEAN, MEAN),
         reads=(r1.name,), writes=(r2.name,), tag=f"{tag}m2")
    k.op("dve", lambda e: e.tensor_sub(VAR, MSQ, TMP),
         reads=(r3.name, r2.name), writes=(r3.name,), tag=f"{tag}var")
    k.op("act", lambda e: e.activation(RSTD, VAR, AF.Sqrt, bias=consts["epsc"][0:1, 0:1], scale=1.0),
         reads=(r3.name, "epsc"), writes=(r2.name,), tag=f"{tag}sqrt")
    k.op("dve", lambda e: e.reciprocal(RSTD, RSTD),
         reads=(r2.name,), writes=(r2.name,), tag=f"{tag}rcp")
    k.op("dve", lambda e: e.scalar_tensor_tensor(NMR, MEAN, -1.0, RSTD, ALU.mult, ALU.mult),
         reads=(r1.name, r2.name), writes=(r1.name,), tag=f"{tag}nmr")
    a_b = sb.alloc(k.u(f"{tag}ab"), [128, width], F32)
    c_b = sb.alloc(k.u(f"{tag}cb"), [128, width], F32)

    def mm_bc(e):
        for c in range(nch):
            e.matmul(k.ps(c), consts["ones_row_f"][:, :], RSTD[0:1, c * 512:(c + 1) * 512], start=True, stop=True)
            r = e.matmul(k.ps(4 + c), consts["ones_row_f"][:, :], NMR[0:1, c * 512:(c + 1) * 512], start=True, stop=True)
        return r

    k.op("pe", mm_bc, reads=(r2.name, r1.name, "ones_row_f"),
         writes=k.pk(0, nch) + k.pk(4, nch), tag=f"{tag}bc")
    k.op("dve", lambda e: e.tensor_copy(a_b[:, :], k.ps(0, slice(0, 128), slice(0, 512 * nch))),
         reads=k.pk(0, nch), writes=(a_b.name,), tag=f"{tag}cpA")
    k.op("dve", lambda e: e.tensor_copy(c_b[:, :], k.ps(4, slice(0, 128), slice(0, 512 * nch))),
         reads=k.pk(4, nch), writes=(c_b.name,), tag=f"{tag}cpC")
    tmpf = [sb.alloc(k.u(f"{tag}tmpf"), [128, width], F32) for _ in range(2)]
    xf = [sb.alloc(k.u(f"{tag}xf"), [128, width], F32) for _ in range(2)]
    for t in range(NT):
        tf = tmpf[t % 2]
        xft = xf[t % 2]
        k.dma("sp", xft[:, :], x_dram[t * 128:(t + 1) * 128, tok0:tok0 + width],
              reads=(f"{x_keyf}:{t}",), writes=(xft.name,), group=f"gxf{t % 2}", tag=f"{tag}ldf{t}")
        k.op("dve", lambda e, b=xft, tf=tf: e.tensor_mul(tf[:, :], b[:, :], a_b[:, :]),
             reads=(xft.name, a_b.name), writes=(tf.name,), tag=f"{tag}nA{t}")
        k.op("dve", lambda e, tf=tf, o=xn[t]: e.tensor_add(o[:, :], tf[:, :], c_b[:, :]),
             reads=(tf.name, c_b.name), writes=(xn[t].name,), tag=f"{tag}nB{t}")
    sb.freeb(*xsq, r1, r2, r3, a_b, c_b, *tmpf, *xf)
    return xn


def build_ffn(k: K, tag, x_dram, x_keyf, xo_dram, xo_keyf, consts, *,
              nj, wgT, wuT, wdT, quant, sg=None, su=None, sdh=None, wrkeys=((), (), ()),
              wrkeyfn=None):
    P, sb = k.P, k.sb
    for half in range(2):
        tok0 = half * 1024
        W = 1024
        xn = build_ln(k, f"{tag}ln{half}", x_dram, x_keyf, tok0, W, consts)
        hid = [sb.alloc(k.u(f"{tag}hid"), [128, W], BF16) for _ in range(nj)]
        wbuf = [sb.alloc(k.u(f"{tag}wg"), [128, 2048], BF16) for _ in range(2)]
        ubuf = [sb.alloc(k.u(f"{tag}wu"), [128, 2048], BF16) for _ in range(2)]
        gsil = [sb.alloc(k.u(f"{tag}gs"), [128, W], BF16) for _ in range(2)]
        def lduv(j):
            wb, ub = wbuf[j % 2], ubuf[j % 2]
            rk0 = wrkeyfn(0, j) if wrkeyfn else wrkeys[0]
            rk1 = wrkeyfn(1, j) if wrkeyfn else wrkeys[1]
            k.dma("pool", wb[:, :], wgT[j], reads=rk0, writes=(wb.name,), group=f"gw{j % 2}", tag=f"{tag}ldg{j}")
            k.dma("pool", ub[:, :], wuT[j], reads=rk1, writes=(ub.name,), group=f"gu{j % 2}", tag=f"{tag}ldu{j}")
            if quant:
                k.op("act", lambda e, b=wb: e.activation(b[:, :], b[:, :], AF.Identity,
                                                         bias=consts["neg128"][:, 0:1], scale=1.0),
                     reads=(wb.name, "neg128"), writes=(wb.name,), tag=f"{tag}cvg{j}")
                k.op("act", lambda e, b=ub: e.activation(b[:, :], b[:, :], AF.Identity,
                                                         bias=consts["neg128"][:, 0:1], scale=1.0),
                     reads=(ub.name, "neg128"), writes=(ub.name,), tag=f"{tag}cvu{j}")

        lduv(0)
        for j in range(nj):
            k.pump()
            wb, ub, gs = wbuf[j % 2], ubuf[j % 2], gsil[j % 2]
            pb = 4 * (j % 2)
            if j + 1 < nj:
                lduv(j + 1)

            def mm_gate(e, wb=wb, ub=ub, xn=xn, pb=pb):
                for t in range(NT):
                    for c in range(2):
                        e.matmul(k.ps(pb + c), wb[:, t * 128:(t + 1) * 128], xn[t][:, c * 512:(c + 1) * 512],
                                 start=(t == 0), stop=(t == NT - 1))
                for t in range(NT):
                    for c in range(2):
                        r = e.matmul(k.ps(pb + 2 + c), ub[:, t * 128:(t + 1) * 128], xn[t][:, c * 512:(c + 1) * 512],
                                     start=(t == 0), stop=(t == NT - 1))
                return r

            k.op("pe", mm_gate, reads=(wb.name, ub.name) + tuple(x.name for x in xn),
                 writes=k.pk(pb, 4), tag=f"{tag}mmg{j}")
            k.pump()
            sc_g = sg[:, j:j + 1] if sg is not None else consts["one_col"][:, 0:1]
            sc_g_key = (sg.name,) if sg is not None else ("one_col",)
            k.op("act", lambda e, gs=gs, sc=sc_g, pb=pb: e.activation(gs[:, :], k.ps(pb, slice(0, 128), slice(0, 1024)), AF.Silu, bias=0.0, scale=sc),
                 reads=k.pk(pb, 2) + sc_g_key, writes=(gs.name,), tag=f"{tag}sil{j}")
            sc_u = su[:, j:j + 1] if su is not None else consts["one_col"][:, 0:1]
            sc_u_key = (su.name,) if su is not None else ("one_col",)
            k.op("dve", lambda e, h=hid[j], gs=gs, sc=sc_u, pb=pb: e.scalar_tensor_tensor(
                    h[:, :], k.ps(pb + 2, slice(0, 128), slice(0, 1024)), sc, gs[:, :], ALU.mult, ALU.mult),
                 reads=k.pk(pb + 2, 2) + (gs.name,) + sc_u_key, writes=(hid[j].name,), tag=f"{tag}hid{j}")
        sb.freeb(*xn, *wbuf, *ubuf, *gsil)
        wdb = [sb.alloc(k.u(f"{tag}wd"), [128, nj * 128], BF16) for _ in range(2)]
        xrs = [sb.alloc(k.u(f"{tag}xr"), [128, W], F32) for _ in range(2)]
        xob = [sb.alloc(k.u(f"{tag}xo"), [128, W], F32) for _ in range(2)]
        def ldd(t):
            wd = wdb[t % 2]
            rk2 = wrkeyfn(2, t) if wrkeyfn else wrkeys[2]
            k.dma("pool", wd[:, :], wdT[t], reads=rk2, writes=(wd.name,), group=f"gd{t % 2}", tag=f"{tag}ldd{t}")
            if quant:
                k.op("act", lambda e, b=wd: e.activation(b[:, :], b[:, :], AF.Identity,
                                                         bias=consts["neg128"][:, 0:1], scale=1.0),
                     reads=(wd.name, "neg128"), writes=(wd.name,), tag=f"{tag}cvd{t}")

        ldd(NT - 1)
        for t in reversed(range(NT)):
            k.pump()
            wd, xr, xo = wdb[t % 2], xrs[t % 2], xob[t % 2]
            if t > 0:
                ldd(t - 1)
            k.dma("sp", xr[:, :], x_dram[t * 128:(t + 1) * 128, tok0:tok0 + W],
                  reads=(f"{x_keyf}:{t}",), writes=(xr.name,), group=f"gxr{t % 2}", tag=f"{tag}ldxr{t}")

            pbd = 4 + 2 * (t % 2)

            def mm_dn(e, wd=wd, hid=hid, pbd=pbd):
                for j in range(nj):
                    for c in range(2):
                        r = e.matmul(k.ps(pbd + c), wd[:, j * 128:(j + 1) * 128], hid[j][:, c * 512:(c + 1) * 512],
                                     start=(j == 0), stop=(j == nj - 1))
                return r

            k.op("pe", mm_dn, reads=(wd.name,) + tuple(h.name for h in hid), writes=k.pk(pbd, 2), tag=f"{tag}mmd{t}")
            sc_d = sdh[:, t:t + 1] if sdh.shape[1] > 1 else sdh[:, 0:1]
            k.op("dve", lambda e, xo=xo, xr=xr, sc=sc_d, pbd=pbd: e.scalar_tensor_tensor(
                    xo[:, :], k.ps(pbd, slice(0, 128), slice(0, 1024)), sc, xr[:, :], ALU.mult, ALU.add),
                 reads=k.pk(pbd, 2) + (xr.name, sdh.name), writes=(xo.name,), tag=f"{tag}res{t}")
            k.dma("sp", xo_dram[t * 128:(t + 1) * 128, tok0:tok0 + W], xo[:, :],
                  reads=(xo.name,), writes=(f"{xo_keyf}:{t}",), group=f"gxw{t % 2}", tag=f"{tag}stx{t}")
        sb.freeb(*hid, *wdb, *xrs, *xob)


def stage_consts(k: K, dr, consts_needed=("ones_mean", "ones_row_f", "one_col")):
    sb, P = k.sb, k.P
    out = {}
    cm = sb.alloc("ones_mean", [128, 1], BF16)
    k.dma("sp", cm[:, :], dr["c_ones_mean"].ap()[:, :], reads=(), writes=("ones_mean",), group="consts")
    out["ones_mean"] = cm
    orf = sb.alloc("ones_row_f", [1, 128], F32)
    k.dma("sp", orf[0:1, :], dr["c_ones_row"].ap()[:, :], reads=(), writes=("ones_row_f",), group="consts")
    out["ones_row_f"] = orf
    oc = sb.alloc("one_col", [128, 1], F32)
    k.dma("sp", oc[:, :], dr["c_one_col"].ap()[:, :], reads=(), writes=("one_col",), group="consts")
    out["one_col"] = oc
    ep = sb.alloc("epsc", [1, 1], F32)
    k.op("dve", lambda e: e.memset(ep[0:1, 0:1], EPS), reads=(), writes=("epsc",), tag="eps")
    out["epsc"] = ep
    o1 = sb.alloc("ones1_bf", [128, 1], BF16)
    k.op("dve", lambda e: e.memset(o1[:, :], 1.0), reads=(), writes=("ones1_bf",), tag="o1")
    out["ones1_bf"] = o1
    hc = sb.alloc("half_col", [128, 1], F32)
    k.op("dve", lambda e: e.memset(hc[:, :], 0.5), reads=(), writes=("half_col",), tag="hc")
    out["half_col"] = hc
    n128 = sb.alloc("neg128", [128, 1], F32)
    k.op("dve", lambda e: e.memset(n128[:, :], -128.0), reads=(), writes=("neg128",), tag="n128")
    out["neg128"] = n128
    return out


CONV_TILES = 2 * NJ1 + (16 * HID1) // 2048


def build_lutconv(k: K, tag, lut_dram, srcs, consts):
    P, sb = k.P, k.sb
    lt = sb.alloc(k.u(f"{tag}lt"), [1, 16], F32)
    dl = sb.alloc(k.u(f"{tag}dlb"), [128, 16], F32)

    def prolog():
        k.dma("sp", lt[0:1, :], lut_dram.ap()[:, :], reads=(), writes=(lt.name,), group="consts")

        def mm_b(e):
            return e.matmul(k.ps(7, slice(0, 128), slice(0, 16)), consts["ones_row_f"][:, :],
                            lt[0:1, 0:16], start=True, stop=True)

        k.op("pe", mm_b, reads=(lt.name, "ones_row_f"), writes=k.pk(7), tag=f"{tag}bc")
        k.op("dve", lambda e: e.tensor_copy(dl[:, :], k.ps(7, slice(0, 128), slice(0, 16))),
             reads=k.pk(7), writes=(dl.name,), tag=f"{tag}cp")

    WMAX = 1024
    i32 = [sb.alloc(k.u(f"{tag}i32"), [128, WMAX], I32) for _ in range(4)]
    ib = [sb.alloc(k.u(f"{tag}ib"), [128, WMAX], BF16) for _ in range(2)]
    ac = [sb.alloc(k.u(f"{tag}ac"), [128, WMAX], BF16) for _ in range(3)]
    tm = [sb.alloc(k.u(f"{tag}tm"), [128, WMAX], BF16) for _ in range(4)]

    def region_v(i, v, w):
        b, a = ib[i % 2], ac[i % 3]
        t = tm[2 * (i % 2) + v % 2]
        eng = "dve"
        k.op("dve", lambda e: e.tensor_scalar(t[0:128, 0:w], b[0:128, 0:w], float(v), None, ALU.is_equal),
             reads=(b.name,), writes=(t.name,), tag=f"{tag}t{v}_{i}")
        k.op(eng, lambda e: e.scalar_tensor_tensor(a[0:128, 0:w], t[0:128, 0:w], dl[:, v:v + 1],
                                                   a[0:128, 0:w], ALU.mult, ALU.add),
             reads=(t.name, dl.name, a.name), writes=(a.name,), tag=f"{tag}s{v}_{i}")

    def ld(i, src, w):
        n = i32[i % 4]
        k.dma("act", n[0:128, 0:w], src, reads=(), writes=(n.name,), group=f"gci{i % 2}", tag=f"{tag}ld{i}")

    def region_a(i, src_next, w):
        n, b, a = i32[i % 4], ib[i % 2], ac[i % 3]
        t = tm[2 * (i % 2)]
        if src_next is not None:
            ld(i + 2, src_next[0], src_next[1])
        k.op("act", lambda e: e.copy(b[0:128, 0:w], n[0:128, 0:w]),
             reads=(n.name,), writes=(b.name,), tag=f"{tag}cast{i}")
        k.op("dve", lambda e: e.tensor_scalar(t[0:128, 0:w], b[0:128, 0:w], 0.0, None, ALU.is_equal),
             reads=(b.name,), writes=(t.name,), tag=f"{tag}t0_{i}")
        k.op("dve", lambda e: e.tensor_scalar(a[0:128, 0:w], t[0:128, 0:w], dl[:, 0:1], None, ALU.mult),
             reads=(t.name, dl.name), writes=(a.name,), tag=f"{tag}s0_{i}")
        for v in range(1, 8):
            region_v(i, v, w)

    def region_b(i, w):
        for v in range(8, 16):
            region_v(i, v, w)

    def region_st(i, dst, key, w):
        a = ac[i % 3]
        k.dma("act", dst, a[0:128, 0:w], reads=(a.name,), writes=(key,), group=f"gco{i % 2}", tag=f"{tag}st{i}")

    def prolog2():
        prolog()
        ld(0, srcs[0][0], srcs[0][3])
        ld(1, srcs[1][0], srcs[1][3])

    steps = [prolog2]
    for i, (src, dst, key, w) in enumerate(srcs):
        nxt = (srcs[i + 2][0], srcs[i + 2][3]) if i + 2 < len(srcs) else None
        steps.append(lambda i=i, nxt=nxt, w=w: region_a(i, nxt, w))
        steps.append(lambda i=i, w=w: region_b(i, w))
        if i >= 2:
            pd, pk, pw = srcs[i - 2][1], srcs[i - 2][2], srcs[i - 2][3]
            steps.append(lambda i=i - 2, dst=pd, key=pk, w=pw: region_st(i, dst, key, w))
    for i in range(len(srcs) - 2, len(srcs)):
        steps.append(lambda i=i, dst=srcs[i][1], key=srcs[i][2], w=srcs[i][3]: region_st(i, dst, key, w))

    def done():
        sb.freeb(lt, dl, *i32, *ib, *ac, *tm)

    steps.append(done)
    return steps


def build_attn0(k: K, tag, x_dram, x_keyf, xo_dram, xo_keyf, consts, W):
    P, sb = k.P, k.sb
    xn = build_ln(k, f"{tag}ln", x_dram, x_keyf, 0, S, consts)
    bias = sb.alloc(k.u(f"{tag}bias"), [128, 64], F32)
    k.dma("sp", bias[:, :], W["a0_bias"].ap()[:, :], reads=(), writes=(bias.name,), group="consts")
    BQ, BK, BV, BO = 0, 16, 32, 48

    wb = [sb.alloc(k.u(f"{tag}wb"), [128, 2048], BF16) for _ in range(2)]

    def proj(dst, wT_dram, bias_col, ph):
        for t in range(NT):
            k.pump()
            b = wb[t % 2]
            pb = 4 * (t % 2)
            k.dma("pool", b[:, :], wT_dram[t * 128:(t + 1) * 128, :],
                  reads=(), writes=(b.name,), group=f"gw{t % 2}", tag=f"{tag}{ph}ld{t}")

            def mm(e, b=b, pb=pb):
                for m in range(NT):
                    for c in range(4):
                        r = e.matmul(k.ps(pb + c), b[:, m * 128:(m + 1) * 128], xn[m][:, c * 512:(c + 1) * 512],
                                     start=(m == 0), stop=(m == NT - 1))
                return r

            k.op("pe", mm, reads=(b.name,) + tuple(x.name for x in xn), writes=k.pk(pb, 4), tag=f"{tag}{ph}mm{t}")
            k.op("act", lambda e, d=dst[t], pb=pb, bc=bias[:, bias_col + t:bias_col + t + 1]: e.activation(
                    d[:, :], k.ps(pb, slice(0, 128), slice(0, 2048)), AF.Identity, bias=bc, scale=1.0),
                 reads=k.pk(pb, 4) + (bias.name,), writes=(dst[t].name,), tag=f"{tag}{ph}ev{t}")

    def rope(dst, ph):
        tcs = [sb.alloc(k.u(f"{tag}rt"), [128, S], F32) for _ in range(2)]
        tcs.append(sb.alloc(k.u(f"{tag}rtb"), [128, S], BF16))
        cosb = sb.alloc(k.u(f"{tag}cos"), [128, S], BF16)
        sinb = sb.alloc(k.u(f"{tag}sin"), [128, S], BF16)
        for p in range(4):
            k.dma("sp", cosb[:, :], W["c0_cos"].ap()[p * 128:(p + 1) * 128, :],
                  reads=(), writes=(cosb.name,), group="gcos", tag=f"{tag}{ph}ldc{p}")
            k.dma("sp", sinb[:, :], W["c0_sin"].ap()[p * 128:(p + 1) * 128, :],
                  reads=(), writes=(sinb.name,), group="gsin", tag=f"{tag}{ph}lds{p}")
            a, c, t2 = tcs[0], tcs[1], tcs[2]
            x0, x1 = dst[p], dst[p + 4]
            for (o, i0, i1, tg) in ((a, x0, cosb, "a"), (c, x0, sinb, "c"), (t2, x1, sinb, "b")):
                k.op("dve", lambda e, o=o, i0=i0, i1=i1: e.tensor_mul(o[:, :], i0[:, :], i1[:, :]),
                     reads=(i0.name, i1.name), writes=(o.name,), tag=f"{tag}{ph}rp{tg}{p}")
            k.op("dve", lambda e, x0=x0, a=a, t2=t2: e.tensor_sub(x0[:, :], a[:, :], t2[:, :]),
                 reads=(a.name, t2.name), writes=(x0.name,), tag=f"{tag}{ph}rs{p}")
            k.op("dve", lambda e, t2=t2, x1=x1: e.tensor_mul(t2[:, :], x1[:, :], cosb[:, :]),
                 reads=(x1.name, cosb.name), writes=(t2.name,), tag=f"{tag}{ph}rp d{p}")
            k.op("dve", lambda e, x1=x1, c=c, t2=t2: e.tensor_add(x1[:, :], c[:, :], t2[:, :]),
                 reads=(c.name, t2.name), writes=(x1.name,), tag=f"{tag}{ph}ra{p}")
        sb.freeb(*tcs, cosb, sinb)

    K_t = [sb.alloc(k.u(f"{tag}K"), [128, S], BF16) for _ in range(NT)]
    proj(K_t, W["a0_wkT"].ap(), BK, "k")
    rope(K_t, "k")
    ksq = [sb.alloc(k.u(f"{tag}ksq"), [128, S], BF16) for _ in range(1)]
    for t in range(NT):
        q = ksq[0]
        k.op("act", lambda e, q=q, s=K_t[t]: e.activation(q[:, :], s[:, :], AF.Square),
             reads=(K_t[t].name,), writes=(q.name,), tag=f"{tag}ksq{t}")

        def mmn(e, q=q, t=t):
            for c in range(4):
                r = e.matmul(k.ps(c, slice(0, 1)), consts["ones1_bf"][:, :], q[:, c * 512:(c + 1) * 512],
                             start=(t == 0), stop=(t == NT - 1))
            return r

        k.op("pe", mmn, reads=(q.name, "ones1_bf"), writes=k.pk(0, 4), tag=f"{tag}mmn{t}")
    nrm = sb.alloc(k.u(f"{tag}nrm"), [1, S], F32)
    invn = sb.alloc(k.u(f"{tag}invn"), [1, S], F32)
    k.op("dve", lambda e: e.tensor_copy(nrm[0:1, :], k.ps(0, slice(0, 1), slice(0, 2048))),
         reads=k.pk(0, 4), writes=(nrm.name,), tag=f"{tag}cpn")
    k.op("act", lambda e: e.activation(invn[0:1, :], nrm[0:1, :], AF.Sqrt),
         reads=(nrm.name,), writes=(invn.name,), tag=f"{tag}sqn")
    k.op("dve", lambda e: e.tensor_scalar(invn[0:1, :], invn[0:1, :], 1e-12, None, ALU.max),
         reads=(invn.name,), writes=(invn.name,), tag=f"{tag}mxn")
    k.op("dve", lambda e: e.reciprocal(invn[0:1, :], invn[0:1, :]),
         reads=(invn.name,), writes=(invn.name,), tag=f"{tag}rcn")
    invb = sb.alloc(k.u(f"{tag}invb"), [128, S], F32)

    def mm_bcn(e):
        for c in range(4):
            r = e.matmul(k.ps(4 + c), consts["ones_row_f"][:, :], invn[0:1, c * 512:(c + 1) * 512], start=True, stop=True)
        return r

    k.op("pe", mm_bcn, reads=(invn.name, "ones_row_f"), writes=k.pk(4, 4), tag=f"{tag}bcn")
    k.op("dve", lambda e: e.tensor_copy(invb[:, :], k.ps(4, slice(0, 128), slice(0, 2048))),
         reads=k.pk(4, 4), writes=(invb.name,), tag=f"{tag}cpb")
    sb.freeb(*ksq, nrm)

    a_col = sb.alloc(k.u(f"{tag}acol"), [128, 16], F32)
    vt = [sb.alloc(k.u(f"{tag}vt"), [128, S], BF16) for _ in range(1)]
    kv = [sb.alloc(k.u(f"{tag}kv"), [128, S], F32) for _ in range(1)]
    for t in range(NT):
        k.pump()
        b = wb[t % 2]
        pb = 4 * (t % 2)
        v = vt[0]
        kvb = kv[0]
        k.dma("pool", b[:, :], W["a0_wvT"].ap()[t * 128:(t + 1) * 128, :],
              reads=(), writes=(b.name,), group=f"gw{t % 2}", tag=f"{tag}vld{t}")

        def mmv(e, b=b, pb=pb):
            for m in range(NT):
                for c in range(4):
                    r = e.matmul(k.ps(pb + c), b[:, m * 128:(m + 1) * 128], xn[m][:, c * 512:(c + 1) * 512],
                                 start=(m == 0), stop=(m == NT - 1))
            return r

        k.op("pe", mmv, reads=(b.name,) + tuple(x.name for x in xn), writes=k.pk(pb, 4), tag=f"{tag}vmm{t}")
        k.op("act", lambda e, v=v, pb=pb, bc=bias[:, BV + t:BV + t + 1]: e.activation(
                v[:, :], k.ps(pb, slice(0, 128), slice(0, 2048)), AF.Identity, bias=bc, scale=1.0),
             reads=k.pk(pb, 4) + (bias.name,), writes=(v.name,), tag=f"{tag}vev{t}")
        k.op("dve", lambda e, kvb=kvb, v=v, s=K_t[t]: e.tensor_mul(kvb[:, :], s[:, :], v[:, :]),
             reads=(K_t[t].name, v.name), writes=(kvb.name,), tag=f"{tag}kv{t}")
        k.op("dve", lambda e, kvb=kvb: e.tensor_mul(kvb[:, :], kvb[:, :], invb[:, :]),
             reads=(kvb.name, invb.name), writes=(kvb.name,), tag=f"{tag}kvn{t}")
        k.op("dve", lambda e, kvb=kvb, t=t: e.reduce_sum(a_col[:, t:t + 1], kvb[:, :], AX.X),
             reads=(kvb.name,), writes=(a_col.name + f":{t}",), tag=f"{tag}A{t}")
    sb.freeb(*vt, *kv, invb, invn)
    sb.freeb(*K_t)

    Y = [sb.alloc(k.u(f"{tag}Y"), [128, S], BF16) for _ in range(NT)]
    proj(Y, W["a0_wqT"].ap(), BQ, "q")
    rope(Y, "q")
    for t in range(NT):
        k.op("dve", lambda e, y=Y[t], sc=a_col[:, t:t + 1]: e.tensor_scalar(y[:, :], y[:, :], sc, None, ALU.mult),
             reads=(Y[t].name, a_col.name + f":{t}", a_col.name), writes=(Y[t].name,), tag=f"{tag}sc{t}")

    xrs = [sb.alloc(k.u(f"{tag}xr"), [128, S], F32) for _ in range(2)]
    xob = [sb.alloc(k.u(f"{tag}xo"), [128, S], F32) for _ in range(1)]
    for t in reversed(range(NT)):
        b = wb[t % 2]
        pb = 4 * (t % 2)
        xr, xo = xrs[t % 2], xob[0]
        k.dma("pool", b[:, :], W["a0_woT"].ap()[t * 128:(t + 1) * 128, :],
              reads=(), writes=(b.name,), group=f"gw{t % 2}", tag=f"{tag}old{t}")
        k.dma("sp", xr[:, :], x_dram[t * 128:(t + 1) * 128, :],
              reads=(f"{x_keyf}:{t}",), writes=(xr.name,), group=f"gxr{t % 2}", tag=f"{tag}ldxr{t}")

        def mmo(e, b=b, pb=pb):
            for m in range(NT):
                for c in range(4):
                    r = e.matmul(k.ps(pb + c), b[:, m * 128:(m + 1) * 128], Y[m][:, c * 512:(c + 1) * 512],
                                 start=(m == 0), stop=(m == NT - 1))
            return r

        k.op("pe", mmo, reads=(b.name,) + tuple(y.name for y in Y), writes=k.pk(pb, 4), tag=f"{tag}omm{t}")
        k.op("dve", lambda e, xo=xo, pb=pb, bc=bias[:, BO + t:BO + t + 1], x2=xn[t]: e.scalar_tensor_tensor(
                xo[:, :], k.ps(pb, slice(0, 128), slice(0, 2048)), bc, x2[:, :], ALU.add, ALU.add),
             reads=k.pk(pb, 4) + (bias.name, xn[t].name), writes=(xo.name,), tag=f"{tag}oev{t}")
        k.op("dve", lambda e, xo=xo, xr=xr: e.tensor_add(xo[:, :], xo[:, :], xr[:, :]),
             reads=(xo.name, xr.name), writes=(xo.name,), tag=f"{tag}ores{t}")
        k.dma("sp", xo_dram[t * 128:(t + 1) * 128, :], xo[:, :],
              reads=(xo.name,), writes=(f"{xo_keyf}:{t}",), group=f"gxw{t % 2}", tag=f"{tag}stx{t}")
    sb.freeb(*xn, *Y, *xrs, *xob, *wb, bias, a_col)


def build_attn0_program(which_io=("ExternalInput", "ExternalOutput")):
    nc = bass.Bass("TRN2", target_bir_lowering=False)
    k = K(nc)
    xin = nc.dram_tensor("xin", [D, S], F32, kind=which_io[0])
    xout = nc.dram_tensor("xout", [D, S], F32, kind=which_io[1])
    W = {}
    for nm in ("a0_wqT", "a0_wkT", "a0_wvT", "a0_woT"):
        W[nm] = nc.dram_tensor(nm, [D, D], F32, kind="ExternalInput")
    W["a0_bias"] = nc.dram_tensor("a0_bias", [128, 64], F32, kind="ExternalInput")
    W["c0_cos"] = nc.dram_tensor("c0_cos", [512, S], BF16, kind="ExternalInput")
    W["c0_sin"] = nc.dram_tensor("c0_sin", [512, S], BF16, kind="ExternalInput")
    c1 = nc.dram_tensor("c_ones_mean", [128, 1], BF16, kind="ExternalInput")
    c2 = nc.dram_tensor("c_ones_row", [1, 128], F32, kind="ExternalInput")
    c3 = nc.dram_tensor("c_one_col", [128, 1], F32, kind="ExternalInput")
    dr = {"c_ones_mean": c1, "c_ones_row": c2, "c_one_col": c3}
    k.psum_init()
    consts = stage_consts(k, dr)
    build_attn0(k, "a0", xin.ap(), "xi", xout.ap(), "xo", consts, W)
    k.P.emit(k.stack)
    return nc, k


def attn0_host_inputs(inputs):
    pi = rope_perm()
    W_in = inputs["L0_attn_W_in"]
    b_in = inputs["L0_attn_b_in"]
    W_out = inputs["L0_attn_W_out"]
    b_out = inputs["L0_attn_b_out"]
    out = {}
    for i, nm in enumerate(("a0_wqT", "a0_wkT", "a0_wvT")):
        blk = W_in[i * D:(i + 1) * D][pi]
        out[nm] = tile_lhsT(blk)
    out["a0_woT"] = tile_lhsT(np.ascontiguousarray(W_out[:, pi]))
    bias = np.zeros((128, 64), np.float32)
    for i in range(3):
        bias[:, 16 * i:16 * i + 16] = col_tiles(b_in[i * D:(i + 1) * D][pi])
    bias[:, 48:64] = col_tiles(b_out)
    out["a0_bias"] = bias
    j = np.arange(512, dtype=np.float64)
    inv_freq = 10000.0 ** (-2.0 * j / 1024.0)
    s = np.arange(S, dtype=np.float64)
    ang = s[None, :] * inv_freq[:, None]
    out["c0_cos"] = np.cos(ang).astype(np.float32).astype(ml_dtypes.bfloat16)
    out["c0_sin"] = np.sin(ang).astype(np.float32).astype(ml_dtypes.bfloat16)
    out["c_ones_mean"] = np.full((128, 1), 1.0 / D, np.float32).astype(ml_dtypes.bfloat16)
    out["c_ones_row"] = np.ones((1, 128), np.float32)
    out["c_one_col"] = np.ones((128, 1), np.float32)
    return out


def tile_lhsT(w):
    O, I = w.shape
    nt, nm = O // 128, I // 128
    return np.ascontiguousarray(w.T.reshape(nm, 128, nt, 128).transpose(2, 1, 0, 3).reshape(nt * 128, O))


def tile_A(wT):
    I, H_ = wT.shape
    nj = H_ // 128
    nt = I // 128
    return np.ascontiguousarray(wT.reshape(nt, 128, nj, 128).transpose(2, 1, 0, 3).reshape(nj * 128, nt * 128))


def tile_B(wdT):
    H_, O = wdT.shape
    nj = H_ // 128
    nt = O // 128
    return np.ascontiguousarray(wdT.reshape(nj, 128, nt, 128).transpose(2, 1, 0, 3).reshape(nt * 128, H_))


def col_tiles(v):
    n = v.shape[0] // 128
    return np.ascontiguousarray(v.reshape(n, 128).T)


def rope_perm():
    p = np.arange(D)
    p[0:512] = np.arange(512) * 2
    p[512:1024] = np.arange(512) * 2 + 1
    return p


def build_ffn0_program(which_io=("ExternalInput", "ExternalOutput")):
    nc = bass.Bass("TRN2", target_bir_lowering=False)
    k = K(nc)
    xin = nc.dram_tensor("xin", [D, S], F32, kind=which_io[0])
    xout = nc.dram_tensor("xout", [D, S], F32, kind=which_io[1])
    wg = nc.dram_tensor("f0_wg", [NJ0 * 128, 2048], I32, kind="ExternalInput")
    wu = nc.dram_tensor("f0_wu", [NJ0 * 128, 2048], I32, kind="ExternalInput")
    wd = nc.dram_tensor("f0_wd", [16 * 128, HID0], I32, kind="ExternalInput")
    sg_d = nc.dram_tensor("f0_sg", [128, NJ0], F32, kind="ExternalInput")
    su_d = nc.dram_tensor("f0_su", [128, NJ0], F32, kind="ExternalInput")
    sd_d = nc.dram_tensor("f0_sd", [128, 16], F32, kind="ExternalInput")
    c1 = nc.dram_tensor("c_ones_mean", [128, 1], BF16, kind="ExternalInput")
    c2 = nc.dram_tensor("c_ones_row", [1, 128], F32, kind="ExternalInput")
    c3 = nc.dram_tensor("c_one_col", [128, 1], F32, kind="ExternalInput")
    dr = {"c_ones_mean": c1, "c_ones_row": c2, "c_one_col": c3}

    k.psum_init()
    consts = stage_consts(k, dr)
    sg = k.sb.alloc("sg", [128, NJ0], F32)
    su = k.sb.alloc("su", [128, NJ0], F32)
    sd = k.sb.alloc("sd", [128, 16], F32)
    sdh = k.sb.alloc("sdh", [128, 16], F32)
    k.dma("sp", sg[:, :], sg_d.ap()[:, :], reads=(), writes=(sg.name,), group="consts")
    k.dma("sp", su[:, :], su_d.ap()[:, :], reads=(), writes=(su.name,), group="consts")
    k.dma("sp", sd[:, :], sd_d.ap()[:, :], reads=(), writes=(sd.name,), group="consts")
    k.op("dve", lambda e: e.tensor_scalar(sdh[:, :], sd[:, :], 0.5, None, ALU.mult),
         reads=(sd.name,), writes=(sdh.name,), tag="sdh")

    build_ffn(k, "f0a", xin.ap(), "xi", xout.ap(), "xo", consts,
              nj=NJ0,
              wgT=[wg.ap()[j * 128:(j + 1) * 128, :] for j in range(NJ0)],
              wuT=[wu.ap()[j * 128:(j + 1) * 128, :] for j in range(NJ0)],
              wdT=[wd.ap()[t * 128:(t + 1) * 128, :] for t in range(16)],
              quant=True, sg=sg, su=su, sdh=sdh)
    k.P.emit(k.stack)
    return nc, k


def ffn0_host_inputs(inputs):
    wgT = tile_A(np.ascontiguousarray(inputs["L0_ffn_wq_gate"].T))
    wuT = tile_A(np.ascontiguousarray(inputs["L0_ffn_wq_up"].T))
    wdT = tile_B(np.ascontiguousarray(inputs["L0_ffn_wq_down"].T))
    return {
        "f0_wg": wgT, "f0_wu": wuT, "f0_wd": wdT,
        "f0_sg": col_tiles(inputs["L0_ffn_s_gate"]).astype(np.float32),
        "f0_su": col_tiles(inputs["L0_ffn_s_up"]).astype(np.float32),
        "f0_sd": col_tiles(inputs["L0_ffn_s_down"]).astype(np.float32),
        "c_ones_mean": np.full((128, 1), 1.0 / D, np.float32).astype(ml_dtypes.bfloat16),
        "c_ones_row": np.ones((1, 128), np.float32),
        "c_one_col": np.ones((128, 1), np.float32),
    }


def build_lutconv_program():
    nc = bass.Bass("TRN2", target_bir_lowering=False)
    k = K(nc)
    dr = {}
    dr["f1_gi"] = nc.dram_tensor("f1_gi", [NJ1 * 128, 2048], I32, kind="ExternalInput")
    dr["f1_ui"] = nc.dram_tensor("f1_ui", [NJ1 * 128, 2048], I32, kind="ExternalInput")
    dr["f1_di"] = nc.dram_tensor("f1_di", [16 * 128, HID1], I32, kind="ExternalInput")
    dr["f1_lut"] = nc.dram_tensor("f1_lut", [1, 16], F32, kind="ExternalInput")
    dr["c_gi"] = nc.dram_tensor("c_gi", [NJ1 * 128, 2048], BF16, kind="ExternalOutput")
    dr["c_ui"] = nc.dram_tensor("c_ui", [NJ1 * 128, 2048], BF16, kind="ExternalOutput")
    dr["c_di"] = nc.dram_tensor("c_di", [16 * 128, HID1], BF16, kind="ExternalOutput")
    c2 = nc.dram_tensor("c_ones_row", [1, 128], F32, kind="ExternalInput")
    k.psum_init()
    orf = k.sb.alloc("ones_row_f", [1, 128], F32)
    k.dma("sp", orf[0:1, :], c2.ap()[:, :], reads=(), writes=("ones_row_f",), group="consts")
    consts = {"ones_row_f": orf}
    srcs = lutconv_srcs(dr)
    steps = build_lutconv(k, "lc", dr["f1_lut"], srcs, consts)
    for st in steps:
        st()
    k.P.emit(k.stack)
    return nc, k


def build_ffn1_program(which_io=("ExternalInput", "ExternalOutput")):
    nc = bass.Bass("TRN2", target_bir_lowering=False)
    k = K(nc)
    xin = nc.dram_tensor("xin", [D, S], F32, kind=which_io[0])
    xout = nc.dram_tensor("xout", [D, S], F32, kind=which_io[1])
    cg = nc.dram_tensor("c_gi", [NJ1 * 128, 2048], BF16, kind="ExternalInput")
    cu = nc.dram_tensor("c_ui", [NJ1 * 128, 2048], BF16, kind="ExternalInput")
    cd = nc.dram_tensor("c_di", [16 * 128, HID1], BF16, kind="ExternalInput")
    c1 = nc.dram_tensor("c_ones_mean", [128, 1], BF16, kind="ExternalInput")
    c2 = nc.dram_tensor("c_ones_row", [1, 128], F32, kind="ExternalInput")
    c3 = nc.dram_tensor("c_one_col", [128, 1], F32, kind="ExternalInput")
    dr = {"c_ones_mean": c1, "c_ones_row": c2, "c_one_col": c3}
    k.psum_init()
    consts = stage_consts(k, dr)
    build_ffn(k, "f1a", xin.ap(), "xi", xout.ap(), "xo", consts,
              nj=NJ1,
              wgT=[cg.ap()[j * 128:(j + 1) * 128, :] for j in range(NJ1)],
              wuT=[cu.ap()[j * 128:(j + 1) * 128, :] for j in range(NJ1)],
              wdT=[cd.ap()[t * 128:(t + 1) * 128, :] for t in range(16)],
              quant=False, sg=None, su=None, sdh=consts["half_col"])
    k.P.emit(k.stack)
    return nc, k


def build_attn1_program(which_io=("ExternalInput", "ExternalOutput")):
    nc = bass.Bass("TRN2", target_bir_lowering=False)
    k = K(nc)
    xin = nc.dram_tensor("xin", [D, S], F32, kind=which_io[0])
    xout = nc.dram_tensor("xout", [D, S], F32, kind=which_io[1])
    W = {}
    for nm in ("a1_wq", "a1_wk", "a1_wv", "a1_wo"):
        W[nm] = nc.dram_tensor(nm, [NHEAD * 128, 2048], F32, kind="ExternalInput")
    W["c1_cosD"] = nc.dram_tensor("c1_cosD", [128, S], F32, kind="ExternalInput")
    W["c1_sinD"] = nc.dram_tensor("c1_sinD", [128, S], F32, kind="ExternalInput")
    W["c1_masks"] = nc.dram_tensor("c1_masks", [128, 4 * 512], BF16, kind="ExternalInput")
    c1 = nc.dram_tensor("c_ones_mean", [128, 1], BF16, kind="ExternalInput")
    c2 = nc.dram_tensor("c_ones_row", [1, 128], F32, kind="ExternalInput")
    c3 = nc.dram_tensor("c_one_col", [128, 1], F32, kind="ExternalInput")
    dr = {"c_ones_mean": c1, "c_ones_row": c2, "c_one_col": c3}
    k.psum_init()
    consts = stage_consts(k, dr)
    dbg = None
    import os
    if os.environ.get("A1DBG"):
        dbg = {nm: nc.dram_tensor(nm, [128, S], BF16, kind="ExternalOutput")
               for nm in ("d_q0", "d_k0", "d_v0", "d_o0", "d_xn0")}
    build_attn1(k, "a1", xin.ap(), "xi", xout.ap(), "xo", consts, W, dbg=dbg)
    k.P.emit(k.stack)
    return nc, k


def base_consts():
    return {
        "c_ones_mean": np.full((128, 1), 1.0 / D, np.float32).astype(ml_dtypes.bfloat16),
        "c_ones_row": np.ones((1, 128), np.float32),
        "c_one_col": np.ones((128, 1), np.float32),
    }


def ffn1_host_inputs(inputs):
    giT = tile_A(np.ascontiguousarray(inputs["L1_ffn_gate_idx"].T))
    uiT = tile_A(np.ascontiguousarray(inputs["L1_ffn_up_idx"].T))
    diT = tile_B(np.ascontiguousarray(inputs["L1_ffn_down_idx"].T))
    return {"f1_gi": giT, "f1_ui": uiT, "f1_di": diT,
            "f1_lut": inputs["L1_ffn_lut"].reshape(1, 16).astype(np.float32)}


def attn1_host_inputs(inputs):
    out = {}
    for nm, key in (("a1_wq", "L1_attn_w_aq"), ("a1_wk", "L1_attn_w_ak"), ("a1_wv", "L1_attn_w_av")):
        w = inputs[key]
        out[nm] = np.ascontiguousarray(
            w.reshape(NHEAD, NT, 128, 128).transpose(0, 2, 1, 3).reshape(NHEAD * 128, 2048))
    wo = inputs["L1_attn_w_ao"].reshape(NHEAD, 128, NT, 128)
    out["a1_wo"] = np.ascontiguousarray(wo.transpose(2, 1, 0, 3).reshape(NT * 128, 2048))
    dm = np.arange(64, dtype=np.float64)
    ang_f = 10000.0 ** (-dm / 64.0)
    s = np.arange(S, dtype=np.float64)
    ang = s[None, :] * ang_f[:, None]
    cos = np.cos(ang)
    sin = np.sin(ang)
    out["c1_cosD"] = np.concatenate([cos, cos], 0).astype(np.float32)
    out["c1_sinD"] = np.concatenate([-sin, sin], 0).astype(np.float32)
    m = np.zeros((128, 4, 512), np.float32)
    for dlt in range(4):
        for qq in range(512):
            qt = qq // 128
            if qt > dlt:
                m[:, dlt, qq] = 1.0
            elif qt == dlt:
                m[:, dlt, qq] = (np.arange(128) <= (qq % 128)).astype(np.float32)
    out["c1_masks"] = m.reshape(128, 4 * 512).astype(ml_dtypes.bfloat16)
    return out


def lutconv_srcs(nc_handles):
    out = []
    for nm, cn in (("f1_gi", "c_gi"), ("f1_ui", "c_ui")):
        s, d = nc_handles[nm].ap(), nc_handles[cn].ap()
        for j in range(NJ1):
            for hh in range(2):
                out.append((s[j * 128:(j + 1) * 128, hh * 1024:(hh + 1) * 1024],
                            d[j * 128:(j + 1) * 128, hh * 1024:(hh + 1) * 1024], f"{cn}:{j}", 1024))
    s, d = nc_handles["f1_di"].ap(), nc_handles["c_di"].ap()
    for t in range(16):
        for hh in range(8):
            w = HID1 // 8
            out.append((s[t * 128:(t + 1) * 128, hh * w:(hh + 1) * w],
                        d[t * 128:(t + 1) * 128, hh * w:(hh + 1) * w], f"c_di:{t}h{hh}", w))
    return out


def build_attn1(k: K, tag, x_dram, x_keyf, xo_dram, xo_keyf, consts, W, dbg=None):
    P, sb = k.P, k.sb
    xn = build_ln(k, f"{tag}ln", x_dram, x_keyf, 0, S, consts)
    cosD = sb.alloc(k.u(f"{tag}cos"), [128, S], F32)
    sinD = sb.alloc(k.u(f"{tag}sin"), [128, S], F32)
    k.dma("sp", cosD[:, :], W["c1_cosD"].ap()[:, :], reads=(), writes=(cosD.name,), group="consts")
    k.dma("sp", sinD[:, :], W["c1_sinD"].ap()[:, :], reads=(), writes=(sinD.name,), group="consts")
    masks = sb.alloc(k.u(f"{tag}msk"), [128, 4 * 512], BF16)
    k.dma("sp", masks[:, :], W["c1_masks"].ap()[:, :], reads=(), writes=(masks.name,), group="consts")
    o_all = [sb.alloc(k.u(f"{tag}o"), [128, S], BF16) for _ in range(NHEAD)]
    GH = 2
    qT = [sb.alloc(k.u(f"{tag}q"), [128, S], BF16) for _ in range(GH)]
    kT = [sb.alloc(k.u(f"{tag}k"), [128, S], BF16) for _ in range(GH)]
    vB = [sb.alloc(k.u(f"{tag}v"), [128, S], BF16) for _ in range(GH)]
    wh = [sb.alloc(k.u(f"{tag}wh"), [128, S], BF16) for _ in range(2)]
    rt = [sb.alloc(k.u(f"{tag}rt"), [128, S], F32) for _ in range(1)]
    rtb = sb.alloc(k.u(f"{tag}rtb"), [128, S], BF16)
    swp = sb.alloc(k.u(f"{tag}swp"), [128, S], BF16)
    ubs = [sb.alloc(k.u(f"{tag}u"), [128, 1024], BF16) for _ in range(2)]
    dnr = sb.alloc(k.u(f"{tag}dnr"), [1, 512], F32)
    of = sb.alloc(k.u(f"{tag}of"), [128, 512], F32)

    def proj_head(h, wdram, gname, dst):
        b = wh[h % 2]
        k.dma("pool", b[:, :], wdram[h * 128:(h + 1) * 128, :],
              reads=(), writes=(b.name,), group=f"gw{h % 2}", tag=f"{tag}{gname}ld{h}")
        pb = 4 * (h % 2)

        def mm(e, b=b, pb=pb):
            for m in range(NT):
                for c in range(4):
                    r = e.matmul(k.ps(pb + c), b[:, m * 128:(m + 1) * 128], xn[m][:, c * 512:(c + 1) * 512],
                                 start=(m == 0), stop=(m == NT - 1))
            return r

        k.op("pe", mm, reads=(b.name,) + tuple(x.name for x in xn), writes=k.pk(pb, 4), tag=f"{tag}{gname}mm{h}")
        k.op("dve", lambda e, d=dst, pb=pb: e.tensor_copy(d[:, :], k.ps(pb, slice(0, 128), slice(0, 2048))),
             reads=k.pk(pb, 4), writes=(dst.name,), tag=f"{tag}{gname}ev{h}")

    def rope1(dst, h, ph):
        k.dma("act", swp[0:64, :], dst[64:128, :], reads=(dst.name,), writes=(swp.name + ":a",),
              group="gsw", tag=f"{tag}{ph}swA{h}")
        k.dma("act", swp[64:128, :], dst[0:64, :], reads=(dst.name,), writes=(swp.name + ":b",),
              group="gsw", tag=f"{tag}{ph}swB{h}")
        t0, t1 = rt[0], rtb
        k.op("dve", lambda e: e.tensor_mul(t0[:, :], dst[:, :], cosD[:, :]),
             reads=(dst.name, cosD.name), writes=(t0.name,), tag=f"{tag}{ph}r1{h}")
        k.op("dve", lambda e: e.tensor_mul(t1[:, :], swp[:, :], sinD[:, :]),
             reads=(swp.name + ":a", swp.name + ":b", sinD.name), writes=(t1.name,), tag=f"{tag}{ph}r2{h}")
        k.op("dve", lambda e: e.tensor_add(dst[:, :], t0[:, :], t1[:, :]),
             reads=(t0.name, t1.name), writes=(dst.name,), tag=f"{tag}{ph}r3{h}")

    wvb = wh
    for g in range(NHEAD // GH):
        heads = tuple(GH * g + i for i in range(GH))
        for h in heads:
            proj_head(h, W["a1_wq"].ap(), "qw", qT[h % GH])
            rope1(qT[h % GH], h, "q")
            proj_head(h, W["a1_wk"].ap(), "kw", kT[h % GH])
            rope1(kT[h % GH], h, "k")
        for h in heads:
            b = wvb[h % GH]
            k.dma("pool", b[:, :], W["a1_wv"].ap()[h * 128:(h + 1) * 128, :],
                  reads=(), writes=(b.name,), group=f"gw{h % 2}", tag=f"{tag}vwld{h}")
        for j in range(NT):
            def mmv(e, j=j, heads=heads):
                for m in range(NT):
                    for hi, h in enumerate(heads):
                        r = e.matmul(k.ps(4 + hi, slice(0, 128), slice(0, 128)),
                                     xn[m][:, j * 128:(j + 1) * 128], wvb[h % GH][:, m * 128:(m + 1) * 128],
                                     start=(m == 0), stop=(m == NT - 1))
                return r

            k.op("pe", mmv, reads=tuple(w.name for w in wvb) + tuple(x.name for x in xn),
                 writes=k.pk(4, GH), tag=f"{tag}vmm{g}_{j}")
            for hi, h in enumerate(heads):
                k.op("dve", lambda e, j=j, hi=hi, h=h: e.tensor_copy(
                        vB[h % GH][:, j * 128:(j + 1) * 128], k.ps(4 + hi, slice(0, 128), slice(0, 128))),
                     reads=k.pk(4 + hi), writes=(vB[h % GH].name + f":{j}",), tag=f"{tag}vev{g}_{j}_{hi}")
        for h in heads:
            q, kk, v = qT[h % GH], kT[h % GH], vB[h % GH]
            vkeys = tuple(v.name + f":{j}" for j in range(NT))
            for p in range(2):
                njp = 8 * p + 8

                def emit_s(j, p=p, kk=kk, q=q):
                    act = [c for c in (2 * p, 2 * p + 1) if j <= 4 * c + 3]
                    c0 = act[0] - 2 * p
                    sb0 = 0 if j % 2 == 0 else 6

                    def mms(e, j=j, act=act, p=p, kk=kk, q=q, sb0=sb0):
                        for c in act:
                            r = e.matmul(k.ps(sb0 + c - 2 * p), kk[:, j * 128:(j + 1) * 128],
                                         q[:, c * 512:(c + 1) * 512], start=True, stop=True)
                        return r

                    k.op("pe", mms, reads=(kk.name, q.name), writes=k.pk(sb0 + c0, len(act)), tag=f"{tag}s{h}_{p}_{j}")

                emit_s(0)
                for j in range(njp):
                    act = [c for c in (2 * p, 2 * p + 1) if j <= 4 * c + 3]
                    c0 = act[0] - 2 * p
                    nact = len(act)
                    sb0 = 0 if j % 2 == 0 else 6
                    ub = ubs[j % 2]
                    if j + 1 < njp:
                        emit_s(j + 1)
                    k.op("act", lambda e, c0=c0, nact=nact, sb0=sb0, ub=ub: e.activation(
                            ub[:, c0 * 512:(c0 + nact) * 512], k.ps(sb0 + c0, slice(0, 128), slice(0, nact * 512)),
                            AF.Exp, bias=0.0, scale=1.0 / 128.0),
                         reads=k.pk(sb0 + c0, nact), writes=(ub.name,), tag=f"{tag}e{h}_{p}_{j}")
                    for c in act:
                        dlt = j - 4 * c
                        if 0 <= dlt <= 3:
                            k.op("dve", lambda e, c=c, p=p, dlt=dlt, ub=ub: e.tensor_mul(
                                    ub[:, (c - 2 * p) * 512:(c - 2 * p + 1) * 512],
                                    ub[:, (c - 2 * p) * 512:(c - 2 * p + 1) * 512],
                                    masks[:, dlt * 512:(dlt + 1) * 512]),
                                 reads=(ub.name, masks.name), writes=(ub.name,), tag=f"{tag}m{h}_{p}_{j}_{c}")

                    def mmo(e, j=j, act=act, p=p, v=v, ub=ub):
                        for c in act:
                            e.matmul(k.ps(2 + c - 2 * p), v[:, j * 128:(j + 1) * 128],
                                     ub[:, (c - 2 * p) * 512:(c - 2 * p + 1) * 512],
                                     start=(j == 0), stop=(j == 4 * c + 3))
                            r = e.matmul(k.ps(4 + c - 2 * p, slice(0, 1), slice(0, 512)),
                                         consts["ones1_bf"][:, :], ub[:, (c - 2 * p) * 512:(c - 2 * p + 1) * 512],
                                         start=(j == 0), stop=(j == 4 * c + 3))
                        return r

                    k.op("pe", mmo, reads=(ub.name, "ones1_bf") + vkeys,
                         writes=k.pk(2, 2) + k.pk(4, 2), tag=f"{tag}o{h}_{p}_{j}")
                for c in (2 * p, 2 * p + 1):
                    ci = c - 2 * p
                    k.op("dve", lambda e, ci=ci: e.tensor_copy(dnr[0:1, :], k.ps(4 + ci, slice(0, 1), slice(0, 512))),
                         reads=k.pk(4 + ci), writes=(dnr.name,), tag=f"{tag}dn{h}_{c}")
                    k.op("dve", lambda e: e.reciprocal(dnr[0:1, :], dnr[0:1, :]),
                         reads=(dnr.name,), writes=(dnr.name,), tag=f"{tag}rc{h}_{c}")

                    def mmb(e):
                        return e.matmul(k.ps(6, slice(0, 128), slice(0, 512)), consts["ones_row_f"][:, :],
                                        dnr[0:1, 0:512], start=True, stop=True)

                    k.op("pe", mmb, reads=(dnr.name, "ones_row_f"), writes=k.pk(6), tag=f"{tag}b{h}_{c}")
                    k.op("act", lambda e, ci=ci: e.copy(of[:, :], k.ps(2 + ci, slice(0, 128), slice(0, 512))),
                         reads=k.pk(2 + ci), writes=(of.name,), tag=f"{tag}cf{h}_{c}")
                    k.op("dve", lambda e, c=c, h=h: e.tensor_mul(o_all[h][:, c * 512:(c + 1) * 512], of[:, :],
                                                                 k.ps(6, slice(0, 128), slice(0, 512))),
                         reads=(of.name,) + k.pk(6), writes=(o_all[h].name,), tag=f"{tag}on{h}_{c}")
    if dbg is not None:
        for nm, buf in (("d_q0", qT[0]), ("d_k0", kT[0]), ("d_v0", vB[0]), ("d_o0", o_all[0]), ("d_xn0", xn[0])):
            k.dma("sp", dbg[nm].ap()[:, :], buf[:, :], reads=(buf.name,), writes=(nm,), group="dbg", tag=f"dbg{nm}")
    sb.freeb(*qT, *kT, *vB, *rt, rtb, swp, *ubs, dnr, of, cosD, sinD, masks)
    sb.freeb(*xn)
    xrs = [sb.alloc(k.u(f"{tag}xr"), [128, S], F32) for _ in range(2)]
    xob = [sb.alloc(k.u(f"{tag}xo"), [128, S], F32) for _ in range(2)]
    for t in reversed(range(NT)):
        b = wh[t % 2]
        pb = 4 * (t % 2)
        xr, xo = xrs[t % 2], xob[t % 2]
        k.dma("pool", b[:, :], W["a1_wo"].ap()[t * 128:(t + 1) * 128, :],
              reads=(), writes=(b.name,), group=f"gw{t % 2}", tag=f"{tag}old{t}")
        k.dma("sp", xr[:, :], x_dram[t * 128:(t + 1) * 128, :],
              reads=(f"{x_keyf}:{t}",), writes=(xr.name,), group=f"gxr{t % 2}", tag=f"{tag}ldxr{t}")

        def mmo2(e, b=b, pb=pb):
            for h in range(NHEAD):
                for c in range(4):
                    r = e.matmul(k.ps(pb + c), b[:, h * 128:(h + 1) * 128], o_all[h][:, c * 512:(c + 1) * 512],
                                 start=(h == 0), stop=(h == NHEAD - 1))
            return r

        k.op("pe", mmo2, reads=(b.name,) + tuple(o.name for o in o_all), writes=k.pk(pb, 4), tag=f"{tag}omm{t}")
        k.op("dve", lambda e, xo=xo, xr=xr, pb=pb: e.tensor_add(xo[:, :], k.ps(pb, slice(0, 128), slice(0, 2048)), xr[:, :]),
             reads=k.pk(pb, 4) + (xr.name,), writes=(xo.name,), tag=f"{tag}oev{t}")
        k.dma("sp", xo_dram[t * 128:(t + 1) * 128, :], xo[:, :],
              reads=(xo.name,), writes=(f"{xo_keyf}:{t}",), group=f"gxw{t % 2}", tag=f"{tag}stx{t}")
    sb.freeb(*o_all, *wh, *xrs, *xob)


def build_attn0(k: K, tag, x_dram, x_keyf, xo_dram, xo_keyf, consts, W):
    P, sb = k.P, k.sb
    xn = build_ln(k, f"{tag}ln", x_dram, x_keyf, 0, S, consts)
    bias = sb.alloc(k.u(f"{tag}bias"), [128, 64], F32)
    k.dma("sp", bias[:, :], W["a0_bias"].ap()[:, :], reads=(), writes=(bias.name,), group="consts")
    BQ, BK, BV, BO = 0, 16, 32, 48

    wb = [sb.alloc(k.u(f"{tag}wb"), [128, 2048], BF16) for _ in range(2)]

    def proj(dst, wT_dram, bias_col, ph):
        for t in range(NT):
            k.pump()
            b = wb[t % 2]
            pb = 4 * (t % 2)
            k.dma("pool", b[:, :], wT_dram[t * 128:(t + 1) * 128, :],
                  reads=(), writes=(b.name,), group=f"gw{t % 2}", tag=f"{tag}{ph}ld{t}")

            def mm(e, b=b, pb=pb):
                for m in range(NT):
                    for c in range(4):
                        r = e.matmul(k.ps(pb + c), b[:, m * 128:(m + 1) * 128], xn[m][:, c * 512:(c + 1) * 512],
                                     start=(m == 0), stop=(m == NT - 1))
                return r

            k.op("pe", mm, reads=(b.name,) + tuple(x.name for x in xn), writes=k.pk(pb, 4), tag=f"{tag}{ph}mm{t}")
            k.op("act", lambda e, d=dst[t], pb=pb, bc=bias[:, bias_col + t:bias_col + t + 1]: e.activation(
                    d[:, :], k.ps(pb, slice(0, 128), slice(0, 2048)), AF.Identity, bias=bc, scale=1.0),
                 reads=k.pk(pb, 4) + (bias.name,), writes=(dst[t].name,), tag=f"{tag}{ph}ev{t}")

    def rope(dst, ph):
        tcs = [sb.alloc(k.u(f"{tag}rt"), [128, S], F32) for _ in range(2)]
        tcs.append(sb.alloc(k.u(f"{tag}rtb"), [128, S], BF16))
        cosb = sb.alloc(k.u(f"{tag}cos"), [128, S], BF16)
        sinb = sb.alloc(k.u(f"{tag}sin"), [128, S], BF16)
        for p in range(4):
            k.dma("sp", cosb[:, :], W["c0_cos"].ap()[p * 128:(p + 1) * 128, :],
                  reads=(), writes=(cosb.name,), group="gcos", tag=f"{tag}{ph}ldc{p}")
            k.dma("sp", sinb[:, :], W["c0_sin"].ap()[p * 128:(p + 1) * 128, :],
                  reads=(), writes=(sinb.name,), group="gsin", tag=f"{tag}{ph}lds{p}")
            a, c, t2 = tcs[0], tcs[1], tcs[2]
            x0, x1 = dst[p], dst[p + 4]
            for (o, i0, i1, tg) in ((a, x0, cosb, "a"), (c, x0, sinb, "c"), (t2, x1, sinb, "b")):
                k.op("dve", lambda e, o=o, i0=i0, i1=i1: e.tensor_mul(o[:, :], i0[:, :], i1[:, :]),
                     reads=(i0.name, i1.name), writes=(o.name,), tag=f"{tag}{ph}rp{tg}{p}")
            k.op("dve", lambda e, x0=x0, a=a, t2=t2: e.tensor_sub(x0[:, :], a[:, :], t2[:, :]),
                 reads=(a.name, t2.name), writes=(x0.name,), tag=f"{tag}{ph}rs{p}")
            k.op("dve", lambda e, t2=t2, x1=x1: e.tensor_mul(t2[:, :], x1[:, :], cosb[:, :]),
                 reads=(x1.name, cosb.name), writes=(t2.name,), tag=f"{tag}{ph}rp d{p}")
            k.op("dve", lambda e, x1=x1, c=c, t2=t2: e.tensor_add(x1[:, :], c[:, :], t2[:, :]),
                 reads=(c.name, t2.name), writes=(x1.name,), tag=f"{tag}{ph}ra{p}")
        sb.freeb(*tcs, cosb, sinb)

    K_t = [sb.alloc(k.u(f"{tag}K"), [128, S], BF16) for _ in range(NT)]
    proj(K_t, W["a0_wkT"].ap(), BK, "k")
    rope(K_t, "k")
    ksq = [sb.alloc(k.u(f"{tag}ksq"), [128, S], BF16) for _ in range(1)]
    for t in range(NT):
        q = ksq[0]
        k.op("act", lambda e, q=q, s=K_t[t]: e.activation(q[:, :], s[:, :], AF.Square),
             reads=(K_t[t].name,), writes=(q.name,), tag=f"{tag}ksq{t}")

        def mmn(e, q=q, t=t):
            for c in range(4):
                r = e.matmul(k.ps(c, slice(0, 1)), consts["ones1_bf"][:, :], q[:, c * 512:(c + 1) * 512],
                             start=(t == 0), stop=(t == NT - 1))
            return r

        k.op("pe", mmn, reads=(q.name, "ones1_bf"), writes=k.pk(0, 4), tag=f"{tag}mmn{t}")
    nrm = sb.alloc(k.u(f"{tag}nrm"), [1, S], F32)
    invn = sb.alloc(k.u(f"{tag}invn"), [1, S], F32)
    k.op("dve", lambda e: e.tensor_copy(nrm[0:1, :], k.ps(0, slice(0, 1), slice(0, 2048))),
         reads=k.pk(0, 4), writes=(nrm.name,), tag=f"{tag}cpn")
    k.op("act", lambda e: e.activation(invn[0:1, :], nrm[0:1, :], AF.Sqrt),
         reads=(nrm.name,), writes=(invn.name,), tag=f"{tag}sqn")
    k.op("dve", lambda e: e.tensor_scalar(invn[0:1, :], invn[0:1, :], 1e-12, None, ALU.max),
         reads=(invn.name,), writes=(invn.name,), tag=f"{tag}mxn")
    k.op("dve", lambda e: e.reciprocal(invn[0:1, :], invn[0:1, :]),
         reads=(invn.name,), writes=(invn.name,), tag=f"{tag}rcn")
    invb = sb.alloc(k.u(f"{tag}invb"), [128, S], F32)

    def mm_bcn(e):
        for c in range(4):
            r = e.matmul(k.ps(4 + c), consts["ones_row_f"][:, :], invn[0:1, c * 512:(c + 1) * 512], start=True, stop=True)
        return r

    k.op("pe", mm_bcn, reads=(invn.name, "ones_row_f"), writes=k.pk(4, 4), tag=f"{tag}bcn")
    k.op("dve", lambda e: e.tensor_copy(invb[:, :], k.ps(4, slice(0, 128), slice(0, 2048))),
         reads=k.pk(4, 4), writes=(invb.name,), tag=f"{tag}cpb")
    sb.freeb(*ksq, nrm)

    a_col = sb.alloc(k.u(f"{tag}acol"), [128, 16], F32)
    vt = [sb.alloc(k.u(f"{tag}vt"), [128, S], BF16) for _ in range(1)]
    kv = [sb.alloc(k.u(f"{tag}kv"), [128, S], F32) for _ in range(1)]
    for t in range(NT):
        k.pump()
        b = wb[t % 2]
        pb = 4 * (t % 2)
        v = vt[0]
        kvb = kv[0]
        k.dma("pool", b[:, :], W["a0_wvT"].ap()[t * 128:(t + 1) * 128, :],
              reads=(), writes=(b.name,), group=f"gw{t % 2}", tag=f"{tag}vld{t}")

        def mmv(e, b=b, pb=pb):
            for m in range(NT):
                for c in range(4):
                    r = e.matmul(k.ps(pb + c), b[:, m * 128:(m + 1) * 128], xn[m][:, c * 512:(c + 1) * 512],
                                 start=(m == 0), stop=(m == NT - 1))
            return r

        k.op("pe", mmv, reads=(b.name,) + tuple(x.name for x in xn), writes=k.pk(pb, 4), tag=f"{tag}vmm{t}")
        k.op("act", lambda e, v=v, pb=pb, bc=bias[:, BV + t:BV + t + 1]: e.activation(
                v[:, :], k.ps(pb, slice(0, 128), slice(0, 2048)), AF.Identity, bias=bc, scale=1.0),
             reads=k.pk(pb, 4) + (bias.name,), writes=(v.name,), tag=f"{tag}vev{t}")
        k.op("dve", lambda e, kvb=kvb, v=v, s=K_t[t]: e.tensor_mul(kvb[:, :], s[:, :], v[:, :]),
             reads=(K_t[t].name, v.name), writes=(kvb.name,), tag=f"{tag}kv{t}")
        k.op("dve", lambda e, kvb=kvb: e.tensor_mul(kvb[:, :], kvb[:, :], invb[:, :]),
             reads=(kvb.name, invb.name), writes=(kvb.name,), tag=f"{tag}kvn{t}")
        k.op("dve", lambda e, kvb=kvb, t=t: e.reduce_sum(a_col[:, t:t + 1], kvb[:, :], AX.X),
             reads=(kvb.name,), writes=(a_col.name + f":{t}",), tag=f"{tag}A{t}")
    sb.freeb(*vt, *kv, invb, invn)
    sb.freeb(*K_t)

    Y = [sb.alloc(k.u(f"{tag}Y"), [128, S], BF16) for _ in range(NT)]
    proj(Y, W["a0_wqT"].ap(), BQ, "q")
    rope(Y, "q")
    for t in range(NT):
        k.op("dve", lambda e, y=Y[t], sc=a_col[:, t:t + 1]: e.tensor_scalar(y[:, :], y[:, :], sc, None, ALU.mult),
             reads=(Y[t].name, a_col.name + f":{t}", a_col.name), writes=(Y[t].name,), tag=f"{tag}sc{t}")

    xrs = [sb.alloc(k.u(f"{tag}xr"), [128, S], F32) for _ in range(2)]
    xob = [sb.alloc(k.u(f"{tag}xo"), [128, S], F32) for _ in range(1)]
    for t in reversed(range(NT)):
        b = wb[t % 2]
        pb = 4 * (t % 2)
        xr, xo = xrs[t % 2], xob[0]
        k.dma("pool", b[:, :], W["a0_woT"].ap()[t * 128:(t + 1) * 128, :],
              reads=(), writes=(b.name,), group=f"gw{t % 2}", tag=f"{tag}old{t}")
        k.dma("sp", xr[:, :], x_dram[t * 128:(t + 1) * 128, :],
              reads=(f"{x_keyf}:{t}",), writes=(xr.name,), group=f"gxr{t % 2}", tag=f"{tag}ldxr{t}")

        def mmo(e, b=b, pb=pb):
            for m in range(NT):
                for c in range(4):
                    r = e.matmul(k.ps(pb + c), b[:, m * 128:(m + 1) * 128], Y[m][:, c * 512:(c + 1) * 512],
                                 start=(m == 0), stop=(m == NT - 1))
            return r

        k.op("pe", mmo, reads=(b.name,) + tuple(y.name for y in Y), writes=k.pk(pb, 4), tag=f"{tag}omm{t}")
        k.op("dve", lambda e, xo=xo, pb=pb, bc=bias[:, BO + t:BO + t + 1], x2=xn[t]: e.scalar_tensor_tensor(
                xo[:, :], k.ps(pb, slice(0, 128), slice(0, 2048)), bc, x2[:, :], ALU.add, ALU.add),
             reads=k.pk(pb, 4) + (bias.name, xn[t].name), writes=(xo.name,), tag=f"{tag}oev{t}")
        k.op("dve", lambda e, xo=xo, xr=xr: e.tensor_add(xo[:, :], xo[:, :], xr[:, :]),
             reads=(xo.name, xr.name), writes=(xo.name,), tag=f"{tag}ores{t}")
        k.dma("sp", xo_dram[t * 128:(t + 1) * 128, :], xo[:, :],
              reads=(xo.name,), writes=(f"{xo_keyf}:{t}",), group=f"gxw{t % 2}", tag=f"{tag}stx{t}")
    sb.freeb(*xn, *Y, *xrs, *xob, *wb, bias, a_col)


def build_attn0_program(which_io=("ExternalInput", "ExternalOutput")):
    nc = bass.Bass("TRN2", target_bir_lowering=False)
    k = K(nc)
    xin = nc.dram_tensor("xin", [D, S], F32, kind=which_io[0])
    xout = nc.dram_tensor("xout", [D, S], F32, kind=which_io[1])
    W = {}
    for nm in ("a0_wqT", "a0_wkT", "a0_wvT", "a0_woT"):
        W[nm] = nc.dram_tensor(nm, [D, D], F32, kind="ExternalInput")
    W["a0_bias"] = nc.dram_tensor("a0_bias", [128, 64], F32, kind="ExternalInput")
    W["c0_cos"] = nc.dram_tensor("c0_cos", [512, S], BF16, kind="ExternalInput")
    W["c0_sin"] = nc.dram_tensor("c0_sin", [512, S], BF16, kind="ExternalInput")
    c1 = nc.dram_tensor("c_ones_mean", [128, 1], BF16, kind="ExternalInput")
    c2 = nc.dram_tensor("c_ones_row", [1, 128], F32, kind="ExternalInput")
    c3 = nc.dram_tensor("c_one_col", [128, 1], F32, kind="ExternalInput")
    dr = {"c_ones_mean": c1, "c_ones_row": c2, "c_one_col": c3}
    k.psum_init()
    consts = stage_consts(k, dr)
    build_attn0(k, "a0", xin.ap(), "xi", xout.ap(), "xo", consts, W)
    k.P.emit(k.stack)
    return nc, k


def attn0_host_inputs(inputs):
    pi = rope_perm()
    W_in = inputs["L0_attn_W_in"]
    b_in = inputs["L0_attn_b_in"]
    W_out = inputs["L0_attn_W_out"]
    b_out = inputs["L0_attn_b_out"]
    out = {}
    for i, nm in enumerate(("a0_wqT", "a0_wkT", "a0_wvT")):
        blk = W_in[i * D:(i + 1) * D][pi]
        out[nm] = tile_lhsT(blk)
    out["a0_woT"] = tile_lhsT(np.ascontiguousarray(W_out[:, pi]))
    bias = np.zeros((128, 64), np.float32)
    for i in range(3):
        bias[:, 16 * i:16 * i + 16] = col_tiles(b_in[i * D:(i + 1) * D][pi])
    bias[:, 48:64] = col_tiles(b_out)
    out["a0_bias"] = bias
    j = np.arange(512, dtype=np.float64)
    inv_freq = 10000.0 ** (-2.0 * j / 1024.0)
    s = np.arange(S, dtype=np.float64)
    ang = s[None, :] * inv_freq[:, None]
    out["c0_cos"] = np.cos(ang).astype(np.float32).astype(ml_dtypes.bfloat16)
    out["c0_sin"] = np.sin(ang).astype(np.float32).astype(ml_dtypes.bfloat16)
    out["c_ones_mean"] = np.full((128, 1), 1.0 / D, np.float32).astype(ml_dtypes.bfloat16)
    out["c_ones_row"] = np.ones((1, 128), np.float32)
    out["c_one_col"] = np.ones((128, 1), np.float32)
    return out


def tile_lhsT(w):
    O, I = w.shape
    nt, nm = O // 128, I // 128
    return np.ascontiguousarray(w.T.reshape(nm, 128, nt, 128).transpose(2, 1, 0, 3).reshape(nt * 128, O))


def tile_A(wT):
    I, H_ = wT.shape
    nj = H_ // 128
    nt = I // 128
    return np.ascontiguousarray(wT.reshape(nt, 128, nj, 128).transpose(2, 1, 0, 3).reshape(nj * 128, nt * 128))


def tile_B(wdT):
    H_, O = wdT.shape
    nj = H_ // 128
    nt = O // 128
    return np.ascontiguousarray(wdT.reshape(nj, 128, nt, 128).transpose(2, 1, 0, 3).reshape(nt * 128, H_))


def col_tiles(v):
    n = v.shape[0] // 128
    return np.ascontiguousarray(v.reshape(n, 128).T)


def rope_perm():
    p = np.arange(D)
    p[0:512] = np.arange(512) * 2
    p[512:1024] = np.arange(512) * 2 + 1
    return p


def build_ffn0_program(which_io=("ExternalInput", "ExternalOutput")):
    nc = bass.Bass("TRN2", target_bir_lowering=False)
    k = K(nc)
    xin = nc.dram_tensor("xin", [D, S], F32, kind=which_io[0])
    xout = nc.dram_tensor("xout", [D, S], F32, kind=which_io[1])
    wg = nc.dram_tensor("f0_wg", [NJ0 * 128, 2048], I32, kind="ExternalInput")
    wu = nc.dram_tensor("f0_wu", [NJ0 * 128, 2048], I32, kind="ExternalInput")
    wd = nc.dram_tensor("f0_wd", [16 * 128, HID0], I32, kind="ExternalInput")
    sg_d = nc.dram_tensor("f0_sg", [128, NJ0], F32, kind="ExternalInput")
    su_d = nc.dram_tensor("f0_su", [128, NJ0], F32, kind="ExternalInput")
    sd_d = nc.dram_tensor("f0_sd", [128, 16], F32, kind="ExternalInput")
    c1 = nc.dram_tensor("c_ones_mean", [128, 1], BF16, kind="ExternalInput")
    c2 = nc.dram_tensor("c_ones_row", [1, 128], F32, kind="ExternalInput")
    c3 = nc.dram_tensor("c_one_col", [128, 1], F32, kind="ExternalInput")
    dr = {"c_ones_mean": c1, "c_ones_row": c2, "c_one_col": c3}

    k.psum_init()
    consts = stage_consts(k, dr)
    sg = k.sb.alloc("sg", [128, NJ0], F32)
    su = k.sb.alloc("su", [128, NJ0], F32)
    sd = k.sb.alloc("sd", [128, 16], F32)
    sdh = k.sb.alloc("sdh", [128, 16], F32)
    k.dma("sp", sg[:, :], sg_d.ap()[:, :], reads=(), writes=(sg.name,), group="consts")
    k.dma("sp", su[:, :], su_d.ap()[:, :], reads=(), writes=(su.name,), group="consts")
    k.dma("sp", sd[:, :], sd_d.ap()[:, :], reads=(), writes=(sd.name,), group="consts")
    k.op("dve", lambda e: e.tensor_scalar(sdh[:, :], sd[:, :], 0.5, None, ALU.mult),
         reads=(sd.name,), writes=(sdh.name,), tag="sdh")

    build_ffn(k, "f0a", xin.ap(), "xi", xout.ap(), "xo", consts,
              nj=NJ0,
              wgT=[wg.ap()[j * 128:(j + 1) * 128, :] for j in range(NJ0)],
              wuT=[wu.ap()[j * 128:(j + 1) * 128, :] for j in range(NJ0)],
              wdT=[wd.ap()[t * 128:(t + 1) * 128, :] for t in range(16)],
              quant=True, sg=sg, su=su, sdh=sdh)
    k.P.emit(k.stack)
    return nc, k


def ffn0_host_inputs(inputs):
    wgT = tile_A(np.ascontiguousarray(inputs["L0_ffn_wq_gate"].T))
    wuT = tile_A(np.ascontiguousarray(inputs["L0_ffn_wq_up"].T))
    wdT = tile_B(np.ascontiguousarray(inputs["L0_ffn_wq_down"].T))
    return {
        "f0_wg": wgT, "f0_wu": wuT, "f0_wd": wdT,
        "f0_sg": col_tiles(inputs["L0_ffn_s_gate"]).astype(np.float32),
        "f0_su": col_tiles(inputs["L0_ffn_s_up"]).astype(np.float32),
        "f0_sd": col_tiles(inputs["L0_ffn_s_down"]).astype(np.float32),
        "c_ones_mean": np.full((128, 1), 1.0 / D, np.float32).astype(ml_dtypes.bfloat16),
        "c_ones_row": np.ones((1, 128), np.float32),
        "c_one_col": np.ones((128, 1), np.float32),
    }


def build_lutconv_program():
    nc = bass.Bass("TRN2", target_bir_lowering=False)
    k = K(nc)
    dr = {}
    dr["f1_gi"] = nc.dram_tensor("f1_gi", [NJ1 * 128, 2048], I32, kind="ExternalInput")
    dr["f1_ui"] = nc.dram_tensor("f1_ui", [NJ1 * 128, 2048], I32, kind="ExternalInput")
    dr["f1_di"] = nc.dram_tensor("f1_di", [16 * 128, HID1], I32, kind="ExternalInput")
    dr["f1_lut"] = nc.dram_tensor("f1_lut", [1, 16], F32, kind="ExternalInput")
    dr["c_gi"] = nc.dram_tensor("c_gi", [NJ1 * 128, 2048], BF16, kind="ExternalOutput")
    dr["c_ui"] = nc.dram_tensor("c_ui", [NJ1 * 128, 2048], BF16, kind="ExternalOutput")
    dr["c_di"] = nc.dram_tensor("c_di", [16 * 128, HID1], BF16, kind="ExternalOutput")
    c2 = nc.dram_tensor("c_ones_row", [1, 128], F32, kind="ExternalInput")
    k.psum_init()
    orf = k.sb.alloc("ones_row_f", [1, 128], F32)
    k.dma("sp", orf[0:1, :], c2.ap()[:, :], reads=(), writes=("ones_row_f",), group="consts")
    consts = {"ones_row_f": orf}
    srcs = lutconv_srcs(dr)
    steps = build_lutconv(k, "lc", dr["f1_lut"], srcs, consts)
    for st in steps:
        st()
    k.P.emit(k.stack)
    return nc, k


def build_ffn1_program(which_io=("ExternalInput", "ExternalOutput")):
    nc = bass.Bass("TRN2", target_bir_lowering=False)
    k = K(nc)
    xin = nc.dram_tensor("xin", [D, S], F32, kind=which_io[0])
    xout = nc.dram_tensor("xout", [D, S], F32, kind=which_io[1])
    cg = nc.dram_tensor("c_gi", [NJ1 * 128, 2048], BF16, kind="ExternalInput")
    cu = nc.dram_tensor("c_ui", [NJ1 * 128, 2048], BF16, kind="ExternalInput")
    cd = nc.dram_tensor("c_di", [16 * 128, HID1], BF16, kind="ExternalInput")
    c1 = nc.dram_tensor("c_ones_mean", [128, 1], BF16, kind="ExternalInput")
    c2 = nc.dram_tensor("c_ones_row", [1, 128], F32, kind="ExternalInput")
    c3 = nc.dram_tensor("c_one_col", [128, 1], F32, kind="ExternalInput")
    dr = {"c_ones_mean": c1, "c_ones_row": c2, "c_one_col": c3}
    k.psum_init()
    consts = stage_consts(k, dr)
    build_ffn(k, "f1a", xin.ap(), "xi", xout.ap(), "xo", consts,
              nj=NJ1,
              wgT=[cg.ap()[j * 128:(j + 1) * 128, :] for j in range(NJ1)],
              wuT=[cu.ap()[j * 128:(j + 1) * 128, :] for j in range(NJ1)],
              wdT=[cd.ap()[t * 128:(t + 1) * 128, :] for t in range(16)],
              quant=False, sg=None, su=None, sdh=consts["half_col"])
    k.P.emit(k.stack)
    return nc, k


def build_attn1_program(which_io=("ExternalInput", "ExternalOutput")):
    nc = bass.Bass("TRN2", target_bir_lowering=False)
    k = K(nc)
    xin = nc.dram_tensor("xin", [D, S], F32, kind=which_io[0])
    xout = nc.dram_tensor("xout", [D, S], F32, kind=which_io[1])
    W = {}
    for nm in ("a1_wq", "a1_wk", "a1_wv", "a1_wo"):
        W[nm] = nc.dram_tensor(nm, [NHEAD * 128, 2048], F32, kind="ExternalInput")
    W["c1_cosD"] = nc.dram_tensor("c1_cosD", [128, S], F32, kind="ExternalInput")
    W["c1_sinD"] = nc.dram_tensor("c1_sinD", [128, S], F32, kind="ExternalInput")
    W["c1_masks"] = nc.dram_tensor("c1_masks", [128, 4 * 512], BF16, kind="ExternalInput")
    c1 = nc.dram_tensor("c_ones_mean", [128, 1], BF16, kind="ExternalInput")
    c2 = nc.dram_tensor("c_ones_row", [1, 128], F32, kind="ExternalInput")
    c3 = nc.dram_tensor("c_one_col", [128, 1], F32, kind="ExternalInput")
    dr = {"c_ones_mean": c1, "c_ones_row": c2, "c_one_col": c3}
    k.psum_init()
    consts = stage_consts(k, dr)
    dbg = None
    import os
    if os.environ.get("A1DBG"):
        dbg = {nm: nc.dram_tensor(nm, [128, S], BF16, kind="ExternalOutput")
               for nm in ("d_q0", "d_k0", "d_v0", "d_o0", "d_xn0")}
    build_attn1(k, "a1", xin.ap(), "xi", xout.ap(), "xo", consts, W, dbg=dbg)
    k.P.emit(k.stack)
    return nc, k


def base_consts():
    return {
        "c_ones_mean": np.full((128, 1), 1.0 / D, np.float32).astype(ml_dtypes.bfloat16),
        "c_ones_row": np.ones((1, 128), np.float32),
        "c_one_col": np.ones((128, 1), np.float32),
    }


def ffn1_host_inputs(inputs):
    giT = tile_A(np.ascontiguousarray(inputs["L1_ffn_gate_idx"].T))
    uiT = tile_A(np.ascontiguousarray(inputs["L1_ffn_up_idx"].T))
    diT = tile_B(np.ascontiguousarray(inputs["L1_ffn_down_idx"].T))
    return {"f1_gi": giT, "f1_ui": uiT, "f1_di": diT,
            "f1_lut": inputs["L1_ffn_lut"].reshape(1, 16).astype(np.float32)}


def attn1_host_inputs(inputs):
    out = {}
    for nm, key in (("a1_wq", "L1_attn_w_aq"), ("a1_wk", "L1_attn_w_ak"), ("a1_wv", "L1_attn_w_av")):
        w = inputs[key]
        out[nm] = np.ascontiguousarray(
            w.reshape(NHEAD, NT, 128, 128).transpose(0, 2, 1, 3).reshape(NHEAD * 128, 2048))
    wo = inputs["L1_attn_w_ao"].reshape(NHEAD, 128, NT, 128)
    out["a1_wo"] = np.ascontiguousarray(wo.transpose(2, 1, 0, 3).reshape(NT * 128, 2048))
    dm = np.arange(64, dtype=np.float64)
    ang_f = 10000.0 ** (-dm / 64.0)
    s = np.arange(S, dtype=np.float64)
    ang = s[None, :] * ang_f[:, None]
    cos = np.cos(ang)
    sin = np.sin(ang)
    out["c1_cosD"] = np.concatenate([cos, cos], 0).astype(np.float32)
    out["c1_sinD"] = np.concatenate([-sin, sin], 0).astype(np.float32)
    m = np.zeros((128, 4, 512), np.float32)
    for dlt in range(4):
        for qq in range(512):
            qt = qq // 128
            if qt > dlt:
                m[:, dlt, qq] = 1.0
            elif qt == dlt:
                m[:, dlt, qq] = (np.arange(128) <= (qq % 128)).astype(np.float32)
    out["c1_masks"] = m.reshape(128, 4 * 512).astype(ml_dtypes.bfloat16)
    return out


CONV_TILES = 2 * NJ1 + (16 * HID1) // 2048
CONV_PAD = 136


def build_full_program(ncores=NCORES):
    nc = bass.Bass("TRN2", target_bir_lowering=False)
    k = K(nc)
    xin = nc.dram_tensor("xin", [D, S], F32, kind="ExternalInput")
    xout = nc.dram_tensor("xout", [D, S], F32, kind="ExternalOutput")
    xh = nc.dram_tensor("xh", [D, S], F32, kind="Internal")
    dr = {}
    dr["f0_wg"] = nc.dram_tensor("f0_wg", [NJ0 * 128, 2048], I32, kind="ExternalInput")
    dr["f0_wu"] = nc.dram_tensor("f0_wu", [NJ0 * 128, 2048], I32, kind="ExternalInput")
    dr["f0_wd"] = nc.dram_tensor("f0_wd", [16 * 128, HID0], I32, kind="ExternalInput")
    for nm in ("f0_sg", "f0_su"):
        dr[nm] = nc.dram_tensor(nm, [128, NJ0], F32, kind="ExternalInput")
    dr["f0_sd"] = nc.dram_tensor("f0_sd", [128, 16], F32, kind="ExternalInput")
    for nm in ("a0_wqT", "a0_wkT", "a0_wvT", "a0_woT"):
        dr[nm] = nc.dram_tensor(nm, [D, D], F32, kind="ExternalInput")
    dr["a0_bias"] = nc.dram_tensor("a0_bias", [128, 64], F32, kind="ExternalInput")
    dr["c0_cos"] = nc.dram_tensor("c0_cos", [512, S], BF16, kind="ExternalInput")
    dr["c0_sin"] = nc.dram_tensor("c0_sin", [512, S], BF16, kind="ExternalInput")
    dr["f1_lut"] = nc.dram_tensor("f1_lut", [1, 16], F32, kind="ExternalInput")
    for nm in ("a1_wq", "a1_wk", "a1_wv", "a1_wo"):
        dr[nm] = nc.dram_tensor(nm, [NHEAD * 128, 2048], F32, kind="ExternalInput")
    dr["c1_cosD"] = nc.dram_tensor("c1_cosD", [128, S], F32, kind="ExternalInput")
    dr["c1_sinD"] = nc.dram_tensor("c1_sinD", [128, S], F32, kind="ExternalInput")
    dr["c1_masks"] = nc.dram_tensor("c1_masks", [128, 4 * 512], BF16, kind="ExternalInput")
    dr["c_ones_mean"] = nc.dram_tensor("c_ones_mean", [128, 1], BF16, kind="ExternalInput")
    dr["c_ones_row"] = nc.dram_tensor("c_ones_row", [1, 128], F32, kind="ExternalInput")
    dr["c_one_col"] = nc.dram_tensor("c_one_col", [128, 1], F32, kind="ExternalInput")

    k.psum_init()
    consts = stage_consts(k, dr)
    sg = k.sb.alloc("sg", [128, NJ0], F32)
    su = k.sb.alloc("su", [128, NJ0], F32)
    sd = k.sb.alloc("sd", [128, 16], F32)
    sdh = k.sb.alloc("sdh", [128, 16], F32)
    k.dma("sp", sg[:, :], dr["f0_sg"].ap()[:, :], reads=(), writes=(sg.name,), group="consts")
    k.dma("sp", su[:, :], dr["f0_su"].ap()[:, :], reads=(), writes=(su.name,), group="consts")
    k.dma("sp", sd[:, :], dr["f0_sd"].ap()[:, :], reads=(), writes=(sd.name,), group="consts")
    k.op("dve", lambda e: e.tensor_scalar(sdh[:, :], sd[:, :], 0.5, None, ALU.mult),
         reads=(sd.name,), writes=(sdh.name,), tag="sdh")

    f1_sh = nc.dram_tensor("f1_sh", [CONV_TILES * 128, 2048], I32, kind="ExternalInput")
    cflat = nc.dram_tensor("cflat", [CONV_TILES * 128, 2048], BF16, kind="Internal")
    TILE_EL = 128 * 2048
    order = []
    for j in range(NJ1):
        order.extend([j, NJ1 + j])
    order.extend(range(2 * NJ1, CONV_TILES))
    srcs = []
    for r in order:
        for hh in range(2):
            o = r * TILE_EL + hh * 1024
            srcs.append((bass.AP(f1_sh, o, [[2048, 128], [1, 1024]]),
                         bass.AP(cflat, o, [[2048, 128], [1, 1024]]),
                         f"cflat:{r}", 1024))
    steps = build_lutconv(k, "lc", dr["f1_lut"], srcs, consts)
    k.bg.extend(steps)

    def cf_ap(off, n, f):
        return bass.AP(cflat, off, [[f, n], [1, f]])

    def cf_keys(off, n, f):
        lo = off // TILE_EL
        hi = (off + n * f - 1) // TILE_EL
        return tuple(f"cflat:{r}" for r in range(lo, hi + 1))

    def ffn0(tag, xi, xik, xo, xok):
        build_ffn(k, tag, xi, xik, xo, xok, consts, nj=NJ0,
                  wgT=[dr["f0_wg"].ap()[j * 128:(j + 1) * 128, :] for j in range(NJ0)],
                  wuT=[dr["f0_wu"].ap()[j * 128:(j + 1) * 128, :] for j in range(NJ0)],
                  wdT=[dr["f0_wd"].ap()[t * 128:(t + 1) * 128, :] for t in range(16)],
                  quant=True, sg=sg, su=su, sdh=sdh)

    DI_OFF = 2 * NJ1 * 128 * 2048

    def ffn1(tag, xi, xik, xo, xok):
        build_ffn(k, tag, xi, xik, xo, xok, consts, nj=NJ1,
                  wgT=[cf_ap(j * 128 * 2048, 128, 2048) for j in range(NJ1)],
                  wuT=[cf_ap((NJ1 + j) * 128 * 2048, 128, 2048) for j in range(NJ1)],
                  wdT=[cf_ap(DI_OFF + t * 128 * HID1, 128, HID1) for t in range(16)],
                  quant=False, sg=None, su=None, sdh=consts["half_col"],
                  wrkeys="cf",
                  wrkeyfn=lambda kind, idx: (
                      cf_keys(idx * 128 * 2048, 128, 2048) if kind == 0 else
                      cf_keys((NJ1 + idx) * 128 * 2048, 128, 2048) if kind == 1 else
                      cf_keys(DI_OFF + idx * 128 * HID1, 128, HID1)))

    ffn0("s1", xin.ap(), "xi", xh.ap(), "xh")
    build_attn0(k, "s2", xh.ap(), "xh", xh.ap(), "xh", consts, dr)
    ffn0("s3", xh.ap(), "xh", xh.ap(), "xh")
    ffn1("s4", xh.ap(), "xh", xh.ap(), "xh")
    build_attn1(k, "s5", xh.ap(), "xh", xh.ap(), "xh", consts, dr)
    ffn1("s6", xh.ap(), "xh", xout.ap(), "xo")
    k.P.emit(k.stack)
    return nc, k


def ffn1_flat(inputs):
    giT = tile_A(np.ascontiguousarray(inputs["L1_ffn_gate_idx"].T))
    uiT = tile_A(np.ascontiguousarray(inputs["L1_ffn_up_idx"].T))
    diT = tile_B(np.ascontiguousarray(inputs["L1_ffn_down_idx"].T))
    flat = np.concatenate([giT.ravel(), uiT.ravel(), diT.ravel()])
    return flat.reshape(CONV_TILES * 128, 2048)


def full_host_inputs(inputs):
    out = {}
    out.update(ffn0_host_inputs(inputs))
    out.update(attn0_host_inputs(inputs))
    out.update(attn1_host_inputs(inputs))
    out["f1_lut"] = inputs["L1_ffn_lut"].reshape(1, 16).astype(np.float32)
    return out


_PROG_CACHE = {}


def _get_prog(name, builder):
    if name not in _PROG_CACHE:
        _PROG_CACHE[name] = builder()
    return _PROG_CACHE[name]


def kernel(**inputs):
    import os
    if os.environ.get("KMODE") == "multi":
        return kernel_multi(**inputs)
    nc, k = _get_prog("full", build_full_program)
    host = full_host_inputs(inputs)
    X = inputs["L0_attn_X"]
    flat = ffn1_flat(inputs)
    in_maps = []
    for b in range(NCORES):
        m = dict(host)
        m["xin"] = np.ascontiguousarray(X[b].T)
        m["f1_sh"] = flat
        in_maps.append(m)
    res = run_bass_kernel_spmd(nc, in_maps, core_ids=list(range(NCORES)))
    return np.stack([res.results[b]["xout"].T for b in range(NCORES)]).astype(np.float32)


def kernel_multi(**inputs):
    consts = base_consts()
    h_f0 = ffn0_host_inputs(inputs)
    h_a0 = attn0_host_inputs(inputs)
    h_f1 = ffn1_host_inputs(inputs)
    h_a1 = attn1_host_inputs(inputs)
    X = inputs["L0_attn_X"]
    x_cur = [np.ascontiguousarray(X[b].T) for b in range(NCORES)]

    nc, k = _get_prog("conv", build_lutconv_program)
    conv_in = dict(h_f1)
    conv_in["c_ones_row"] = consts["c_ones_row"]
    res = run_bass_kernel_spmd(nc, [conv_in] * NCORES, core_ids=list(range(NCORES)))
    caches = res.results

    seq = [
        ("ffn0", build_ffn0_program, h_f0, False),
        ("attn0", build_attn0_program, {**h_a0, **consts}, False),
        ("ffn0", build_ffn0_program, h_f0, False),
        ("ffn1", build_ffn1_program, dict(consts), True),
        ("attn1", build_attn1_program, {**h_a1, **consts}, False),
        ("ffn1", build_ffn1_program, dict(consts), True),
    ]
    for name, builder, hostmap, needs_cache in seq:
        nc, k = _get_prog(name, builder)
        in_maps = []
        for b in range(NCORES):
            m = dict(hostmap)
            m["xin"] = x_cur[b]
            if needs_cache:
                m.update({kk: caches[b][kk] for kk in ("c_gi", "c_ui", "c_di")})
            in_maps.append(m)
        res = run_bass_kernel_spmd(nc, in_maps, core_ids=list(range(NCORES)))
        x_cur = [res.results[b]["xout"] for b in range(NCORES)]

    return np.stack([x.T for x in x_cur]).astype(np.float32)


if __name__ == "__main__":
    nc, k = build_ffn0_program()
    print("ops:", len(k.P.ops))


# version 15
# speedup vs baseline: 1.0630x; 1.0630x over previous
import numpy as np
import ml_dtypes
from contextlib import ExitStack

import concourse.bass as bass
import concourse.mybir as mybir
from concourse.bass_utils import run_bass_kernel_spmd

F32 = mybir.dt.float32
BF16 = mybir.dt.bfloat16
I32 = mybir.dt.int32
AF = mybir.ActivationFunctionType
ALU = mybir.AluOpType
AX = mybir.AxisListType

NCORES = 8
D = 2048
S = 2048
NT = 16
HID0 = 5632
NJ0 = 44
HID1 = 5504
NJ1 = 43
NHEAD = 16
DH = 128
EPS = 1e-5

ENGINES = ("pe", "act", "dve", "pool", "sp")
_BLK_ATTR = {"pe": "tensor", "act": "scalar", "dve": "vector", "pool": "gpsimd", "sp": "sync"}


class Op:
    __slots__ = ("idx", "eng", "emit", "reads", "writes", "tag", "is_dma",
                 "group", "gidx", "deps", "need_inc", "inc_idx", "waits", "gwaits")

    def __init__(self, idx, eng, emit, reads, writes, tag, is_dma, group, gidx):
        self.idx = idx
        self.eng = eng
        self.emit = emit
        self.reads = reads
        self.writes = writes
        self.tag = tag
        self.is_dma = is_dma
        self.group = group
        self.gidx = gidx
        self.deps = []
        self.need_inc = is_dma
        self.inc_idx = None
        self.waits = []
        self.gwaits = {}


class Prog:
    def __init__(self, nc):
        self.nc = nc
        self.ops = []
        self.last_writer = {}
        self.readers = {}
        self.group_count = {}
        self.children = {}
        self.group_waiters = {}
        self.pending_alias = {}

    def _dep(self, op, j):
        if j is not None and j != op.idx:
            op.deps.append(j)

    def _touch_alias(self, op, key):
        if ":" in key:
            par = key.split(":", 1)[0]
            for j in self.pending_alias.get(par, ()):
                self._dep(op, j)
            deps = self.pending_alias.pop(key, None)
        else:
            deps = self.pending_alias.pop(key, None)
        if deps:
            for j in deps:
                self._dep(op, j)

    def add(self, eng, emit, reads=(), writes=(), tag="", group=None):
        is_dma = group is not None
        gidx = None
        if is_dma:
            self.group_count[group] = self.group_count.get(group, 0) + 1
            gidx = self.group_count[group]
        op = Op(len(self.ops), eng, emit, tuple(reads), tuple(writes), tag, is_dma, group, gidx)

        def related(key):
            if ":" in key:
                yield key
                yield key.split(":", 1)[0]
            else:
                yield key
                for ch in self.children.get(key, ()):
                    yield ch

        for k in op.reads:
            self._touch_alias(op, k)
            for rk in related(k):
                self._dep(op, self.last_writer.get(rk))
        for k in op.writes:
            self._touch_alias(op, k)
            for rk in related(k):
                self._dep(op, self.last_writer.get(rk))
                for j in self.readers.get(rk, ()):
                    self._dep(op, j)
        if is_dma:
            for j in self.group_waiters.pop(group, ()):
                self._dep(op, j)
        for k in op.reads:
            self.readers.setdefault(k, []).append(op.idx)
            if ":" in k:
                self.children.setdefault(k.split(":", 1)[0], set()).add(k)
        for k in op.writes:
            self.last_writer[k] = op.idx
            self.readers[k] = []
            if ":" in k:
                self.children.setdefault(k.split(":", 1)[0], set()).add(k)
            else:
                for ch in self.children.get(k, ()):
                    self.last_writer.pop(ch, None)
                    self.readers.pop(ch, None)
                self.children.pop(k, None)
        drop = set()
        for j in set(op.deps):
            dop = self.ops[j]
            if dop.is_dma:
                if is_dma and dop.group == group:
                    assert dop.eng == eng, (
                        f"same-group dma dep needs same queue: {dop.tag} -> {tag}")
                    drop.add(j)
                    continue
                self.group_waiters.setdefault(dop.group, []).append(op.idx)
                v = 16 * (self.group_count[dop.group] - (1 if is_dma and dop.group == group else 0))
                if self.gwaits_get(op, dop.group) < v:
                    op.gwaits[dop.group] = v
        if drop:
            op.deps = [j for j in op.deps if j not in drop]
        self.ops.append(op)
        return op.idx

    @staticmethod
    def gwaits_get(op, g):
        return op.gwaits.get(g, 0)

    def alias_ops(self, new_keys, dep_ops):
        if dep_ops:
            for nk in new_keys:
                self.pending_alias.setdefault(nk, []).extend(dep_ops)

    def resolve_accessors(self, keys):
        deps = []
        for k in keys:
            j = self.last_writer.get(k)
            rd = self.readers.get(k, ())
            if j is not None:
                if not (self.ops[j].is_dma and rd):
                    deps.append(j)
            deps.extend(rd)
        return deps

    def check(self):
        per_eng = {e: [] for e in ENGINES}
        for op in self.ops:
            per_eng[op.eng].append(op)
        pos = {e: 0 for e in ENGINES}
        sem = {}
        progress = True
        while progress:
            progress = False
            for e in ENGINES:
                while pos[e] < len(per_eng[e]):
                    op = per_eng[e][pos[e]]
                    ok = all(sem.get(k, 0) >= v for k, v in op.waits)
                    if not ok:
                        break
                    if op.is_dma:
                        sem[("g", op.group)] = sem.get(("g", op.group), 0) + 16
                    elif op.need_inc:
                        sem[("e", e)] = sem.get(("e", e), 0) + 1
                    pos[e] += 1
                    progress = True
        stuck = {e: per_eng[e][pos[e]] for e in ENGINES if pos[e] < len(per_eng[e])}
        if stuck:
            msg = []
            for e, op in stuck.items():
                blocked = [(k, v, sem.get(k, 0)) for k, v in op.waits if sem.get(k, 0) < v]
                msg.append(f"{e}: op#{op.idx} tag={op.tag} blocked_on={blocked}")
            raise RuntimeError("tracker deadlock:\n" + "\n".join(msg))

    def emit(self, stack):
        nc = self.nc
        for op in self.ops:
            for j in set(op.deps):
                self.ops[j].need_inc = True
        counts = {e: 0 for e in ENGINES}
        for op in self.ops:
            if op.is_dma:
                continue
            if op.need_inc:
                counts[op.eng] += 1
                op.inc_idx = counts[op.eng]
        eng_sems = {}
        for e in ENGINES:
            if counts[e] > 0:
                eng_sems[e] = stack.enter_context(nc.semaphore(f"sem_{e}"))
        grp_sems = {}
        for g in self.group_count:
            grp_sems[g] = stack.enter_context(nc.semaphore(f"sem_g_{g}"))
        know = {e: {} for e in ENGINES}
        snap = {}

        def ev(op, waiter=None):
            if op.is_dma:
                v = 16 * op.gidx if waiter is None else waiter.gwaits[op.group]
                return ("g", op.group), v
            return ("e", op.eng), op.inc_idx

        def merge(dst, src):
            for k, v in src.items():
                if dst.get(k, -1) < v:
                    dst[k] = v

        for op in self.ops:
            kn = know[op.eng]
            for j in sorted(set(op.deps)):
                dop = self.ops[j]
                key, val = ev(dop, op)
                if kn.get(key, 0) >= val:
                    continue
                op.waits.append((key, val))
                merge(kn, snap.get(j, {}))
                if kn.get(key, 0) < val:
                    kn[key] = val
            if op.need_inc:
                s = dict(kn)
                key, val = ev(op)
                s[key] = val
                snap[op.idx] = s

        per_eng = {e: [] for e in ENGINES}
        for op in self.ops:
            per_eng[op.eng].append(op)

        def sem_of(key):
            kind, name = key
            return grp_sems[name] if kind == "g" else eng_sems[name]

        self.check()
        blk = stack.enter_context(nc.Block())
        for e in ENGINES:
            ops = per_eng[e]
            if not ops and e != "sp":
                continue

            def body(engine, ops=ops, e=e):
                for op in ops:
                    for key, val in op.waits:
                        engine.wait_ge(sem_of(key), val)
                    inst = op.emit(engine)
                    if op.is_dma:
                        inst.then_inc(grp_sems[op.group], 16)
                    elif op.need_inc:
                        inst.then_inc(eng_sems[op.eng], 1)
                if e == "sp":
                    for g, cnt in self.group_count.items():
                        engine.wait_ge(grp_sems[g], 16 * cnt)
                    for oe in ENGINES:
                        if oe != "sp" and counts[oe] > 0:
                            engine.wait_ge(eng_sems[oe], counts[oe])

            getattr(blk, _BLK_ATTR[e])(body)


class Buf:
    __slots__ = ("name", "shape", "dtype", "off", "size", "_ap")

    def __init__(self, name, shape, dtype, off, size, ap):
        self.name = name
        self.shape = shape
        self.dtype = dtype
        self.off = off
        self.size = size
        self._ap = ap

    def __getitem__(self, sl):
        return self._ap[sl]

    @property
    def ap(self):
        return self._ap


_DT_SIZE = {F32: 4, BF16: 2, I32: 4}


class SBAlloc:

    def __init__(self, nc, prog, base=16512, top=229344):
        self.nc = nc
        self.prog = prog
        self.size = top - base
        self.arena = nc.alloc_sbuf_tensor_at("arena", [128, self.size], mybir.dt.uint8,
                                             offset=base, align_bytes=64)
        self.free = [(0, self.size)]
        self.n = 0
        self.live = {}
        self.region_keys = []

    def alloc(self, name, shape, dtype, keys=None):
        assert len(shape) == 2
        per_part = shape[1] * _DT_SIZE[dtype]
        size = (per_part + 63) & ~63
        for i, (a, b) in enumerate(self.free):
            if b - a >= size:
                off = a
                if a + size == b:
                    self.free.pop(i)
                else:
                    self.free[i] = (a + size, b)
                break
        else:
            livestr = ", ".join(f"{n}@{o}+{s}" for n, (o, s) in sorted(self.live.items(), key=lambda x: x[1][0]))
            raise MemoryError(f"SBUF out of space for {name} ({size}B): free={self.free}\nlive: {livestr}")
        self.n += 1
        ap = self.arena.ap()[0:shape[0], off:off + shape[1] * _DT_SIZE[dtype]].bitcast(dtype)
        buf = Buf(name, shape, dtype, off, size, ap)
        self.live[name] = (off, size)
        hit = []
        nr = []
        for (ra, rb, ks) in self.region_keys:
            if ra < off + size and off < rb:
                hit.extend(ks)
                if ra < off:
                    nr.append((ra, off, ks))
                if rb > off + size:
                    nr.append((off + size, rb, ks))
            else:
                nr.append((ra, rb, ks))
        self.region_keys = nr
        if hit:
            self.prog.alias_ops([buf.name], sorted(set(hit)))
        return buf

    def freeb(self, *bufs):
        for buf in bufs:
            self.live.pop(buf.name, None)
            self.free.append((buf.off, buf.off + buf.size))
            pre = buf.name + ":"
            ks = set([buf.name])
            for kk in list(self.prog.last_writer) + list(self.prog.readers):
                if kk.startswith(pre):
                    ks.add(kk)
            dep_ops = sorted(set(self.prog.resolve_accessors(sorted(ks))))
            self.region_keys.append((buf.off, buf.off + buf.size, dep_ops))
        self.free.sort()
        out = []
        for a, b in self.free:
            if out and out[-1][1] >= a:
                out[-1] = (out[-1][0], max(out[-1][1], b))
            else:
                out.append((a, b))
        self.free = out


def _mk(prog, sb):
    class H:
        pass
    return H


class K:

    def __init__(self, nc):
        self.nc = nc
        self.P = Prog(nc)
        self.stack = ExitStack()
        self.sb = SBAlloc(nc, self.P)
        self.ps_h = None
        self.uid = 0
        self.bg = []

    def pump(self, n=None):
        if n is None:
            n = 2 if len(self.bg) > 240 else 1
        for _ in range(n):
            if self.bg:
                self.bg.pop(0)()

    def u(self, s):
        self.uid += 1
        return f"{s}{self.uid}"

    def psum_init(self):
        self.ps_h = self.stack.enter_context(self.nc.psum_tensor("PS", [128, 4096], F32))

    def ps(self, bank, rows=slice(0, 128), cols=slice(0, 512)):
        c0 = bank * 512 + cols.start
        c1 = bank * 512 + cols.stop
        return self.ps_h.ap()[rows, c0:c1]

    @staticmethod
    def pk(bank, n=1):
        return tuple(f"PSB{b}" for b in range(bank, bank + n))

    def dma(self, queue, dst_ap, src_ap, reads, writes, group, tag="", accum=False):
        if accum:
            fn = lambda e, d=dst_ap, s=src_ap: e.dma_start(d, s, accum_op=ALU.add)
        else:
            fn = lambda e, d=dst_ap, s=src_ap: e.dma_start(d, s)
        self.P.add(queue, fn, reads=reads, writes=writes, tag=tag, group=group)

    def op(self, eng, fn, reads=(), writes=(), tag=""):
        self.P.add(eng, fn, reads=reads, writes=writes, tag=tag)


def build_ln(k: K, tag, x_dram, x_keyf, tok0, width, consts):
    P, sb, nc = k.P, k.sb, k.nc
    nch = width // 512
    assert nch <= 4
    xn = [sb.alloc(k.u(f"{tag}xn"), [128, width], BF16) for _ in range(NT)]
    xsq = [sb.alloc(k.u(f"{tag}xsq"), [128, width], BF16) for _ in range(2)]
    for t in range(NT):
        if t % 4 == 0:
            k.pump()
        k.dma("pool", xn[t][:, :], x_dram[t * 128:(t + 1) * 128, tok0:tok0 + width],
              reads=(f"{x_keyf}:{t}",), writes=(xn[t].name,), group="gxbf", tag=f"{tag}ldx{t}")
        q = xsq[t % 2]
        k.op("act", lambda e, a=q, b=xn[t]: e.activation(a[:, :], b[:, :], AF.Square),
             reads=(xn[t].name,), writes=(q.name,), tag=f"{tag}sq{t}")

        def mm_mean(e, t=t, b=xn[t]):
            for c in range(nch):
                r = e.matmul(k.ps(c, slice(0, 1)), consts["ones_mean"][:, :],
                             b[:, c * 512:(c + 1) * 512], start=(t == 0), stop=(t == NT - 1))
            return r

        k.op("pe", mm_mean, reads=(xn[t].name, "ones_mean"), writes=k.pk(0, nch), tag=f"{tag}mmM{t}")

        def mm_sq(e, t=t, q=q):
            for c in range(nch):
                r = e.matmul(k.ps(4 + c, slice(0, 1)), consts["ones_mean"][:, :],
                             q[:, c * 512:(c + 1) * 512], start=(t == 0), stop=(t == NT - 1))
            return r

        k.op("pe", mm_sq, reads=(q.name, "ones_mean"), writes=k.pk(4, nch), tag=f"{tag}mmS{t}")
    r1 = sb.alloc(k.u(f"{tag}r1"), [1, width], F32)
    r2 = sb.alloc(k.u(f"{tag}r2"), [1, width], F32)
    r3 = sb.alloc(k.u(f"{tag}r3"), [1, width], F32)
    MEAN = r1[0:1, :]
    NMR = r1[0:1, :]
    TMP = r2[0:1, :]
    RSTD = r2[0:1, :]
    MSQ = r3[0:1, :]
    VAR = r3[0:1, :]
    k.op("dve", lambda e: e.tensor_copy(MEAN, k.ps(0, slice(0, 1), slice(0, 512 * nch))),
         reads=k.pk(0, nch), writes=(r1.name,), tag=f"{tag}cpM")
    k.op("dve", lambda e: e.tensor_copy(MSQ, k.ps(4, slice(0, 1), slice(0, 512 * nch))),
         reads=k.pk(4, nch), writes=(r3.name,), tag=f"{tag}cpS")
    k.op("dve", lambda e: e.tensor_mul(TMP, MEAN, MEAN),
         reads=(r1.name,), writes=(r2.name,), tag=f"{tag}m2")
    k.op("dve", lambda e: e.tensor_sub(VAR, MSQ, TMP),
         reads=(r3.name, r2.name), writes=(r3.name,), tag=f"{tag}var")
    k.op("act", lambda e: e.activation(RSTD, VAR, AF.Sqrt, bias=consts["epsc"][0:1, 0:1], scale=1.0),
         reads=(r3.name, "epsc"), writes=(r2.name,), tag=f"{tag}sqrt")
    k.op("dve", lambda e: e.reciprocal(RSTD, RSTD),
         reads=(r2.name,), writes=(r2.name,), tag=f"{tag}rcp")
    k.op("dve", lambda e: e.scalar_tensor_tensor(NMR, MEAN, -1.0, RSTD, ALU.mult, ALU.mult),
         reads=(r1.name, r2.name), writes=(r1.name,), tag=f"{tag}nmr")
    a_b = sb.alloc(k.u(f"{tag}ab"), [128, width], F32)
    c_b = sb.alloc(k.u(f"{tag}cb"), [128, width], F32)

    def mm_bc(e):
        for c in range(nch):
            e.matmul(k.ps(c), consts["ones_row_f"][:, :], RSTD[0:1, c * 512:(c + 1) * 512], start=True, stop=True)
            r = e.matmul(k.ps(4 + c), consts["ones_row_f"][:, :], NMR[0:1, c * 512:(c + 1) * 512], start=True, stop=True)
        return r

    k.op("pe", mm_bc, reads=(r2.name, r1.name, "ones_row_f"),
         writes=k.pk(0, nch) + k.pk(4, nch), tag=f"{tag}bc")
    k.op("dve", lambda e: e.tensor_copy(a_b[:, :], k.ps(0, slice(0, 128), slice(0, 512 * nch))),
         reads=k.pk(0, nch), writes=(a_b.name,), tag=f"{tag}cpA")
    k.op("dve", lambda e: e.tensor_copy(c_b[:, :], k.ps(4, slice(0, 128), slice(0, 512 * nch))),
         reads=k.pk(4, nch), writes=(c_b.name,), tag=f"{tag}cpC")
    tmpf = [sb.alloc(k.u(f"{tag}tmpf"), [128, width], F32) for _ in range(2)]
    xf = [sb.alloc(k.u(f"{tag}xf"), [128, width], F32) for _ in range(2)]
    for t in range(NT):
        tf = tmpf[t % 2]
        xft = xf[t % 2]
        k.dma("sp", xft[:, :], x_dram[t * 128:(t + 1) * 128, tok0:tok0 + width],
              reads=(f"{x_keyf}:{t}",), writes=(xft.name,), group=f"gxf{t % 2}", tag=f"{tag}ldf{t}")
        k.op("dve", lambda e, b=xft, tf=tf: e.tensor_mul(tf[:, :], b[:, :], a_b[:, :]),
             reads=(xft.name, a_b.name), writes=(tf.name,), tag=f"{tag}nA{t}")
        k.op("dve", lambda e, tf=tf, o=xn[t]: e.tensor_add(o[:, :], tf[:, :], c_b[:, :]),
             reads=(tf.name, c_b.name), writes=(xn[t].name,), tag=f"{tag}nB{t}")
    sb.freeb(*xsq, r1, r2, r3, a_b, c_b, *tmpf, *xf)
    return xn


def build_ffn(k: K, tag, x_dram, x_keyf, xo_dram, xo_keyf, consts, *,
              nj, wgT, wuT, wdT, quant, sg=None, su=None, sdh=None, wrkeys=((), (), ()),
              wrkeyfn=None):
    P, sb = k.P, k.sb
    for half in range(2):
        tok0 = half * 1024
        W = 1024
        xn = build_ln(k, f"{tag}ln{half}", x_dram, x_keyf, tok0, W, consts)
        hid = [sb.alloc(k.u(f"{tag}hid"), [128, W], BF16) for _ in range(nj)]
        wbuf = [sb.alloc(k.u(f"{tag}wg"), [128, 2048], BF16) for _ in range(2)]
        ubuf = [sb.alloc(k.u(f"{tag}wu"), [128, 2048], BF16) for _ in range(2)]
        gsil = [sb.alloc(k.u(f"{tag}gs"), [128, W], BF16) for _ in range(2)]
        def lduv(j):
            wb, ub = wbuf[j % 2], ubuf[j % 2]
            rk0 = wrkeyfn(0, j) if wrkeyfn else wrkeys[0]
            rk1 = wrkeyfn(1, j) if wrkeyfn else wrkeys[1]
            k.dma("pool", wb[:, :], wgT[j], reads=rk0, writes=(wb.name,), group=f"gw{j % 2}", tag=f"{tag}ldg{j}")
            k.dma("pool", ub[:, :], wuT[j], reads=rk1, writes=(ub.name,), group=f"gu{j % 2}", tag=f"{tag}ldu{j}")
            if quant:
                k.op("act", lambda e, b=wb: e.activation(b[:, :], b[:, :], AF.Identity,
                                                         bias=consts["neg128"][:, 0:1], scale=1.0),
                     reads=(wb.name, "neg128"), writes=(wb.name,), tag=f"{tag}cvg{j}")
                k.op("act", lambda e, b=ub: e.activation(b[:, :], b[:, :], AF.Identity,
                                                         bias=consts["neg128"][:, 0:1], scale=1.0),
                     reads=(ub.name, "neg128"), writes=(ub.name,), tag=f"{tag}cvu{j}")

        lduv(0)
        for j in range(nj):
            k.pump()
            wb, ub, gs = wbuf[j % 2], ubuf[j % 2], gsil[j % 2]
            pb = 4 * (j % 2)
            if j + 1 < nj:
                lduv(j + 1)

            def mm_gate(e, wb=wb, ub=ub, xn=xn, pb=pb):
                for t in range(NT):
                    for c in range(2):
                        e.matmul(k.ps(pb + c), wb[:, t * 128:(t + 1) * 128], xn[t][:, c * 512:(c + 1) * 512],
                                 start=(t == 0), stop=(t == NT - 1))
                for t in range(NT):
                    for c in range(2):
                        r = e.matmul(k.ps(pb + 2 + c), ub[:, t * 128:(t + 1) * 128], xn[t][:, c * 512:(c + 1) * 512],
                                     start=(t == 0), stop=(t == NT - 1))
                return r

            k.op("pe", mm_gate, reads=(wb.name, ub.name) + tuple(x.name for x in xn),
                 writes=k.pk(pb, 4), tag=f"{tag}mmg{j}")
            k.pump()
            sc_g = sg[:, j:j + 1] if sg is not None else consts["one_col"][:, 0:1]
            sc_g_key = (sg.name,) if sg is not None else ("one_col",)
            k.op("act", lambda e, gs=gs, sc=sc_g, pb=pb: e.activation(gs[:, :], k.ps(pb, slice(0, 128), slice(0, 1024)), AF.Silu, bias=0.0, scale=sc),
                 reads=k.pk(pb, 2) + sc_g_key, writes=(gs.name,), tag=f"{tag}sil{j}")
            sc_u = su[:, j:j + 1] if su is not None else consts["one_col"][:, 0:1]
            sc_u_key = (su.name,) if su is not None else ("one_col",)
            k.op("dve", lambda e, h=hid[j], gs=gs, sc=sc_u, pb=pb: e.scalar_tensor_tensor(
                    h[:, :], k.ps(pb + 2, slice(0, 128), slice(0, 1024)), sc, gs[:, :], ALU.mult, ALU.mult),
                 reads=k.pk(pb + 2, 2) + (gs.name,) + sc_u_key, writes=(hid[j].name,), tag=f"{tag}hid{j}")
        sb.freeb(*xn, *wbuf, *ubuf, *gsil)
        wdb = [sb.alloc(k.u(f"{tag}wd"), [128, nj * 128], BF16) for _ in range(2)]
        xrs = [sb.alloc(k.u(f"{tag}xr"), [128, W], F32) for _ in range(2)]
        xob = [sb.alloc(k.u(f"{tag}xo"), [128, W], F32) for _ in range(2)]
        def ldd(t):
            wd = wdb[t % 2]
            rk2 = wrkeyfn(2, t) if wrkeyfn else wrkeys[2]
            k.dma("pool", wd[:, :], wdT[t], reads=rk2, writes=(wd.name,), group=f"gd{t % 2}", tag=f"{tag}ldd{t}")
            if quant:
                k.op("act", lambda e, b=wd: e.activation(b[:, :], b[:, :], AF.Identity,
                                                         bias=consts["neg128"][:, 0:1], scale=1.0),
                     reads=(wd.name, "neg128"), writes=(wd.name,), tag=f"{tag}cvd{t}")

        ldd(NT - 1)
        for t in reversed(range(NT)):
            k.pump()
            wd, xr, xo = wdb[t % 2], xrs[t % 2], xob[t % 2]
            if t > 0:
                ldd(t - 1)
            k.dma("sp", xr[:, :], x_dram[t * 128:(t + 1) * 128, tok0:tok0 + W],
                  reads=(f"{x_keyf}:{t}",), writes=(xr.name,), group=f"gxr{t % 2}", tag=f"{tag}ldxr{t}")

            pbd = 4 + 2 * (t % 2)

            def mm_dn(e, wd=wd, hid=hid, pbd=pbd):
                for j in range(nj):
                    for c in range(2):
                        r = e.matmul(k.ps(pbd + c), wd[:, j * 128:(j + 1) * 128], hid[j][:, c * 512:(c + 1) * 512],
                                     start=(j == 0), stop=(j == nj - 1))
                return r

            k.op("pe", mm_dn, reads=(wd.name,) + tuple(h.name for h in hid), writes=k.pk(pbd, 2), tag=f"{tag}mmd{t}")
            sc_d = sdh[:, t:t + 1] if sdh.shape[1] > 1 else sdh[:, 0:1]
            k.op("dve", lambda e, xo=xo, xr=xr, sc=sc_d, pbd=pbd: e.scalar_tensor_tensor(
                    xo[:, :], k.ps(pbd, slice(0, 128), slice(0, 1024)), sc, xr[:, :], ALU.mult, ALU.add),
                 reads=k.pk(pbd, 2) + (xr.name, sdh.name), writes=(xo.name,), tag=f"{tag}res{t}")
            k.dma("sp", xo_dram[t * 128:(t + 1) * 128, tok0:tok0 + W], xo[:, :],
                  reads=(xo.name,), writes=(f"{xo_keyf}:{t}",), group=f"gxw{t % 2}", tag=f"{tag}stx{t}")
        sb.freeb(*hid, *wdb, *xrs, *xob)


def stage_consts(k: K, dr, consts_needed=("ones_mean", "ones_row_f", "one_col")):
    sb, P = k.sb, k.P
    out = {}
    cm = sb.alloc("ones_mean", [128, 1], BF16)
    k.dma("sp", cm[:, :], dr["c_ones_mean"].ap()[:, :], reads=(), writes=("ones_mean",), group="consts")
    out["ones_mean"] = cm
    orf = sb.alloc("ones_row_f", [1, 128], F32)
    k.dma("sp", orf[0:1, :], dr["c_ones_row"].ap()[:, :], reads=(), writes=("ones_row_f",), group="consts")
    out["ones_row_f"] = orf
    oc = sb.alloc("one_col", [128, 1], F32)
    k.dma("sp", oc[:, :], dr["c_one_col"].ap()[:, :], reads=(), writes=("one_col",), group="consts")
    out["one_col"] = oc
    ep = sb.alloc("epsc", [1, 1], F32)
    k.op("dve", lambda e: e.memset(ep[0:1, 0:1], EPS), reads=(), writes=("epsc",), tag="eps")
    out["epsc"] = ep
    o1 = sb.alloc("ones1_bf", [128, 1], BF16)
    k.op("dve", lambda e: e.memset(o1[:, :], 1.0), reads=(), writes=("ones1_bf",), tag="o1")
    out["ones1_bf"] = o1
    hc = sb.alloc("half_col", [128, 1], F32)
    k.op("dve", lambda e: e.memset(hc[:, :], 0.5), reads=(), writes=("half_col",), tag="hc")
    out["half_col"] = hc
    n128 = sb.alloc("neg128", [128, 1], F32)
    k.op("dve", lambda e: e.memset(n128[:, :], -128.0), reads=(), writes=("neg128",), tag="n128")
    out["neg128"] = n128
    return out


CONV_TILES = 2 * NJ1 + (16 * HID1) // 2048


def build_lutconv(k: K, tag, lut_dram, srcs, consts):
    P, sb = k.P, k.sb
    lt = sb.alloc(k.u(f"{tag}lt"), [1, 16], F32)
    dl = sb.alloc(k.u(f"{tag}dlb"), [128, 16], F32)

    def prolog():
        k.dma("sp", lt[0:1, :], lut_dram.ap()[:, :], reads=(), writes=(lt.name,), group="consts")

        def mm_b(e):
            return e.matmul(k.ps(7, slice(0, 128), slice(0, 16)), consts["ones_row_f"][:, :],
                            lt[0:1, 0:16], start=True, stop=True)

        k.op("pe", mm_b, reads=(lt.name, "ones_row_f"), writes=k.pk(7), tag=f"{tag}bc")
        k.op("dve", lambda e: e.tensor_copy(dl[:, :], k.ps(7, slice(0, 128), slice(0, 16))),
             reads=k.pk(7), writes=(dl.name,), tag=f"{tag}cp")

    WMAX = 1024
    i32 = [sb.alloc(k.u(f"{tag}i32"), [128, WMAX], I32) for _ in range(4)]
    ib = [sb.alloc(k.u(f"{tag}ib"), [128, WMAX], BF16) for _ in range(2)]
    ac = [sb.alloc(k.u(f"{tag}ac"), [128, WMAX], BF16) for _ in range(3)]
    tm = [sb.alloc(k.u(f"{tag}tm"), [128, WMAX], BF16) for _ in range(4)]

    def region_v(i, v, w):
        b, a = ib[i % 2], ac[i % 3]
        t = tm[2 * (i % 2) + v % 2]
        eng = "dve"
        k.op("dve", lambda e: e.tensor_scalar(t[0:128, 0:w], b[0:128, 0:w], float(v), None, ALU.is_equal),
             reads=(b.name,), writes=(t.name,), tag=f"{tag}t{v}_{i}")
        k.op(eng, lambda e: e.scalar_tensor_tensor(a[0:128, 0:w], t[0:128, 0:w], dl[:, v:v + 1],
                                                   a[0:128, 0:w], ALU.mult, ALU.add),
             reads=(t.name, dl.name, a.name), writes=(a.name,), tag=f"{tag}s{v}_{i}")

    def ld(i, src, w):
        n = i32[i % 4]
        k.dma("act", n[0:128, 0:w], src, reads=(), writes=(n.name,), group=f"gci{i % 2}", tag=f"{tag}ld{i}")

    def region_a(i, src_next, w):
        n, b, a = i32[i % 4], ib[i % 2], ac[i % 3]
        t = tm[2 * (i % 2)]
        if src_next is not None:
            ld(i + 2, src_next[0], src_next[1])
        k.op("act", lambda e: e.copy(b[0:128, 0:w], n[0:128, 0:w]),
             reads=(n.name,), writes=(b.name,), tag=f"{tag}cast{i}")
        k.op("dve", lambda e: e.tensor_scalar(t[0:128, 0:w], b[0:128, 0:w], 0.0, None, ALU.is_equal),
             reads=(b.name,), writes=(t.name,), tag=f"{tag}t0_{i}")
        k.op("dve", lambda e: e.tensor_scalar(a[0:128, 0:w], t[0:128, 0:w], dl[:, 0:1], None, ALU.mult),
             reads=(t.name, dl.name), writes=(a.name,), tag=f"{tag}s0_{i}")
        for v in range(1, 8):
            region_v(i, v, w)

    def region_b(i, w):
        for v in range(8, 16):
            region_v(i, v, w)

    def region_st(i, dst, key, w):
        a = ac[i % 3]
        k.dma("act", dst, a[0:128, 0:w], reads=(a.name,), writes=(key,), group=f"gco{i % 2}", tag=f"{tag}st{i}")

    def prolog2():
        prolog()
        ld(0, srcs[0][0], srcs[0][3])
        ld(1, srcs[1][0], srcs[1][3])

    steps = [prolog2]
    for i, (src, dst, key, w) in enumerate(srcs):
        nxt = (srcs[i + 2][0], srcs[i + 2][3]) if i + 2 < len(srcs) else None
        steps.append(lambda i=i, nxt=nxt, w=w: region_a(i, nxt, w))
        steps.append(lambda i=i, w=w: region_b(i, w))
        if i >= 2:
            pd, pk, pw = srcs[i - 2][1], srcs[i - 2][2], srcs[i - 2][3]
            steps.append(lambda i=i - 2, dst=pd, key=pk, w=pw: region_st(i, dst, key, w))
    for i in range(len(srcs) - 2, len(srcs)):
        steps.append(lambda i=i, dst=srcs[i][1], key=srcs[i][2], w=srcs[i][3]: region_st(i, dst, key, w))

    def done():
        sb.freeb(lt, dl, *i32, *ib, *ac, *tm)

    steps.append(done)
    return steps


def build_attn0(k: K, tag, x_dram, x_keyf, xo_dram, xo_keyf, consts, W):
    P, sb = k.P, k.sb
    xn = build_ln(k, f"{tag}ln", x_dram, x_keyf, 0, S, consts)
    bias = sb.alloc(k.u(f"{tag}bias"), [128, 64], F32)
    k.dma("sp", bias[:, :], W["a0_bias"].ap()[:, :], reads=(), writes=(bias.name,), group="consts")
    BQ, BK, BV, BO = 0, 16, 32, 48

    wb = [sb.alloc(k.u(f"{tag}wb"), [128, 2048], BF16) for _ in range(2)]

    def proj(dst, wT_dram, bias_col, ph):
        for t in range(NT):
            k.pump()
            b = wb[t % 2]
            pb = 4 * (t % 2)
            k.dma("pool", b[:, :], wT_dram[t * 128:(t + 1) * 128, :],
                  reads=(), writes=(b.name,), group=f"gw{t % 2}", tag=f"{tag}{ph}ld{t}")

            def mm(e, b=b, pb=pb):
                for m in range(NT):
                    for c in range(4):
                        r = e.matmul(k.ps(pb + c), b[:, m * 128:(m + 1) * 128], xn[m][:, c * 512:(c + 1) * 512],
                                     start=(m == 0), stop=(m == NT - 1))
                return r

            k.op("pe", mm, reads=(b.name,) + tuple(x.name for x in xn), writes=k.pk(pb, 4), tag=f"{tag}{ph}mm{t}")
            k.op("act", lambda e, d=dst[t], pb=pb, bc=bias[:, bias_col + t:bias_col + t + 1]: e.activation(
                    d[:, :], k.ps(pb, slice(0, 128), slice(0, 2048)), AF.Identity, bias=bc, scale=1.0),
                 reads=k.pk(pb, 4) + (bias.name,), writes=(dst[t].name,), tag=f"{tag}{ph}ev{t}")

    def rope(dst, ph):
        tcs = [sb.alloc(k.u(f"{tag}rt"), [128, S], F32) for _ in range(2)]
        tcs.append(sb.alloc(k.u(f"{tag}rtb"), [128, S], BF16))
        cosb = sb.alloc(k.u(f"{tag}cos"), [128, S], BF16)
        sinb = sb.alloc(k.u(f"{tag}sin"), [128, S], BF16)
        for p in range(4):
            k.dma("sp", cosb[:, :], W["c0_cos"].ap()[p * 128:(p + 1) * 128, :],
                  reads=(), writes=(cosb.name,), group="gcos", tag=f"{tag}{ph}ldc{p}")
            k.dma("sp", sinb[:, :], W["c0_sin"].ap()[p * 128:(p + 1) * 128, :],
                  reads=(), writes=(sinb.name,), group="gsin", tag=f"{tag}{ph}lds{p}")
            a, c, t2 = tcs[0], tcs[1], tcs[2]
            x0, x1 = dst[p], dst[p + 4]
            for (o, i0, i1, tg) in ((a, x0, cosb, "a"), (c, x0, sinb, "c"), (t2, x1, sinb, "b")):
                k.op("dve", lambda e, o=o, i0=i0, i1=i1: e.tensor_mul(o[:, :], i0[:, :], i1[:, :]),
                     reads=(i0.name, i1.name), writes=(o.name,), tag=f"{tag}{ph}rp{tg}{p}")
            k.op("dve", lambda e, x0=x0, a=a, t2=t2: e.tensor_sub(x0[:, :], a[:, :], t2[:, :]),
                 reads=(a.name, t2.name), writes=(x0.name,), tag=f"{tag}{ph}rs{p}")
            k.op("dve", lambda e, t2=t2, x1=x1: e.tensor_mul(t2[:, :], x1[:, :], cosb[:, :]),
                 reads=(x1.name, cosb.name), writes=(t2.name,), tag=f"{tag}{ph}rp d{p}")
            k.op("dve", lambda e, x1=x1, c=c, t2=t2: e.tensor_add(x1[:, :], c[:, :], t2[:, :]),
                 reads=(c.name, t2.name), writes=(x1.name,), tag=f"{tag}{ph}ra{p}")
        sb.freeb(*tcs, cosb, sinb)

    K_t = [sb.alloc(k.u(f"{tag}K"), [128, S], BF16) for _ in range(NT)]
    proj(K_t, W["a0_wkT"].ap(), BK, "k")
    rope(K_t, "k")
    ksq = [sb.alloc(k.u(f"{tag}ksq"), [128, S], BF16) for _ in range(1)]
    for t in range(NT):
        q = ksq[0]
        k.op("act", lambda e, q=q, s=K_t[t]: e.activation(q[:, :], s[:, :], AF.Square),
             reads=(K_t[t].name,), writes=(q.name,), tag=f"{tag}ksq{t}")

        def mmn(e, q=q, t=t):
            for c in range(4):
                r = e.matmul(k.ps(c, slice(0, 1)), consts["ones1_bf"][:, :], q[:, c * 512:(c + 1) * 512],
                             start=(t == 0), stop=(t == NT - 1))
            return r

        k.op("pe", mmn, reads=(q.name, "ones1_bf"), writes=k.pk(0, 4), tag=f"{tag}mmn{t}")
    nrm = sb.alloc(k.u(f"{tag}nrm"), [1, S], F32)
    invn = sb.alloc(k.u(f"{tag}invn"), [1, S], F32)
    k.op("dve", lambda e: e.tensor_copy(nrm[0:1, :], k.ps(0, slice(0, 1), slice(0, 2048))),
         reads=k.pk(0, 4), writes=(nrm.name,), tag=f"{tag}cpn")
    k.op("act", lambda e: e.activation(invn[0:1, :], nrm[0:1, :], AF.Sqrt),
         reads=(nrm.name,), writes=(invn.name,), tag=f"{tag}sqn")
    k.op("dve", lambda e: e.tensor_scalar(invn[0:1, :], invn[0:1, :], 1e-12, None, ALU.max),
         reads=(invn.name,), writes=(invn.name,), tag=f"{tag}mxn")
    k.op("dve", lambda e: e.reciprocal(invn[0:1, :], invn[0:1, :]),
         reads=(invn.name,), writes=(invn.name,), tag=f"{tag}rcn")
    invb = sb.alloc(k.u(f"{tag}invb"), [128, S], F32)

    def mm_bcn(e):
        for c in range(4):
            r = e.matmul(k.ps(4 + c), consts["ones_row_f"][:, :], invn[0:1, c * 512:(c + 1) * 512], start=True, stop=True)
        return r

    k.op("pe", mm_bcn, reads=(invn.name, "ones_row_f"), writes=k.pk(4, 4), tag=f"{tag}bcn")
    k.op("dve", lambda e: e.tensor_copy(invb[:, :], k.ps(4, slice(0, 128), slice(0, 2048))),
         reads=k.pk(4, 4), writes=(invb.name,), tag=f"{tag}cpb")
    sb.freeb(*ksq, nrm)

    a_col = sb.alloc(k.u(f"{tag}acol"), [128, 16], F32)
    vt = [sb.alloc(k.u(f"{tag}vt"), [128, S], BF16) for _ in range(1)]
    kv = [sb.alloc(k.u(f"{tag}kv"), [128, S], F32) for _ in range(1)]
    for t in range(NT):
        k.pump()
        b = wb[t % 2]
        pb = 4 * (t % 2)
        v = vt[0]
        kvb = kv[0]
        k.dma("pool", b[:, :], W["a0_wvT"].ap()[t * 128:(t + 1) * 128, :],
              reads=(), writes=(b.name,), group=f"gw{t % 2}", tag=f"{tag}vld{t}")

        def mmv(e, b=b, pb=pb):
            for m in range(NT):
                for c in range(4):
                    r = e.matmul(k.ps(pb + c), b[:, m * 128:(m + 1) * 128], xn[m][:, c * 512:(c + 1) * 512],
                                 start=(m == 0), stop=(m == NT - 1))
            return r

        k.op("pe", mmv, reads=(b.name,) + tuple(x.name for x in xn), writes=k.pk(pb, 4), tag=f"{tag}vmm{t}")
        k.op("act", lambda e, v=v, pb=pb, bc=bias[:, BV + t:BV + t + 1]: e.activation(
                v[:, :], k.ps(pb, slice(0, 128), slice(0, 2048)), AF.Identity, bias=bc, scale=1.0),
             reads=k.pk(pb, 4) + (bias.name,), writes=(v.name,), tag=f"{tag}vev{t}")
        k.op("dve", lambda e, kvb=kvb, v=v, s=K_t[t]: e.tensor_mul(kvb[:, :], s[:, :], v[:, :]),
             reads=(K_t[t].name, v.name), writes=(kvb.name,), tag=f"{tag}kv{t}")
        k.op("dve", lambda e, kvb=kvb: e.tensor_mul(kvb[:, :], kvb[:, :], invb[:, :]),
             reads=(kvb.name, invb.name), writes=(kvb.name,), tag=f"{tag}kvn{t}")
        k.op("dve", lambda e, kvb=kvb, t=t: e.reduce_sum(a_col[:, t:t + 1], kvb[:, :], AX.X),
             reads=(kvb.name,), writes=(a_col.name + f":{t}",), tag=f"{tag}A{t}")
    sb.freeb(*vt, *kv, invb, invn)
    sb.freeb(*K_t)

    Y = [sb.alloc(k.u(f"{tag}Y"), [128, S], BF16) for _ in range(NT)]
    proj(Y, W["a0_wqT"].ap(), BQ, "q")
    rope(Y, "q")
    for t in range(NT):
        k.op("dve", lambda e, y=Y[t], sc=a_col[:, t:t + 1]: e.tensor_scalar(y[:, :], y[:, :], sc, None, ALU.mult),
             reads=(Y[t].name, a_col.name + f":{t}", a_col.name), writes=(Y[t].name,), tag=f"{tag}sc{t}")

    xrs = [sb.alloc(k.u(f"{tag}xr"), [128, S], F32) for _ in range(2)]
    xob = [sb.alloc(k.u(f"{tag}xo"), [128, S], F32) for _ in range(1)]
    for t in reversed(range(NT)):
        b = wb[t % 2]
        pb = 4 * (t % 2)
        xr, xo = xrs[t % 2], xob[0]
        k.dma("pool", b[:, :], W["a0_woT"].ap()[t * 128:(t + 1) * 128, :],
              reads=(), writes=(b.name,), group=f"gw{t % 2}", tag=f"{tag}old{t}")
        k.dma("sp", xr[:, :], x_dram[t * 128:(t + 1) * 128, :],
              reads=(f"{x_keyf}:{t}",), writes=(xr.name,), group=f"gxr{t % 2}", tag=f"{tag}ldxr{t}")

        def mmo(e, b=b, pb=pb):
            for m in range(NT):
                for c in range(4):
                    r = e.matmul(k.ps(pb + c), b[:, m * 128:(m + 1) * 128], Y[m][:, c * 512:(c + 1) * 512],
                                 start=(m == 0), stop=(m == NT - 1))
            return r

        k.op("pe", mmo, reads=(b.name,) + tuple(y.name for y in Y), writes=k.pk(pb, 4), tag=f"{tag}omm{t}")
        k.op("dve", lambda e, xo=xo, pb=pb, bc=bias[:, BO + t:BO + t + 1], x2=xn[t]: e.scalar_tensor_tensor(
                xo[:, :], k.ps(pb, slice(0, 128), slice(0, 2048)), bc, x2[:, :], ALU.add, ALU.add),
             reads=k.pk(pb, 4) + (bias.name, xn[t].name), writes=(xo.name,), tag=f"{tag}oev{t}")
        k.op("dve", lambda e, xo=xo, xr=xr: e.tensor_add(xo[:, :], xo[:, :], xr[:, :]),
             reads=(xo.name, xr.name), writes=(xo.name,), tag=f"{tag}ores{t}")
        k.dma("sp", xo_dram[t * 128:(t + 1) * 128, :], xo[:, :],
              reads=(xo.name,), writes=(f"{xo_keyf}:{t}",), group=f"gxw{t % 2}", tag=f"{tag}stx{t}")
    sb.freeb(*xn, *Y, *xrs, *xob, *wb, bias, a_col)


def build_attn0_program(which_io=("ExternalInput", "ExternalOutput")):
    nc = bass.Bass("TRN2", target_bir_lowering=False)
    k = K(nc)
    xin = nc.dram_tensor("xin", [D, S], F32, kind=which_io[0])
    xout = nc.dram_tensor("xout", [D, S], F32, kind=which_io[1])
    W = {}
    for nm in ("a0_wqT", "a0_wkT", "a0_wvT", "a0_woT"):
        W[nm] = nc.dram_tensor(nm, [D, D], F32, kind="ExternalInput")
    W["a0_bias"] = nc.dram_tensor("a0_bias", [128, 64], F32, kind="ExternalInput")
    W["c0_cos"] = nc.dram_tensor("c0_cos", [512, S], BF16, kind="ExternalInput")
    W["c0_sin"] = nc.dram_tensor("c0_sin", [512, S], BF16, kind="ExternalInput")
    c1 = nc.dram_tensor("c_ones_mean", [128, 1], BF16, kind="ExternalInput")
    c2 = nc.dram_tensor("c_ones_row", [1, 128], F32, kind="ExternalInput")
    c3 = nc.dram_tensor("c_one_col", [128, 1], F32, kind="ExternalInput")
    dr = {"c_ones_mean": c1, "c_ones_row": c2, "c_one_col": c3}
    k.psum_init()
    consts = stage_consts(k, dr)
    build_attn0(k, "a0", xin.ap(), "xi", xout.ap(), "xo", consts, W)
    k.P.emit(k.stack)
    return nc, k


def attn0_host_inputs(inputs):
    pi = rope_perm()
    W_in = inputs["L0_attn_W_in"]
    b_in = inputs["L0_attn_b_in"]
    W_out = inputs["L0_attn_W_out"]
    b_out = inputs["L0_attn_b_out"]
    out = {}
    for i, nm in enumerate(("a0_wqT", "a0_wkT", "a0_wvT")):
        blk = W_in[i * D:(i + 1) * D][pi]
        out[nm] = tile_lhsT(blk)
    out["a0_woT"] = tile_lhsT(np.ascontiguousarray(W_out[:, pi]))
    bias = np.zeros((128, 64), np.float32)
    for i in range(3):
        bias[:, 16 * i:16 * i + 16] = col_tiles(b_in[i * D:(i + 1) * D][pi])
    bias[:, 48:64] = col_tiles(b_out)
    out["a0_bias"] = bias
    j = np.arange(512, dtype=np.float64)
    inv_freq = 10000.0 ** (-2.0 * j / 1024.0)
    s = np.arange(S, dtype=np.float64)
    ang = s[None, :] * inv_freq[:, None]
    out["c0_cos"] = np.cos(ang).astype(np.float32).astype(ml_dtypes.bfloat16)
    out["c0_sin"] = np.sin(ang).astype(np.float32).astype(ml_dtypes.bfloat16)
    out["c_ones_mean"] = np.full((128, 1), 1.0 / D, np.float32).astype(ml_dtypes.bfloat16)
    out["c_ones_row"] = np.ones((1, 128), np.float32)
    out["c_one_col"] = np.ones((128, 1), np.float32)
    return out


def tile_lhsT(w):
    O, I = w.shape
    nt, nm = O // 128, I // 128
    return np.ascontiguousarray(w.T.reshape(nm, 128, nt, 128).transpose(2, 1, 0, 3).reshape(nt * 128, O))


def tile_A(wT):
    I, H_ = wT.shape
    nj = H_ // 128
    nt = I // 128
    return np.ascontiguousarray(wT.reshape(nt, 128, nj, 128).transpose(2, 1, 0, 3).reshape(nj * 128, nt * 128))


def tile_B(wdT):
    H_, O = wdT.shape
    nj = H_ // 128
    nt = O // 128
    return np.ascontiguousarray(wdT.reshape(nj, 128, nt, 128).transpose(2, 1, 0, 3).reshape(nt * 128, H_))


def col_tiles(v):
    n = v.shape[0] // 128
    return np.ascontiguousarray(v.reshape(n, 128).T)


def rope_perm():
    p = np.arange(D)
    p[0:512] = np.arange(512) * 2
    p[512:1024] = np.arange(512) * 2 + 1
    return p


def build_ffn0_program(which_io=("ExternalInput", "ExternalOutput")):
    nc = bass.Bass("TRN2", target_bir_lowering=False)
    k = K(nc)
    xin = nc.dram_tensor("xin", [D, S], F32, kind=which_io[0])
    xout = nc.dram_tensor("xout", [D, S], F32, kind=which_io[1])
    wg = nc.dram_tensor("f0_wg", [NJ0 * 128, 2048], I32, kind="ExternalInput")
    wu = nc.dram_tensor("f0_wu", [NJ0 * 128, 2048], I32, kind="ExternalInput")
    wd = nc.dram_tensor("f0_wd", [16 * 128, HID0], I32, kind="ExternalInput")
    sg_d = nc.dram_tensor("f0_sg", [128, NJ0], F32, kind="ExternalInput")
    su_d = nc.dram_tensor("f0_su", [128, NJ0], F32, kind="ExternalInput")
    sd_d = nc.dram_tensor("f0_sd", [128, 16], F32, kind="ExternalInput")
    c1 = nc.dram_tensor("c_ones_mean", [128, 1], BF16, kind="ExternalInput")
    c2 = nc.dram_tensor("c_ones_row", [1, 128], F32, kind="ExternalInput")
    c3 = nc.dram_tensor("c_one_col", [128, 1], F32, kind="ExternalInput")
    dr = {"c_ones_mean": c1, "c_ones_row": c2, "c_one_col": c3}

    k.psum_init()
    consts = stage_consts(k, dr)
    sg = k.sb.alloc("sg", [128, NJ0], F32)
    su = k.sb.alloc("su", [128, NJ0], F32)
    sd = k.sb.alloc("sd", [128, 16], F32)
    sdh = k.sb.alloc("sdh", [128, 16], F32)
    k.dma("sp", sg[:, :], sg_d.ap()[:, :], reads=(), writes=(sg.name,), group="consts")
    k.dma("sp", su[:, :], su_d.ap()[:, :], reads=(), writes=(su.name,), group="consts")
    k.dma("sp", sd[:, :], sd_d.ap()[:, :], reads=(), writes=(sd.name,), group="consts")
    k.op("dve", lambda e: e.tensor_scalar(sdh[:, :], sd[:, :], 0.5, None, ALU.mult),
         reads=(sd.name,), writes=(sdh.name,), tag="sdh")

    build_ffn(k, "f0a", xin.ap(), "xi", xout.ap(), "xo", consts,
              nj=NJ0,
              wgT=[wg.ap()[j * 128:(j + 1) * 128, :] for j in range(NJ0)],
              wuT=[wu.ap()[j * 128:(j + 1) * 128, :] for j in range(NJ0)],
              wdT=[wd.ap()[t * 128:(t + 1) * 128, :] for t in range(16)],
              quant=True, sg=sg, su=su, sdh=sdh)
    k.P.emit(k.stack)
    return nc, k


def ffn0_host_inputs(inputs):
    wgT = tile_A(np.ascontiguousarray(inputs["L0_ffn_wq_gate"].T))
    wuT = tile_A(np.ascontiguousarray(inputs["L0_ffn_wq_up"].T))
    wdT = tile_B(np.ascontiguousarray(inputs["L0_ffn_wq_down"].T))
    return {
        "f0_wg": wgT, "f0_wu": wuT, "f0_wd": wdT,
        "f0_sg": col_tiles(inputs["L0_ffn_s_gate"]).astype(np.float32),
        "f0_su": col_tiles(inputs["L0_ffn_s_up"]).astype(np.float32),
        "f0_sd": col_tiles(inputs["L0_ffn_s_down"]).astype(np.float32),
        "c_ones_mean": np.full((128, 1), 1.0 / D, np.float32).astype(ml_dtypes.bfloat16),
        "c_ones_row": np.ones((1, 128), np.float32),
        "c_one_col": np.ones((128, 1), np.float32),
    }


def build_lutconv_program():
    nc = bass.Bass("TRN2", target_bir_lowering=False)
    k = K(nc)
    dr = {}
    dr["f1_gi"] = nc.dram_tensor("f1_gi", [NJ1 * 128, 2048], I32, kind="ExternalInput")
    dr["f1_ui"] = nc.dram_tensor("f1_ui", [NJ1 * 128, 2048], I32, kind="ExternalInput")
    dr["f1_di"] = nc.dram_tensor("f1_di", [16 * 128, HID1], I32, kind="ExternalInput")
    dr["f1_lut"] = nc.dram_tensor("f1_lut", [1, 16], F32, kind="ExternalInput")
    dr["c_gi"] = nc.dram_tensor("c_gi", [NJ1 * 128, 2048], BF16, kind="ExternalOutput")
    dr["c_ui"] = nc.dram_tensor("c_ui", [NJ1 * 128, 2048], BF16, kind="ExternalOutput")
    dr["c_di"] = nc.dram_tensor("c_di", [16 * 128, HID1], BF16, kind="ExternalOutput")
    c2 = nc.dram_tensor("c_ones_row", [1, 128], F32, kind="ExternalInput")
    k.psum_init()
    orf = k.sb.alloc("ones_row_f", [1, 128], F32)
    k.dma("sp", orf[0:1, :], c2.ap()[:, :], reads=(), writes=("ones_row_f",), group="consts")
    consts = {"ones_row_f": orf}
    srcs = lutconv_srcs(dr)
    steps = build_lutconv(k, "lc", dr["f1_lut"], srcs, consts)
    for st in steps:
        st()
    k.P.emit(k.stack)
    return nc, k


def build_ffn1_program(which_io=("ExternalInput", "ExternalOutput")):
    nc = bass.Bass("TRN2", target_bir_lowering=False)
    k = K(nc)
    xin = nc.dram_tensor("xin", [D, S], F32, kind=which_io[0])
    xout = nc.dram_tensor("xout", [D, S], F32, kind=which_io[1])
    cg = nc.dram_tensor("c_gi", [NJ1 * 128, 2048], BF16, kind="ExternalInput")
    cu = nc.dram_tensor("c_ui", [NJ1 * 128, 2048], BF16, kind="ExternalInput")
    cd = nc.dram_tensor("c_di", [16 * 128, HID1], BF16, kind="ExternalInput")
    c1 = nc.dram_tensor("c_ones_mean", [128, 1], BF16, kind="ExternalInput")
    c2 = nc.dram_tensor("c_ones_row", [1, 128], F32, kind="ExternalInput")
    c3 = nc.dram_tensor("c_one_col", [128, 1], F32, kind="ExternalInput")
    dr = {"c_ones_mean": c1, "c_ones_row": c2, "c_one_col": c3}
    k.psum_init()
    consts = stage_consts(k, dr)
    build_ffn(k, "f1a", xin.ap(), "xi", xout.ap(), "xo", consts,
              nj=NJ1,
              wgT=[cg.ap()[j * 128:(j + 1) * 128, :] for j in range(NJ1)],
              wuT=[cu.ap()[j * 128:(j + 1) * 128, :] for j in range(NJ1)],
              wdT=[cd.ap()[t * 128:(t + 1) * 128, :] for t in range(16)],
              quant=False, sg=None, su=None, sdh=consts["half_col"])
    k.P.emit(k.stack)
    return nc, k


def build_attn1_program(which_io=("ExternalInput", "ExternalOutput")):
    nc = bass.Bass("TRN2", target_bir_lowering=False)
    k = K(nc)
    xin = nc.dram_tensor("xin", [D, S], F32, kind=which_io[0])
    xout = nc.dram_tensor("xout", [D, S], F32, kind=which_io[1])
    W = {}
    for nm in ("a1_wq", "a1_wk", "a1_wv", "a1_wo"):
        W[nm] = nc.dram_tensor(nm, [NHEAD * 128, 2048], F32, kind="ExternalInput")
    W["c1_cosD"] = nc.dram_tensor("c1_cosD", [128, S], F32, kind="ExternalInput")
    W["c1_sinD"] = nc.dram_tensor("c1_sinD", [128, S], F32, kind="ExternalInput")
    W["c1_masks"] = nc.dram_tensor("c1_masks", [128, 4 * 512], BF16, kind="ExternalInput")
    c1 = nc.dram_tensor("c_ones_mean", [128, 1], BF16, kind="ExternalInput")
    c2 = nc.dram_tensor("c_ones_row", [1, 128], F32, kind="ExternalInput")
    c3 = nc.dram_tensor("c_one_col", [128, 1], F32, kind="ExternalInput")
    dr = {"c_ones_mean": c1, "c_ones_row": c2, "c_one_col": c3}
    k.psum_init()
    consts = stage_consts(k, dr)
    dbg = None
    import os
    if os.environ.get("A1DBG"):
        dbg = {nm: nc.dram_tensor(nm, [128, S], BF16, kind="ExternalOutput")
               for nm in ("d_q0", "d_k0", "d_v0", "d_o0", "d_xn0")}
    build_attn1(k, "a1", xin.ap(), "xi", xout.ap(), "xo", consts, W, dbg=dbg)
    k.P.emit(k.stack)
    return nc, k


def base_consts():
    return {
        "c_ones_mean": np.full((128, 1), 1.0 / D, np.float32).astype(ml_dtypes.bfloat16),
        "c_ones_row": np.ones((1, 128), np.float32),
        "c_one_col": np.ones((128, 1), np.float32),
    }


def ffn1_host_inputs(inputs):
    giT = tile_A(np.ascontiguousarray(inputs["L1_ffn_gate_idx"].T))
    uiT = tile_A(np.ascontiguousarray(inputs["L1_ffn_up_idx"].T))
    diT = tile_B(np.ascontiguousarray(inputs["L1_ffn_down_idx"].T))
    return {"f1_gi": giT, "f1_ui": uiT, "f1_di": diT,
            "f1_lut": inputs["L1_ffn_lut"].reshape(1, 16).astype(np.float32)}


def attn1_host_inputs(inputs):
    out = {}
    for nm, key in (("a1_wq", "L1_attn_w_aq"), ("a1_wk", "L1_attn_w_ak"), ("a1_wv", "L1_attn_w_av")):
        w = inputs[key]
        out[nm] = np.ascontiguousarray(
            w.reshape(NHEAD, NT, 128, 128).transpose(0, 2, 1, 3).reshape(NHEAD * 128, 2048))
    wo = inputs["L1_attn_w_ao"].reshape(NHEAD, 128, NT, 128)
    out["a1_wo"] = np.ascontiguousarray(wo.transpose(2, 1, 0, 3).reshape(NT * 128, 2048))
    dm = np.arange(64, dtype=np.float64)
    ang_f = 10000.0 ** (-dm / 64.0)
    s = np.arange(S, dtype=np.float64)
    ang = s[None, :] * ang_f[:, None]
    cos = np.cos(ang)
    sin = np.sin(ang)
    out["c1_cosD"] = np.concatenate([cos, cos], 0).astype(np.float32)
    out["c1_sinD"] = np.concatenate([-sin, sin], 0).astype(np.float32)
    m = np.zeros((128, 4, 512), np.float32)
    for dlt in range(4):
        for qq in range(512):
            qt = qq // 128
            if qt > dlt:
                m[:, dlt, qq] = 1.0
            elif qt == dlt:
                m[:, dlt, qq] = (np.arange(128) <= (qq % 128)).astype(np.float32)
    out["c1_masks"] = m.reshape(128, 4 * 512).astype(ml_dtypes.bfloat16)
    return out


def lutconv_srcs(nc_handles):
    out = []
    for nm, cn in (("f1_gi", "c_gi"), ("f1_ui", "c_ui")):
        s, d = nc_handles[nm].ap(), nc_handles[cn].ap()
        for j in range(NJ1):
            for hh in range(2):
                out.append((s[j * 128:(j + 1) * 128, hh * 1024:(hh + 1) * 1024],
                            d[j * 128:(j + 1) * 128, hh * 1024:(hh + 1) * 1024], f"{cn}:{j}", 1024))
    s, d = nc_handles["f1_di"].ap(), nc_handles["c_di"].ap()
    for t in range(16):
        for hh in range(8):
            w = HID1 // 8
            out.append((s[t * 128:(t + 1) * 128, hh * w:(hh + 1) * w],
                        d[t * 128:(t + 1) * 128, hh * w:(hh + 1) * w], f"c_di:{t}h{hh}", w))
    return out


def build_attn1(k: K, tag, x_dram, x_keyf, xo_dram, xo_keyf, consts, W, dbg=None):
    P, sb = k.P, k.sb
    xn = build_ln(k, f"{tag}ln", x_dram, x_keyf, 0, S, consts)
    cosD = sb.alloc(k.u(f"{tag}cos"), [128, S], F32)
    sinD = sb.alloc(k.u(f"{tag}sin"), [128, S], F32)
    k.dma("sp", cosD[:, :], W["c1_cosD"].ap()[:, :], reads=(), writes=(cosD.name,), group="consts")
    k.dma("sp", sinD[:, :], W["c1_sinD"].ap()[:, :], reads=(), writes=(sinD.name,), group="consts")
    masks = sb.alloc(k.u(f"{tag}msk"), [128, 4 * 512], BF16)
    k.dma("sp", masks[:, :], W["c1_masks"].ap()[:, :], reads=(), writes=(masks.name,), group="consts")
    o_all = [sb.alloc(k.u(f"{tag}o"), [128, S], BF16) for _ in range(NHEAD)]
    GH = 2
    qT = [sb.alloc(k.u(f"{tag}q"), [128, S], BF16) for _ in range(GH)]
    kT = [sb.alloc(k.u(f"{tag}k"), [128, S], BF16) for _ in range(GH)]
    vB = [sb.alloc(k.u(f"{tag}v"), [128, S], BF16) for _ in range(GH)]
    wh = [sb.alloc(k.u(f"{tag}wh"), [128, S], BF16) for _ in range(2)]
    rt = [sb.alloc(k.u(f"{tag}rt"), [128, S], F32) for _ in range(1)]
    rtb = sb.alloc(k.u(f"{tag}rtb"), [128, S], BF16)
    swp = sb.alloc(k.u(f"{tag}swp"), [128, S], BF16)
    ubs = [sb.alloc(k.u(f"{tag}u"), [128, 1024], BF16) for _ in range(2)]
    dnr = sb.alloc(k.u(f"{tag}dnr"), [1, 512], F32)
    of = sb.alloc(k.u(f"{tag}of"), [128, 512], F32)

    def proj_head(h, wdram, gname, dst):
        b = wh[h % 2]
        k.dma("pool", b[:, :], wdram[h * 128:(h + 1) * 128, :],
              reads=(), writes=(b.name,), group=f"gw{h % 2}", tag=f"{tag}{gname}ld{h}")
        pb = 4 * (h % 2)

        def mm(e, b=b, pb=pb):
            for m in range(NT):
                for c in range(4):
                    r = e.matmul(k.ps(pb + c), b[:, m * 128:(m + 1) * 128], xn[m][:, c * 512:(c + 1) * 512],
                                 start=(m == 0), stop=(m == NT - 1))
            return r

        k.op("pe", mm, reads=(b.name,) + tuple(x.name for x in xn), writes=k.pk(pb, 4), tag=f"{tag}{gname}mm{h}")
        k.op("dve", lambda e, d=dst, pb=pb: e.tensor_copy(d[:, :], k.ps(pb, slice(0, 128), slice(0, 2048))),
             reads=k.pk(pb, 4), writes=(dst.name,), tag=f"{tag}{gname}ev{h}")

    def rope1(dst, h, ph):
        k.dma("act", swp[0:64, :], dst[64:128, :], reads=(dst.name,), writes=(swp.name + ":a",),
              group="gsw", tag=f"{tag}{ph}swA{h}")
        k.dma("act", swp[64:128, :], dst[0:64, :], reads=(dst.name,), writes=(swp.name + ":b",),
              group="gsw", tag=f"{tag}{ph}swB{h}")
        t0, t1 = rt[0], rtb
        k.op("dve", lambda e: e.tensor_mul(t0[:, :], dst[:, :], cosD[:, :]),
             reads=(dst.name, cosD.name), writes=(t0.name,), tag=f"{tag}{ph}r1{h}")
        k.op("dve", lambda e: e.tensor_mul(t1[:, :], swp[:, :], sinD[:, :]),
             reads=(swp.name + ":a", swp.name + ":b", sinD.name), writes=(t1.name,), tag=f"{tag}{ph}r2{h}")
        k.op("dve", lambda e: e.tensor_add(dst[:, :], t0[:, :], t1[:, :]),
             reads=(t0.name, t1.name), writes=(dst.name,), tag=f"{tag}{ph}r3{h}")

    wvb = wh
    for g in range(NHEAD // GH):
        heads = tuple(GH * g + i for i in range(GH))
        for h in heads:
            proj_head(h, W["a1_wq"].ap(), "qw", qT[h % GH])
            rope1(qT[h % GH], h, "q")
            proj_head(h, W["a1_wk"].ap(), "kw", kT[h % GH])
            rope1(kT[h % GH], h, "k")
        for h in heads:
            b = wvb[h % GH]
            k.dma("pool", b[:, :], W["a1_wv"].ap()[h * 128:(h + 1) * 128, :],
                  reads=(), writes=(b.name,), group=f"gw{h % 2}", tag=f"{tag}vwld{h}")
        for j in range(NT):
            def mmv(e, j=j, heads=heads):
                for m in range(NT):
                    for hi, h in enumerate(heads):
                        r = e.matmul(k.ps(4 + hi, slice(0, 128), slice(0, 128)),
                                     xn[m][:, j * 128:(j + 1) * 128], wvb[h % GH][:, m * 128:(m + 1) * 128],
                                     start=(m == 0), stop=(m == NT - 1))
                return r

            k.op("pe", mmv, reads=tuple(w.name for w in wvb) + tuple(x.name for x in xn),
                 writes=k.pk(4, GH), tag=f"{tag}vmm{g}_{j}")
            for hi, h in enumerate(heads):
                k.op("dve", lambda e, j=j, hi=hi, h=h: e.tensor_copy(
                        vB[h % GH][:, j * 128:(j + 1) * 128], k.ps(4 + hi, slice(0, 128), slice(0, 128))),
                     reads=k.pk(4 + hi), writes=(vB[h % GH].name + f":{j}",), tag=f"{tag}vev{g}_{j}_{hi}")
        for h in heads:
            q, kk, v = qT[h % GH], kT[h % GH], vB[h % GH]
            vkeys = tuple(v.name + f":{j}" for j in range(NT))
            for p in range(2):
                njp = 8 * p + 8

                def emit_s(j, p=p, kk=kk, q=q):
                    act = [c for c in (2 * p, 2 * p + 1) if j <= 4 * c + 3]
                    c0 = act[0] - 2 * p
                    sb0 = 0 if j % 2 == 0 else 6

                    def mms(e, j=j, act=act, p=p, kk=kk, q=q, sb0=sb0):
                        for c in act:
                            r = e.matmul(k.ps(sb0 + c - 2 * p), kk[:, j * 128:(j + 1) * 128],
                                         q[:, c * 512:(c + 1) * 512], start=True, stop=True)
                        return r

                    k.op("pe", mms, reads=(kk.name, q.name), writes=k.pk(sb0 + c0, len(act)), tag=f"{tag}s{h}_{p}_{j}")

                emit_s(0)
                for j in range(njp):
                    act = [c for c in (2 * p, 2 * p + 1) if j <= 4 * c + 3]
                    c0 = act[0] - 2 * p
                    nact = len(act)
                    sb0 = 0 if j % 2 == 0 else 6
                    ub = ubs[j % 2]
                    if j + 1 < njp:
                        emit_s(j + 1)
                    k.op("act", lambda e, c0=c0, nact=nact, sb0=sb0, ub=ub: e.activation(
                            ub[:, c0 * 512:(c0 + nact) * 512], k.ps(sb0 + c0, slice(0, 128), slice(0, nact * 512)),
                            AF.Exp, bias=0.0, scale=1.0 / 128.0),
                         reads=k.pk(sb0 + c0, nact), writes=(ub.name,), tag=f"{tag}e{h}_{p}_{j}")
                    for c in act:
                        dlt = j - 4 * c
                        if 0 <= dlt <= 3:
                            k.op("dve", lambda e, c=c, p=p, dlt=dlt, ub=ub: e.tensor_mul(
                                    ub[:, (c - 2 * p) * 512:(c - 2 * p + 1) * 512],
                                    ub[:, (c - 2 * p) * 512:(c - 2 * p + 1) * 512],
                                    masks[:, dlt * 512:(dlt + 1) * 512]),
                                 reads=(ub.name, masks.name), writes=(ub.name,), tag=f"{tag}m{h}_{p}_{j}_{c}")

                    def mmo(e, j=j, act=act, p=p, v=v, ub=ub):
                        for c in act:
                            e.matmul(k.ps(2 + c - 2 * p), v[:, j * 128:(j + 1) * 128],
                                     ub[:, (c - 2 * p) * 512:(c - 2 * p + 1) * 512],
                                     start=(j == 0), stop=(j == 4 * c + 3))
                            r = e.matmul(k.ps(4 + c - 2 * p, slice(0, 1), slice(0, 512)),
                                         consts["ones1_bf"][:, :], ub[:, (c - 2 * p) * 512:(c - 2 * p + 1) * 512],
                                         start=(j == 0), stop=(j == 4 * c + 3))
                        return r

                    k.op("pe", mmo, reads=(ub.name, "ones1_bf") + vkeys,
                         writes=k.pk(2, 2) + k.pk(4, 2), tag=f"{tag}o{h}_{p}_{j}")
                for c in (2 * p, 2 * p + 1):
                    ci = c - 2 * p
                    k.op("dve", lambda e, ci=ci: e.tensor_copy(dnr[0:1, :], k.ps(4 + ci, slice(0, 1), slice(0, 512))),
                         reads=k.pk(4 + ci), writes=(dnr.name,), tag=f"{tag}dn{h}_{c}")
                    k.op("dve", lambda e: e.reciprocal(dnr[0:1, :], dnr[0:1, :]),
                         reads=(dnr.name,), writes=(dnr.name,), tag=f"{tag}rc{h}_{c}")

                    def mmb(e):
                        return e.matmul(k.ps(6, slice(0, 128), slice(0, 512)), consts["ones_row_f"][:, :],
                                        dnr[0:1, 0:512], start=True, stop=True)

                    k.op("pe", mmb, reads=(dnr.name, "ones_row_f"), writes=k.pk(6), tag=f"{tag}b{h}_{c}")
                    k.op("act", lambda e, ci=ci: e.copy(of[:, :], k.ps(2 + ci, slice(0, 128), slice(0, 512))),
                         reads=k.pk(2 + ci), writes=(of.name,), tag=f"{tag}cf{h}_{c}")
                    k.op("dve", lambda e, c=c, h=h: e.tensor_mul(o_all[h][:, c * 512:(c + 1) * 512], of[:, :],
                                                                 k.ps(6, slice(0, 128), slice(0, 512))),
                         reads=(of.name,) + k.pk(6), writes=(o_all[h].name,), tag=f"{tag}on{h}_{c}")
    if dbg is not None:
        for nm, buf in (("d_q0", qT[0]), ("d_k0", kT[0]), ("d_v0", vB[0]), ("d_o0", o_all[0]), ("d_xn0", xn[0])):
            k.dma("sp", dbg[nm].ap()[:, :], buf[:, :], reads=(buf.name,), writes=(nm,), group="dbg", tag=f"dbg{nm}")
    sb.freeb(*qT, *kT, *vB, *rt, rtb, swp, *ubs, dnr, of, cosD, sinD, masks)
    sb.freeb(*xn)
    xrs = [sb.alloc(k.u(f"{tag}xr"), [128, S], F32) for _ in range(2)]
    xob = [sb.alloc(k.u(f"{tag}xo"), [128, S], F32) for _ in range(2)]
    for t in reversed(range(NT)):
        b = wh[t % 2]
        pb = 4 * (t % 2)
        xr, xo = xrs[t % 2], xob[t % 2]
        k.dma("pool", b[:, :], W["a1_wo"].ap()[t * 128:(t + 1) * 128, :],
              reads=(), writes=(b.name,), group=f"gw{t % 2}", tag=f"{tag}old{t}")
        k.dma("sp", xr[:, :], x_dram[t * 128:(t + 1) * 128, :],
              reads=(f"{x_keyf}:{t}",), writes=(xr.name,), group=f"gxr{t % 2}", tag=f"{tag}ldxr{t}")

        def mmo2(e, b=b, pb=pb):
            for h in range(NHEAD):
                for c in range(4):
                    r = e.matmul(k.ps(pb + c), b[:, h * 128:(h + 1) * 128], o_all[h][:, c * 512:(c + 1) * 512],
                                 start=(h == 0), stop=(h == NHEAD - 1))
            return r

        k.op("pe", mmo2, reads=(b.name,) + tuple(o.name for o in o_all), writes=k.pk(pb, 4), tag=f"{tag}omm{t}")
        k.op("dve", lambda e, xo=xo, xr=xr, pb=pb: e.tensor_add(xo[:, :], k.ps(pb, slice(0, 128), slice(0, 2048)), xr[:, :]),
             reads=k.pk(pb, 4) + (xr.name,), writes=(xo.name,), tag=f"{tag}oev{t}")
        k.dma("sp", xo_dram[t * 128:(t + 1) * 128, :], xo[:, :],
              reads=(xo.name,), writes=(f"{xo_keyf}:{t}",), group=f"gxw{t % 2}", tag=f"{tag}stx{t}")
    sb.freeb(*o_all, *wh, *xrs, *xob)


def build_attn0(k: K, tag, x_dram, x_keyf, xo_dram, xo_keyf, consts, W):
    P, sb = k.P, k.sb
    xn = build_ln(k, f"{tag}ln", x_dram, x_keyf, 0, S, consts)
    bias = sb.alloc(k.u(f"{tag}bias"), [128, 64], F32)
    k.dma("sp", bias[:, :], W["a0_bias"].ap()[:, :], reads=(), writes=(bias.name,), group="consts")
    BQ, BK, BV, BO = 0, 16, 32, 48

    wb = [sb.alloc(k.u(f"{tag}wb"), [128, 2048], BF16) for _ in range(2)]

    def proj(dst, wT_dram, bias_col, ph):
        for t in range(NT):
            k.pump()
            b = wb[t % 2]
            pb = 4 * (t % 2)
            k.dma("pool", b[:, :], wT_dram[t * 128:(t + 1) * 128, :],
                  reads=(), writes=(b.name,), group=f"gw{t % 2}", tag=f"{tag}{ph}ld{t}")

            def mm(e, b=b, pb=pb):
                for m in range(NT):
                    for c in range(4):
                        r = e.matmul(k.ps(pb + c), b[:, m * 128:(m + 1) * 128], xn[m][:, c * 512:(c + 1) * 512],
                                     start=(m == 0), stop=(m == NT - 1))
                return r

            k.op("pe", mm, reads=(b.name,) + tuple(x.name for x in xn), writes=k.pk(pb, 4), tag=f"{tag}{ph}mm{t}")
            k.op("act", lambda e, d=dst[t], pb=pb, bc=bias[:, bias_col + t:bias_col + t + 1]: e.activation(
                    d[:, :], k.ps(pb, slice(0, 128), slice(0, 2048)), AF.Identity, bias=bc, scale=1.0),
                 reads=k.pk(pb, 4) + (bias.name,), writes=(dst[t].name,), tag=f"{tag}{ph}ev{t}")

    def rope(dst, ph):
        tcs = [sb.alloc(k.u(f"{tag}rt"), [128, S], F32) for _ in range(2)]
        tcs.append(sb.alloc(k.u(f"{tag}rtb"), [128, S], BF16))
        cosb = sb.alloc(k.u(f"{tag}cos"), [128, S], BF16)
        sinb = sb.alloc(k.u(f"{tag}sin"), [128, S], BF16)
        for p in range(4):
            k.dma("sp", cosb[:, :], W["c0_cos"].ap()[p * 128:(p + 1) * 128, :],
                  reads=(), writes=(cosb.name,), group="gcos", tag=f"{tag}{ph}ldc{p}")
            k.dma("sp", sinb[:, :], W["c0_sin"].ap()[p * 128:(p + 1) * 128, :],
                  reads=(), writes=(sinb.name,), group="gsin", tag=f"{tag}{ph}lds{p}")
            a, c, t2 = tcs[0], tcs[1], tcs[2]
            x0, x1 = dst[p], dst[p + 4]
            for (o, i0, i1, tg) in ((a, x0, cosb, "a"), (c, x0, sinb, "c"), (t2, x1, sinb, "b")):
                k.op("dve", lambda e, o=o, i0=i0, i1=i1: e.tensor_mul(o[:, :], i0[:, :], i1[:, :]),
                     reads=(i0.name, i1.name), writes=(o.name,), tag=f"{tag}{ph}rp{tg}{p}")
            k.op("dve", lambda e, x0=x0, a=a, t2=t2: e.tensor_sub(x0[:, :], a[:, :], t2[:, :]),
                 reads=(a.name, t2.name), writes=(x0.name,), tag=f"{tag}{ph}rs{p}")
            k.op("dve", lambda e, t2=t2, x1=x1: e.tensor_mul(t2[:, :], x1[:, :], cosb[:, :]),
                 reads=(x1.name, cosb.name), writes=(t2.name,), tag=f"{tag}{ph}rp d{p}")
            k.op("dve", lambda e, x1=x1, c=c, t2=t2: e.tensor_add(x1[:, :], c[:, :], t2[:, :]),
                 reads=(c.name, t2.name), writes=(x1.name,), tag=f"{tag}{ph}ra{p}")
        sb.freeb(*tcs, cosb, sinb)

    K_t = [sb.alloc(k.u(f"{tag}K"), [128, S], BF16) for _ in range(NT)]
    proj(K_t, W["a0_wkT"].ap(), BK, "k")
    rope(K_t, "k")
    ksq = [sb.alloc(k.u(f"{tag}ksq"), [128, S], BF16) for _ in range(1)]
    for t in range(NT):
        q = ksq[0]
        k.op("act", lambda e, q=q, s=K_t[t]: e.activation(q[:, :], s[:, :], AF.Square),
             reads=(K_t[t].name,), writes=(q.name,), tag=f"{tag}ksq{t}")

        def mmn(e, q=q, t=t):
            for c in range(4):
                r = e.matmul(k.ps(c, slice(0, 1)), consts["ones1_bf"][:, :], q[:, c * 512:(c + 1) * 512],
                             start=(t == 0), stop=(t == NT - 1))
            return r

        k.op("pe", mmn, reads=(q.name, "ones1_bf"), writes=k.pk(0, 4), tag=f"{tag}mmn{t}")
    nrm = sb.alloc(k.u(f"{tag}nrm"), [1, S], F32)
    invn = sb.alloc(k.u(f"{tag}invn"), [1, S], F32)
    k.op("dve", lambda e: e.tensor_copy(nrm[0:1, :], k.ps(0, slice(0, 1), slice(0, 2048))),
         reads=k.pk(0, 4), writes=(nrm.name,), tag=f"{tag}cpn")
    k.op("act", lambda e: e.activation(invn[0:1, :], nrm[0:1, :], AF.Sqrt),
         reads=(nrm.name,), writes=(invn.name,), tag=f"{tag}sqn")
    k.op("dve", lambda e: e.tensor_scalar(invn[0:1, :], invn[0:1, :], 1e-12, None, ALU.max),
         reads=(invn.name,), writes=(invn.name,), tag=f"{tag}mxn")
    k.op("dve", lambda e: e.reciprocal(invn[0:1, :], invn[0:1, :]),
         reads=(invn.name,), writes=(invn.name,), tag=f"{tag}rcn")
    invb = sb.alloc(k.u(f"{tag}invb"), [128, S], F32)

    def mm_bcn(e):
        for c in range(4):
            r = e.matmul(k.ps(4 + c), consts["ones_row_f"][:, :], invn[0:1, c * 512:(c + 1) * 512], start=True, stop=True)
        return r

    k.op("pe", mm_bcn, reads=(invn.name, "ones_row_f"), writes=k.pk(4, 4), tag=f"{tag}bcn")
    k.op("dve", lambda e: e.tensor_copy(invb[:, :], k.ps(4, slice(0, 128), slice(0, 2048))),
         reads=k.pk(4, 4), writes=(invb.name,), tag=f"{tag}cpb")
    sb.freeb(*ksq, nrm)

    a_col = sb.alloc(k.u(f"{tag}acol"), [128, 16], F32)
    vt = [sb.alloc(k.u(f"{tag}vt"), [128, S], BF16) for _ in range(1)]
    kv = [sb.alloc(k.u(f"{tag}kv"), [128, S], F32) for _ in range(1)]
    for t in range(NT):
        k.pump()
        b = wb[t % 2]
        pb = 4 * (t % 2)
        v = vt[0]
        kvb = kv[0]
        k.dma("pool", b[:, :], W["a0_wvT"].ap()[t * 128:(t + 1) * 128, :],
              reads=(), writes=(b.name,), group=f"gw{t % 2}", tag=f"{tag}vld{t}")

        def mmv(e, b=b, pb=pb):
            for m in range(NT):
                for c in range(4):
                    r = e.matmul(k.ps(pb + c), b[:, m * 128:(m + 1) * 128], xn[m][:, c * 512:(c + 1) * 512],
                                 start=(m == 0), stop=(m == NT - 1))
            return r

        k.op("pe", mmv, reads=(b.name,) + tuple(x.name for x in xn), writes=k.pk(pb, 4), tag=f"{tag}vmm{t}")
        k.op("act", lambda e, v=v, pb=pb, bc=bias[:, BV + t:BV + t + 1]: e.activation(
                v[:, :], k.ps(pb, slice(0, 128), slice(0, 2048)), AF.Identity, bias=bc, scale=1.0),
             reads=k.pk(pb, 4) + (bias.name,), writes=(v.name,), tag=f"{tag}vev{t}")
        k.op("dve", lambda e, kvb=kvb, v=v, s=K_t[t]: e.tensor_mul(kvb[:, :], s[:, :], v[:, :]),
             reads=(K_t[t].name, v.name), writes=(kvb.name,), tag=f"{tag}kv{t}")
        k.op("dve", lambda e, kvb=kvb: e.tensor_mul(kvb[:, :], kvb[:, :], invb[:, :]),
             reads=(kvb.name, invb.name), writes=(kvb.name,), tag=f"{tag}kvn{t}")
        k.op("dve", lambda e, kvb=kvb, t=t: e.reduce_sum(a_col[:, t:t + 1], kvb[:, :], AX.X),
             reads=(kvb.name,), writes=(a_col.name + f":{t}",), tag=f"{tag}A{t}")
    sb.freeb(*vt, *kv, invb, invn)
    sb.freeb(*K_t)

    Y = [sb.alloc(k.u(f"{tag}Y"), [128, S], BF16) for _ in range(NT)]
    proj(Y, W["a0_wqT"].ap(), BQ, "q")
    rope(Y, "q")
    for t in range(NT):
        k.op("dve", lambda e, y=Y[t], sc=a_col[:, t:t + 1]: e.tensor_scalar(y[:, :], y[:, :], sc, None, ALU.mult),
             reads=(Y[t].name, a_col.name + f":{t}", a_col.name), writes=(Y[t].name,), tag=f"{tag}sc{t}")

    xrs = [sb.alloc(k.u(f"{tag}xr"), [128, S], F32) for _ in range(2)]
    xob = [sb.alloc(k.u(f"{tag}xo"), [128, S], F32) for _ in range(1)]
    for t in reversed(range(NT)):
        b = wb[t % 2]
        pb = 4 * (t % 2)
        xr, xo = xrs[t % 2], xob[0]
        k.dma("pool", b[:, :], W["a0_woT"].ap()[t * 128:(t + 1) * 128, :],
              reads=(), writes=(b.name,), group=f"gw{t % 2}", tag=f"{tag}old{t}")
        k.dma("sp", xr[:, :], x_dram[t * 128:(t + 1) * 128, :],
              reads=(f"{x_keyf}:{t}",), writes=(xr.name,), group=f"gxr{t % 2}", tag=f"{tag}ldxr{t}")

        def mmo(e, b=b, pb=pb):
            for m in range(NT):
                for c in range(4):
                    r = e.matmul(k.ps(pb + c), b[:, m * 128:(m + 1) * 128], Y[m][:, c * 512:(c + 1) * 512],
                                 start=(m == 0), stop=(m == NT - 1))
            return r

        k.op("pe", mmo, reads=(b.name,) + tuple(y.name for y in Y), writes=k.pk(pb, 4), tag=f"{tag}omm{t}")
        k.op("dve", lambda e, xo=xo, pb=pb, bc=bias[:, BO + t:BO + t + 1], x2=xn[t]: e.scalar_tensor_tensor(
                xo[:, :], k.ps(pb, slice(0, 128), slice(0, 2048)), bc, x2[:, :], ALU.add, ALU.add),
             reads=k.pk(pb, 4) + (bias.name, xn[t].name), writes=(xo.name,), tag=f"{tag}oev{t}")
        k.op("dve", lambda e, xo=xo, xr=xr: e.tensor_add(xo[:, :], xo[:, :], xr[:, :]),
             reads=(xo.name, xr.name), writes=(xo.name,), tag=f"{tag}ores{t}")
        k.dma("sp", xo_dram[t * 128:(t + 1) * 128, :], xo[:, :],
              reads=(xo.name,), writes=(f"{xo_keyf}:{t}",), group=f"gxw{t % 2}", tag=f"{tag}stx{t}")
    sb.freeb(*xn, *Y, *xrs, *xob, *wb, bias, a_col)


def build_attn0_program(which_io=("ExternalInput", "ExternalOutput")):
    nc = bass.Bass("TRN2", target_bir_lowering=False)
    k = K(nc)
    xin = nc.dram_tensor("xin", [D, S], F32, kind=which_io[0])
    xout = nc.dram_tensor("xout", [D, S], F32, kind=which_io[1])
    W = {}
    for nm in ("a0_wqT", "a0_wkT", "a0_wvT", "a0_woT"):
        W[nm] = nc.dram_tensor(nm, [D, D], F32, kind="ExternalInput")
    W["a0_bias"] = nc.dram_tensor("a0_bias", [128, 64], F32, kind="ExternalInput")
    W["c0_cos"] = nc.dram_tensor("c0_cos", [512, S], BF16, kind="ExternalInput")
    W["c0_sin"] = nc.dram_tensor("c0_sin", [512, S], BF16, kind="ExternalInput")
    c1 = nc.dram_tensor("c_ones_mean", [128, 1], BF16, kind="ExternalInput")
    c2 = nc.dram_tensor("c_ones_row", [1, 128], F32, kind="ExternalInput")
    c3 = nc.dram_tensor("c_one_col", [128, 1], F32, kind="ExternalInput")
    dr = {"c_ones_mean": c1, "c_ones_row": c2, "c_one_col": c3}
    k.psum_init()
    consts = stage_consts(k, dr)
    build_attn0(k, "a0", xin.ap(), "xi", xout.ap(), "xo", consts, W)
    k.P.emit(k.stack)
    return nc, k


def attn0_host_inputs(inputs):
    pi = rope_perm()
    W_in = inputs["L0_attn_W_in"]
    b_in = inputs["L0_attn_b_in"]
    W_out = inputs["L0_attn_W_out"]
    b_out = inputs["L0_attn_b_out"]
    out = {}
    for i, nm in enumerate(("a0_wqT", "a0_wkT", "a0_wvT")):
        blk = W_in[i * D:(i + 1) * D][pi]
        out[nm] = tile_lhsT(blk)
    out["a0_woT"] = tile_lhsT(np.ascontiguousarray(W_out[:, pi]))
    bias = np.zeros((128, 64), np.float32)
    for i in range(3):
        bias[:, 16 * i:16 * i + 16] = col_tiles(b_in[i * D:(i + 1) * D][pi])
    bias[:, 48:64] = col_tiles(b_out)
    out["a0_bias"] = bias
    j = np.arange(512, dtype=np.float64)
    inv_freq = 10000.0 ** (-2.0 * j / 1024.0)
    s = np.arange(S, dtype=np.float64)
    ang = s[None, :] * inv_freq[:, None]
    out["c0_cos"] = np.cos(ang).astype(np.float32).astype(ml_dtypes.bfloat16)
    out["c0_sin"] = np.sin(ang).astype(np.float32).astype(ml_dtypes.bfloat16)
    out["c_ones_mean"] = np.full((128, 1), 1.0 / D, np.float32).astype(ml_dtypes.bfloat16)
    out["c_ones_row"] = np.ones((1, 128), np.float32)
    out["c_one_col"] = np.ones((128, 1), np.float32)
    return out


def tile_lhsT(w):
    O, I = w.shape
    nt, nm = O // 128, I // 128
    return np.ascontiguousarray(w.T.reshape(nm, 128, nt, 128).transpose(2, 1, 0, 3).reshape(nt * 128, O))


def tile_A(wT):
    I, H_ = wT.shape
    nj = H_ // 128
    nt = I // 128
    return np.ascontiguousarray(wT.reshape(nt, 128, nj, 128).transpose(2, 1, 0, 3).reshape(nj * 128, nt * 128))


def tile_B(wdT):
    H_, O = wdT.shape
    nj = H_ // 128
    nt = O // 128
    return np.ascontiguousarray(wdT.reshape(nj, 128, nt, 128).transpose(2, 1, 0, 3).reshape(nt * 128, H_))


def col_tiles(v):
    n = v.shape[0] // 128
    return np.ascontiguousarray(v.reshape(n, 128).T)


def rope_perm():
    p = np.arange(D)
    p[0:512] = np.arange(512) * 2
    p[512:1024] = np.arange(512) * 2 + 1
    return p


def build_ffn0_program(which_io=("ExternalInput", "ExternalOutput")):
    nc = bass.Bass("TRN2", target_bir_lowering=False)
    k = K(nc)
    xin = nc.dram_tensor("xin", [D, S], F32, kind=which_io[0])
    xout = nc.dram_tensor("xout", [D, S], F32, kind=which_io[1])
    wg = nc.dram_tensor("f0_wg", [NJ0 * 128, 2048], I32, kind="ExternalInput")
    wu = nc.dram_tensor("f0_wu", [NJ0 * 128, 2048], I32, kind="ExternalInput")
    wd = nc.dram_tensor("f0_wd", [16 * 128, HID0], I32, kind="ExternalInput")
    sg_d = nc.dram_tensor("f0_sg", [128, NJ0], F32, kind="ExternalInput")
    su_d = nc.dram_tensor("f0_su", [128, NJ0], F32, kind="ExternalInput")
    sd_d = nc.dram_tensor("f0_sd", [128, 16], F32, kind="ExternalInput")
    c1 = nc.dram_tensor("c_ones_mean", [128, 1], BF16, kind="ExternalInput")
    c2 = nc.dram_tensor("c_ones_row", [1, 128], F32, kind="ExternalInput")
    c3 = nc.dram_tensor("c_one_col", [128, 1], F32, kind="ExternalInput")
    dr = {"c_ones_mean": c1, "c_ones_row": c2, "c_one_col": c3}

    k.psum_init()
    consts = stage_consts(k, dr)
    sg = k.sb.alloc("sg", [128, NJ0], F32)
    su = k.sb.alloc("su", [128, NJ0], F32)
    sd = k.sb.alloc("sd", [128, 16], F32)
    sdh = k.sb.alloc("sdh", [128, 16], F32)
    k.dma("sp", sg[:, :], sg_d.ap()[:, :], reads=(), writes=(sg.name,), group="consts")
    k.dma("sp", su[:, :], su_d.ap()[:, :], reads=(), writes=(su.name,), group="consts")
    k.dma("sp", sd[:, :], sd_d.ap()[:, :], reads=(), writes=(sd.name,), group="consts")
    k.op("dve", lambda e: e.tensor_scalar(sdh[:, :], sd[:, :], 0.5, None, ALU.mult),
         reads=(sd.name,), writes=(sdh.name,), tag="sdh")

    build_ffn(k, "f0a", xin.ap(), "xi", xout.ap(), "xo", consts,
              nj=NJ0,
              wgT=[wg.ap()[j * 128:(j + 1) * 128, :] for j in range(NJ0)],
              wuT=[wu.ap()[j * 128:(j + 1) * 128, :] for j in range(NJ0)],
              wdT=[wd.ap()[t * 128:(t + 1) * 128, :] for t in range(16)],
              quant=True, sg=sg, su=su, sdh=sdh)
    k.P.emit(k.stack)
    return nc, k


def ffn0_host_inputs(inputs):
    wgT = tile_A(np.ascontiguousarray(inputs["L0_ffn_wq_gate"].T))
    wuT = tile_A(np.ascontiguousarray(inputs["L0_ffn_wq_up"].T))
    wdT = tile_B(np.ascontiguousarray(inputs["L0_ffn_wq_down"].T))
    return {
        "f0_wg": wgT, "f0_wu": wuT, "f0_wd": wdT,
        "f0_sg": col_tiles(inputs["L0_ffn_s_gate"]).astype(np.float32),
        "f0_su": col_tiles(inputs["L0_ffn_s_up"]).astype(np.float32),
        "f0_sd": col_tiles(inputs["L0_ffn_s_down"]).astype(np.float32),
        "c_ones_mean": np.full((128, 1), 1.0 / D, np.float32).astype(ml_dtypes.bfloat16),
        "c_ones_row": np.ones((1, 128), np.float32),
        "c_one_col": np.ones((128, 1), np.float32),
    }


def build_lutconv_program():
    nc = bass.Bass("TRN2", target_bir_lowering=False)
    k = K(nc)
    dr = {}
    dr["f1_gi"] = nc.dram_tensor("f1_gi", [NJ1 * 128, 2048], I32, kind="ExternalInput")
    dr["f1_ui"] = nc.dram_tensor("f1_ui", [NJ1 * 128, 2048], I32, kind="ExternalInput")
    dr["f1_di"] = nc.dram_tensor("f1_di", [16 * 128, HID1], I32, kind="ExternalInput")
    dr["f1_lut"] = nc.dram_tensor("f1_lut", [1, 16], F32, kind="ExternalInput")
    dr["c_gi"] = nc.dram_tensor("c_gi", [NJ1 * 128, 2048], BF16, kind="ExternalOutput")
    dr["c_ui"] = nc.dram_tensor("c_ui", [NJ1 * 128, 2048], BF16, kind="ExternalOutput")
    dr["c_di"] = nc.dram_tensor("c_di", [16 * 128, HID1], BF16, kind="ExternalOutput")
    c2 = nc.dram_tensor("c_ones_row", [1, 128], F32, kind="ExternalInput")
    k.psum_init()
    orf = k.sb.alloc("ones_row_f", [1, 128], F32)
    k.dma("sp", orf[0:1, :], c2.ap()[:, :], reads=(), writes=("ones_row_f",), group="consts")
    consts = {"ones_row_f": orf}
    srcs = lutconv_srcs(dr)
    steps = build_lutconv(k, "lc", dr["f1_lut"], srcs, consts)
    for st in steps:
        st()
    k.P.emit(k.stack)
    return nc, k


def build_ffn1_program(which_io=("ExternalInput", "ExternalOutput")):
    nc = bass.Bass("TRN2", target_bir_lowering=False)
    k = K(nc)
    xin = nc.dram_tensor("xin", [D, S], F32, kind=which_io[0])
    xout = nc.dram_tensor("xout", [D, S], F32, kind=which_io[1])
    cg = nc.dram_tensor("c_gi", [NJ1 * 128, 2048], BF16, kind="ExternalInput")
    cu = nc.dram_tensor("c_ui", [NJ1 * 128, 2048], BF16, kind="ExternalInput")
    cd = nc.dram_tensor("c_di", [16 * 128, HID1], BF16, kind="ExternalInput")
    c1 = nc.dram_tensor("c_ones_mean", [128, 1], BF16, kind="ExternalInput")
    c2 = nc.dram_tensor("c_ones_row", [1, 128], F32, kind="ExternalInput")
    c3 = nc.dram_tensor("c_one_col", [128, 1], F32, kind="ExternalInput")
    dr = {"c_ones_mean": c1, "c_ones_row": c2, "c_one_col": c3}
    k.psum_init()
    consts = stage_consts(k, dr)
    build_ffn(k, "f1a", xin.ap(), "xi", xout.ap(), "xo", consts,
              nj=NJ1,
              wgT=[cg.ap()[j * 128:(j + 1) * 128, :] for j in range(NJ1)],
              wuT=[cu.ap()[j * 128:(j + 1) * 128, :] for j in range(NJ1)],
              wdT=[cd.ap()[t * 128:(t + 1) * 128, :] for t in range(16)],
              quant=False, sg=None, su=None, sdh=consts["half_col"])
    k.P.emit(k.stack)
    return nc, k


def build_attn1_program(which_io=("ExternalInput", "ExternalOutput")):
    nc = bass.Bass("TRN2", target_bir_lowering=False)
    k = K(nc)
    xin = nc.dram_tensor("xin", [D, S], F32, kind=which_io[0])
    xout = nc.dram_tensor("xout", [D, S], F32, kind=which_io[1])
    W = {}
    for nm in ("a1_wq", "a1_wk", "a1_wv", "a1_wo"):
        W[nm] = nc.dram_tensor(nm, [NHEAD * 128, 2048], F32, kind="ExternalInput")
    W["c1_cosD"] = nc.dram_tensor("c1_cosD", [128, S], F32, kind="ExternalInput")
    W["c1_sinD"] = nc.dram_tensor("c1_sinD", [128, S], F32, kind="ExternalInput")
    W["c1_masks"] = nc.dram_tensor("c1_masks", [128, 4 * 512], BF16, kind="ExternalInput")
    c1 = nc.dram_tensor("c_ones_mean", [128, 1], BF16, kind="ExternalInput")
    c2 = nc.dram_tensor("c_ones_row", [1, 128], F32, kind="ExternalInput")
    c3 = nc.dram_tensor("c_one_col", [128, 1], F32, kind="ExternalInput")
    dr = {"c_ones_mean": c1, "c_ones_row": c2, "c_one_col": c3}
    k.psum_init()
    consts = stage_consts(k, dr)
    dbg = None
    import os
    if os.environ.get("A1DBG"):
        dbg = {nm: nc.dram_tensor(nm, [128, S], BF16, kind="ExternalOutput")
               for nm in ("d_q0", "d_k0", "d_v0", "d_o0", "d_xn0")}
    build_attn1(k, "a1", xin.ap(), "xi", xout.ap(), "xo", consts, W, dbg=dbg)
    k.P.emit(k.stack)
    return nc, k


def base_consts():
    return {
        "c_ones_mean": np.full((128, 1), 1.0 / D, np.float32).astype(ml_dtypes.bfloat16),
        "c_ones_row": np.ones((1, 128), np.float32),
        "c_one_col": np.ones((128, 1), np.float32),
    }


def ffn1_host_inputs(inputs):
    giT = tile_A(np.ascontiguousarray(inputs["L1_ffn_gate_idx"].T))
    uiT = tile_A(np.ascontiguousarray(inputs["L1_ffn_up_idx"].T))
    diT = tile_B(np.ascontiguousarray(inputs["L1_ffn_down_idx"].T))
    return {"f1_gi": giT, "f1_ui": uiT, "f1_di": diT,
            "f1_lut": inputs["L1_ffn_lut"].reshape(1, 16).astype(np.float32)}


def attn1_host_inputs(inputs):
    out = {}
    for nm, key in (("a1_wq", "L1_attn_w_aq"), ("a1_wk", "L1_attn_w_ak"), ("a1_wv", "L1_attn_w_av")):
        w = inputs[key]
        out[nm] = np.ascontiguousarray(
            w.reshape(NHEAD, NT, 128, 128).transpose(0, 2, 1, 3).reshape(NHEAD * 128, 2048))
    wo = inputs["L1_attn_w_ao"].reshape(NHEAD, 128, NT, 128)
    out["a1_wo"] = np.ascontiguousarray(wo.transpose(2, 1, 0, 3).reshape(NT * 128, 2048))
    dm = np.arange(64, dtype=np.float64)
    ang_f = 10000.0 ** (-dm / 64.0)
    s = np.arange(S, dtype=np.float64)
    ang = s[None, :] * ang_f[:, None]
    cos = np.cos(ang)
    sin = np.sin(ang)
    out["c1_cosD"] = np.concatenate([cos, cos], 0).astype(np.float32)
    out["c1_sinD"] = np.concatenate([-sin, sin], 0).astype(np.float32)
    m = np.zeros((128, 4, 512), np.float32)
    for dlt in range(4):
        for qq in range(512):
            qt = qq // 128
            if qt > dlt:
                m[:, dlt, qq] = 1.0
            elif qt == dlt:
                m[:, dlt, qq] = (np.arange(128) <= (qq % 128)).astype(np.float32)
    out["c1_masks"] = m.reshape(128, 4 * 512).astype(ml_dtypes.bfloat16)
    return out


CONV_TILES = 2 * NJ1 + (16 * HID1) // 2048
CONV_PAD = 136


def build_full_program(ncores=NCORES):
    nc = bass.Bass("TRN2", target_bir_lowering=False)
    k = K(nc)
    xin = nc.dram_tensor("xin", [D, S], F32, kind="ExternalInput")
    xout = nc.dram_tensor("xout", [D, S], F32, kind="ExternalOutput")
    xh = nc.dram_tensor("xh", [D, S], F32, kind="Internal")
    dr = {}
    dr["f0_wg"] = nc.dram_tensor("f0_wg", [NJ0 * 128, 2048], I32, kind="ExternalInput")
    dr["f0_wu"] = nc.dram_tensor("f0_wu", [NJ0 * 128, 2048], I32, kind="ExternalInput")
    dr["f0_wd"] = nc.dram_tensor("f0_wd", [16 * 128, HID0], I32, kind="ExternalInput")
    for nm in ("f0_sg", "f0_su"):
        dr[nm] = nc.dram_tensor(nm, [128, NJ0], F32, kind="ExternalInput")
    dr["f0_sd"] = nc.dram_tensor("f0_sd", [128, 16], F32, kind="ExternalInput")
    for nm in ("a0_wqT", "a0_wkT", "a0_wvT", "a0_woT"):
        dr[nm] = nc.dram_tensor(nm, [D, D], F32, kind="ExternalInput")
    dr["a0_bias"] = nc.dram_tensor("a0_bias", [128, 64], F32, kind="ExternalInput")
    dr["c0_cos"] = nc.dram_tensor("c0_cos", [512, S], BF16, kind="ExternalInput")
    dr["c0_sin"] = nc.dram_tensor("c0_sin", [512, S], BF16, kind="ExternalInput")
    dr["f1_lut"] = nc.dram_tensor("f1_lut", [1, 16], F32, kind="ExternalInput")
    for nm in ("a1_wq", "a1_wk", "a1_wv", "a1_wo"):
        dr[nm] = nc.dram_tensor(nm, [NHEAD * 128, 2048], F32, kind="ExternalInput")
    dr["c1_cosD"] = nc.dram_tensor("c1_cosD", [128, S], F32, kind="ExternalInput")
    dr["c1_sinD"] = nc.dram_tensor("c1_sinD", [128, S], F32, kind="ExternalInput")
    dr["c1_masks"] = nc.dram_tensor("c1_masks", [128, 4 * 512], BF16, kind="ExternalInput")
    dr["c_ones_mean"] = nc.dram_tensor("c_ones_mean", [128, 1], BF16, kind="ExternalInput")
    dr["c_ones_row"] = nc.dram_tensor("c_ones_row", [1, 128], F32, kind="ExternalInput")
    dr["c_one_col"] = nc.dram_tensor("c_one_col", [128, 1], F32, kind="ExternalInput")

    k.psum_init()
    consts = stage_consts(k, dr)
    sg = k.sb.alloc("sg", [128, NJ0], F32)
    su = k.sb.alloc("su", [128, NJ0], F32)
    sd = k.sb.alloc("sd", [128, 16], F32)
    sdh = k.sb.alloc("sdh", [128, 16], F32)
    k.dma("sp", sg[:, :], dr["f0_sg"].ap()[:, :], reads=(), writes=(sg.name,), group="consts")
    k.dma("sp", su[:, :], dr["f0_su"].ap()[:, :], reads=(), writes=(su.name,), group="consts")
    k.dma("sp", sd[:, :], dr["f0_sd"].ap()[:, :], reads=(), writes=(sd.name,), group="consts")
    k.op("dve", lambda e: e.tensor_scalar(sdh[:, :], sd[:, :], 0.5, None, ALU.mult),
         reads=(sd.name,), writes=(sdh.name,), tag="sdh")

    f1_sh = nc.dram_tensor("f1_sh", [CONV_TILES * 128, 2048], I32, kind="ExternalInput")
    cflat = nc.dram_tensor("cflat", [CONV_TILES * 128, 2048], BF16, kind="Internal")
    TILE_EL = 128 * 2048
    order = []
    for j in range(NJ1):
        order.extend([j, NJ1 + j])
    order.extend(range(2 * NJ1, CONV_TILES))
    srcs = []
    for r in order:
        for hh in range(2):
            o = r * TILE_EL + hh * 1024
            srcs.append((bass.AP(f1_sh, o, [[2048, 128], [1, 1024]]),
                         bass.AP(cflat, o, [[2048, 128], [1, 1024]]),
                         f"cflat:{r}", 1024))
    steps = build_lutconv(k, "lc", dr["f1_lut"], srcs, consts)
    k.bg.extend(steps)

    def cf_ap(off, n, f):
        return bass.AP(cflat, off, [[f, n], [1, f]])

    def cf_keys(off, n, f):
        lo = off // TILE_EL
        hi = (off + n * f - 1) // TILE_EL
        return tuple(f"cflat:{r}" for r in range(lo, hi + 1))

    def ffn0(tag, xi, xik, xo, xok):
        build_ffn(k, tag, xi, xik, xo, xok, consts, nj=NJ0,
                  wgT=[dr["f0_wg"].ap()[j * 128:(j + 1) * 128, :] for j in range(NJ0)],
                  wuT=[dr["f0_wu"].ap()[j * 128:(j + 1) * 128, :] for j in range(NJ0)],
                  wdT=[dr["f0_wd"].ap()[t * 128:(t + 1) * 128, :] for t in range(16)],
                  quant=True, sg=sg, su=su, sdh=sdh)

    DI_OFF = 2 * NJ1 * 128 * 2048

    def ffn1(tag, xi, xik, xo, xok):
        build_ffn(k, tag, xi, xik, xo, xok, consts, nj=NJ1,
                  wgT=[cf_ap(j * 128 * 2048, 128, 2048) for j in range(NJ1)],
                  wuT=[cf_ap((NJ1 + j) * 128 * 2048, 128, 2048) for j in range(NJ1)],
                  wdT=[cf_ap(DI_OFF + t * 128 * HID1, 128, HID1) for t in range(16)],
                  quant=False, sg=None, su=None, sdh=consts["half_col"],
                  wrkeys="cf",
                  wrkeyfn=lambda kind, idx: (
                      cf_keys(idx * 128 * 2048, 128, 2048) if kind == 0 else
                      cf_keys((NJ1 + idx) * 128 * 2048, 128, 2048) if kind == 1 else
                      cf_keys(DI_OFF + idx * 128 * HID1, 128, HID1)))

    ffn0("s1", xin.ap(), "xi", xh.ap(), "xh")
    build_attn0(k, "s2", xh.ap(), "xh", xh.ap(), "xh", consts, dr)
    ffn0("s3", xh.ap(), "xh", xh.ap(), "xh")
    ffn1("s4", xh.ap(), "xh", xh.ap(), "xh")
    build_attn1(k, "s5", xh.ap(), "xh", xh.ap(), "xh", consts, dr)
    ffn1("s6", xh.ap(), "xh", xout.ap(), "xo")
    k.P.emit(k.stack)
    return nc, k


def ffn1_flat(inputs):
    giT = tile_A(np.ascontiguousarray(inputs["L1_ffn_gate_idx"].T))
    uiT = tile_A(np.ascontiguousarray(inputs["L1_ffn_up_idx"].T))
    diT = tile_B(np.ascontiguousarray(inputs["L1_ffn_down_idx"].T))
    flat = np.concatenate([giT.ravel(), uiT.ravel(), diT.ravel()])
    return flat.reshape(CONV_TILES * 128, 2048)


def full_host_inputs(inputs):
    out = {}
    out.update(ffn0_host_inputs(inputs))
    out.update(attn0_host_inputs(inputs))
    out.update(attn1_host_inputs(inputs))
    out["f1_lut"] = inputs["L1_ffn_lut"].reshape(1, 16).astype(np.float32)
    return out


_PROG_CACHE = {}


def _get_prog(name, builder):
    if name not in _PROG_CACHE:
        _PROG_CACHE[name] = builder()
    return _PROG_CACHE[name]


def kernel(**inputs):
    import os
    if os.environ.get("KMODE") == "multi":
        return kernel_multi(**inputs)
    nc, k = _get_prog("full", build_full_program)
    host = full_host_inputs(inputs)
    X = inputs["L0_attn_X"]
    flat = ffn1_flat(inputs)
    in_maps = []
    for b in range(NCORES):
        m = dict(host)
        m["xin"] = np.ascontiguousarray(X[b].T)
        m["f1_sh"] = flat
        in_maps.append(m)
    res = run_bass_kernel_spmd(nc, in_maps, core_ids=list(range(NCORES)))
    return np.stack([res.results[b]["xout"].T for b in range(NCORES)]).astype(np.float32)


def kernel_multi(**inputs):
    consts = base_consts()
    h_f0 = ffn0_host_inputs(inputs)
    h_a0 = attn0_host_inputs(inputs)
    h_f1 = ffn1_host_inputs(inputs)
    h_a1 = attn1_host_inputs(inputs)
    X = inputs["L0_attn_X"]
    x_cur = [np.ascontiguousarray(X[b].T) for b in range(NCORES)]

    nc, k = _get_prog("conv", build_lutconv_program)
    conv_in = dict(h_f1)
    conv_in["c_ones_row"] = consts["c_ones_row"]
    res = run_bass_kernel_spmd(nc, [conv_in] * NCORES, core_ids=list(range(NCORES)))
    caches = res.results

    seq = [
        ("ffn0", build_ffn0_program, h_f0, False),
        ("attn0", build_attn0_program, {**h_a0, **consts}, False),
        ("ffn0", build_ffn0_program, h_f0, False),
        ("ffn1", build_ffn1_program, dict(consts), True),
        ("attn1", build_attn1_program, {**h_a1, **consts}, False),
        ("ffn1", build_ffn1_program, dict(consts), True),
    ]
    for name, builder, hostmap, needs_cache in seq:
        nc, k = _get_prog(name, builder)
        in_maps = []
        for b in range(NCORES):
            m = dict(hostmap)
            m["xin"] = x_cur[b]
            if needs_cache:
                m.update({kk: caches[b][kk] for kk in ("c_gi", "c_ui", "c_di")})
            in_maps.append(m)
        res = run_bass_kernel_spmd(nc, in_maps, core_ids=list(range(NCORES)))
        x_cur = [res.results[b]["xout"] for b in range(NCORES)]

    return np.stack([x.T for x in x_cur]).astype(np.float32)


if __name__ == "__main__":
    nc, k = build_ffn0_program()
    print("ops:", len(k.P.ops))
